# version 05
# speedup vs baseline: 1.1548x; 1.0300x over previous
import math
import numpy as np
from contextlib import ExitStack
from collections import defaultdict
import traceback

import ml_dtypes
import concourse.bass as bass
import concourse.mybir as mybir
from concourse.bass_utils import run_bass_kernel_spmd

F32 = mybir.dt.float32
BF16 = mybir.dt.bfloat16
AF = mybir.ActivationFunctionType
ALU = mybir.AluOpType
AX = mybir.AxisListType
BF = ml_dtypes.bfloat16

USE_ACT_LRELU = True
LN_SKIP_TP = False
LN_SKIP_STATS = False
LN_TP_MODE = 0

FC_VALS = [1e-5, -0.5, 0.125, 1e-12, 2.5, 1e-20, 0.08838834764831845, 1.0 / 1024, 8.0, 0.0]
FC = {v: i for i, v in enumerate(FC_VALS)}

NCORES = 8
NTOK = 2048
D = 1024
NCH = D // 128
NTILE = NTOK // 128
TT = 512
NTT = NTOK // TT

ENGINES = ("tensor", "vector", "scalar", "gpsimd", "sync")


class _Op:
    __slots__ = ("attr", "args", "kwargs", "incs", "src")

    def __init__(self, attr, args, kwargs, src):
        self.attr, self.args, self.kwargs, self.incs, self.src = attr, args, kwargs, [], src


AUTO_SEM = {"vector": "VE", "scalar": "SE", "gpsimd": "GE"}


class _Eng:

    def __init__(self, L, name):
        self._L = L
        self._name = name

    def __getattr__(self, attr):
        L, name = self._L, self._name

        def call(*args, **kwargs):
            fr = traceback.extract_stack(limit=3)[0]
            src = f"{fr.filename}:{fr.lineno}"
            auto = AUTO_SEM.get(name)
            if auto is not None and attr not in ("wait_ge", "nop") \
                    and not attr.startswith("dma_start"):
                q = L.q[name]
                for prev in reversed(q):
                    if prev.attr in ("wait_ge", "nop") or prev.attr.startswith("dma_start"):
                        continue
                    if not prev.incs:
                        prev.incs.append((L.sem(auto), 1))
                        L.cnt[auto] += 1
                    break
                tgt = L.cnt[auto]
                if tgt > 0 and L._autow.get(name, -1) < tgt:
                    q.append(_Op("wait_ge", (L.sem(auto), tgt), {}, src))
                    L._autow[name] = tgt
            L.q[name].append(_Op(attr, args, kwargs, src))
            return self

        return call


class Launch:
    def __init__(self, name):
        self.name = name
        self.nc = bass.Bass("TRN2", target_bir_lowering=False)
        self.stack = ExitStack()
        self.q = {e: [] for e in ENGINES}
        self.t = _Eng(self, "tensor")
        self.v = _Eng(self, "vector")
        self.s = _Eng(self, "scalar")
        self.g = _Eng(self, "gpsimd")
        self.y = _Eng(self, "sync")
        self.sems = {}
        self.cnt = defaultdict(int)
        self.out_names = []
        self.in_names = []
        self._autow = {}
        self._finalized = False

    def din(self, name, shape, dtype=F32):
        self.in_names.append(name)
        return self.nc.dram_tensor(name, list(shape), dtype, kind="ExternalInput").ap()

    def dout(self, name, shape, dtype=F32):
        self.out_names.append(name)
        return self.nc.dram_tensor(name, list(shape), dtype, kind="ExternalOutput").ap()

    def sb(self, name, shape, dtype):
        return self.stack.enter_context(self.nc.sbuf_tensor(name, list(shape), dtype))

    def ps(self, name, shape, dtype=F32):
        return self.stack.enter_context(self.nc.psum_tensor(name, list(shape), dtype))

    def sem(self, name):
        if name not in self.sems:
            self.sems[name] = self.stack.enter_context(self.nc.semaphore(name))
        return self.sems[name]

    def inc(self, eng, semname, n=None):
        sem = self.sem(semname)
        last = self.q[eng._name][-1]
        if last.incs:
            assert last.incs == [(sem, 1)], f"double inc on {last.attr} at {last.src}"
            return self.cnt[semname]
        if n is None:
            n = 16 if last.attr.startswith("dma_start") else 1
        last.incs.append((sem, n))
        self.cnt[semname] += n
        return self.cnt[semname]

    def wait(self, eng, semname, val=None):
        sem = self.sem(semname)
        if val is None:
            val = self.cnt[semname]
        if val <= 0:
            return
        eng.wait_ge(sem, val)

    def dma(self, eng, semname, out, in_):
        self.wait(eng, semname)
        eng.dma_start(out=out, in_=in_)
        return self.inc(eng, semname)

    def selfsync(self, eng, semname):
        self.inc(eng, semname)
        self.wait(eng, semname)

    def finalize(self):
        if self._finalized:
            return self
        self._finalized = True
        nc = self.nc
        with self.stack, nc.Block() as blk:
            for e in ENGINES:
                ops = self.q[e]
                if not ops:
                    continue

                def mk(ops, ename):
                    def f(eng):
                        for op in ops:
                            try:
                                inst = getattr(eng, op.attr)(*op.args, **op.kwargs)
                                for (sem, n) in op.incs:
                                    inst = inst.then_inc(sem, n)
                            except Exception as ex:
                                raise RuntimeError(
                                    f"[{self.name}/{ename}] {op.attr} from {op.src}: {ex}") from ex
                    return f

                getattr(blk, e)(mk(ops, e))
        return self

    def run(self, in_maps, **kw):
        self.finalize()
        filt = [{k: m[k] for k in self.in_names} for m in in_maps]
        res = run_bass_kernel_spmd(self.nc, filt, core_ids=list(range(len(filt))), **kw)
        return [{k: np.asarray(r[k]) for k in self.out_names} for r in res.results]



def shard_tokens(x2d):
    return [np.ascontiguousarray(x2d[c * NTOK:(c + 1) * NTOK]) for c in range(NCORES)]


def fmaj(x2d):
    return np.ascontiguousarray(np.asarray(x2d).T)


def const_inputs():
    return {
        "c_eyeb": np.eye(128, dtype=np.float32).astype(BF),
        "c_eyef": np.eye(128, dtype=np.float32),
        "c_ones1b": np.ones((1, 128), dtype=np.float32).astype(BF),
        "c_sel16": _sel16(),
        "c_fcol": np.tile(np.array([FC_VALS], dtype=np.float32), (128, 1)),
        "c_ones64c": np.ones((64, 1), dtype=np.float32).astype(BF),
        "c_ones128c": np.ones((128, 1), dtype=np.float32).astype(BF),
        "c_ones1f": np.ones((1, 128), dtype=np.float32),
        "c_utri32": np.triu(np.ones((32, 32), dtype=np.float32)).astype(BF),
        "c_ones128f": np.ones((128, 1), dtype=np.float32),
    }


def _sel16():
    m = np.zeros((16, 16 * 128), dtype=np.float32)
    for e in range(16):
        m[e, e * 128:(e + 1) * 128] = 1.0
    return m.astype(BF)


def declare_consts(L):
    return {
        "eyeb": L.din("c_eyeb", [128, 128], BF16),
        "eyef": L.din("c_eyef", [128, 128], F32),
        "ones1b": L.din("c_ones1b", [1, 128], BF16),
        "sel16": L.din("c_sel16", [16, 16 * 128], BF16),
        "fcol": L.din("c_fcol", [128, len(FC_VALS)], F32),
        "ones64c": L.din("c_ones64c", [64, 1], BF16),
        "ones128c": L.din("c_ones128c", [128, 1], BF16),
        "ones1f": L.din("c_ones1f", [1, 128], F32),
        "utri32": L.din("c_utri32", [32, 32], BF16),
        "ones128f": L.din("c_ones128f", [128, 1], F32),
    }


def wslab_ap(W, f0, f1, p=128):
    return W[:, f0:f1].rearrange("(c p) f -> p c f", p=p)


def slab_dst(sb_ap, nk):
    return sb_ap.rearrange("p (c f) -> p c f", c=nk)



def emit_ln_fm(L, x_in, o_xnT_tiles, xnT_b, eyef, pt, fcol, pref="ln", ntok=NTOK):
    xtile = L.sb(pref + "_x", [128, D], F32)
    sq = L.sb(pref + "_sq", [128, D], F32)
    stat = L.sb(pref + "_stat", [128, 8], F32)
    xn_tm = L.sb(pref + "_xn", [128, D], F32)
    st_buf = [L.sb(f"{pref}_st{i}", [128, D], F32) for i in range(2)]
    inv_d = 1.0 / D
    st_done = {}
    prev_xn_done = None
    for i in range(ntok // 128):
        if prev_xn_done is not None:
            L.wait(L.y, "VE", prev_xn_done)
        tgt = L.dma(L.y, "LD", xtile[:], x_in[i * 128:(i + 1) * 128, :])
        L.wait(L.v, "LD", tgt)
        if LN_SKIP_STATS:
            L.v.tensor_copy(xn_tm[:], xtile[:])
            prev_xn_done = L.inc(L.v, "VE")
        else:
            _emit_ln_stats = True
        if not LN_SKIP_STATS:
            L.v.reduce_sum(stat[:, 0:1], xtile[:], axis=AX.X)
            L.v.tensor_mul(sq[:], xtile[:], xtile[:])
            L.v.reduce_sum(stat[:, 1:2], sq[:], axis=AX.X)
            L.v.tensor_scalar_mul(stat[:, 2:3], stat[:, 0:1], inv_d)
            L.v.tensor_scalar_mul(stat[:, 3:4], stat[:, 1:2], inv_d)
            L.v.tensor_mul(stat[:, 4:5], stat[:, 2:3], stat[:, 2:3])
            L.v.tensor_sub(stat[:, 5:6], stat[:, 3:4], stat[:, 4:5])
            ve_var = L.inc(L.v, "VE")
            L.wait(L.s, "VE", ve_var)
            L.s.activation(stat[:, 6:7], stat[:, 5:6], AF.Ln, bias=fcol[:, FC[1e-5]:FC[1e-5] + 1])
            L.s.activation(stat[:, 7:8], stat[:, 6:7], AF.Exp, scale=fcol[:, FC[-0.5]:FC[-0.5] + 1])
            se_rstd = L.inc(L.s, "SE")
            L.wait(L.v, "SE", se_rstd)
            L.v.tensor_scalar(xn_tm[:], xtile[:], stat[:, 2:3], stat[:, 7:8],
                              op0=ALU.subtract, op1=ALU.mult)
            prev_xn_done = L.inc(L.v, "VE")
        buf = st_buf[i % 2]
        if LN_SKIP_TP:
            L.wait(L.y, "VE", prev_xn_done)
            st_done[i] = L.dma(L.y, "ST", o_xnT_tiles[i], xn_tm[:])
            continue
        if i >= 2:
            L.wait(L.t, "ST", st_done[i - 2])
        L.wait(L.t, "VE", prev_xn_done)
        for c in range(NCH):
            L.t.transpose(pt, xn_tm[:, c * 128:(c + 1) * 128], eyef[:])
            pe_tp = L.inc(L.t, "PE")
            if LN_TP_MODE in (0, 2):
                L.wait(L.s, "PE", pe_tp)
                L.s.copy(buf[:, c * 128:(c + 1) * 128], pt)
                se_c = L.inc(L.s, "SE")
                L.wait(L.t, "SE", se_c)
            if LN_TP_MODE in (0, 1):
                L.wait(L.v, "PE", pe_tp)
                if LN_TP_MODE == 0:
                    L.wait(L.v, "SE", se_c)
                L.v.tensor_copy(xnT_b[:, c * ntok + i * 128: c * ntok + (i + 1) * 128], pt)
                ve_c = L.inc(L.v, "VE")
                L.wait(L.t, "VE", ve_c)
        if LN_TP_MODE in (0, 2):
            L.wait(L.y, "SE")
            st_done[i] = L.dma(L.y, "ST", o_xnT_tiles[i], buf[:])
        else:
            L.wait(L.y, "VE", prev_xn_done)
            st_done[i] = L.dma(L.y, "ST", o_xnT_tiles[i], xn_tm[:])
    return L.cnt["VE"], st_done


def emit_proj_b(L, wsb, wstride, xT_sb, fchunks, evac, banks, bank_free, kchunks=NCH,
                pe_wait=None, xstride=NTOK):
    nb = len(banks)
    i = 0
    for (fcol, tag) in fchunks:
        for tt in range(NTT):
            bi = i % nb
            sem, cntv = bank_free.get(bi, (None, 0))
            if sem is not None:
                L.wait(L.t, sem, cntv)
            if pe_wait is not None:
                pe_wait()
                pe_wait = None
            for k in range(kchunks):
                L.t.matmul(banks[bi][:],
                           wsb[:, k * wstride + fcol: k * wstride + fcol + 128],
                           xT_sb[:, k * xstride + tt * TT: k * xstride + (tt + 1) * TT],
                           start=(k == 0), stop=(k == kchunks - 1))
            pe_done = L.inc(L.t, "PE")
            bank_free[bi] = evac(tag, tt, banks[bi], pe_done)
            i += 1



def build_A0(stop=None):
    L = Launch("A0")
    C = declare_consts(L)
    x0 = L.din("x0", [NTOK, D])
    WcT = L.din("a0_WcT", [D, 10240])
    bc_col = L.din("a0_bc_col", [128, 80])
    bc_row = L.din("a0_bc_row", [1, 10240])
    wrT = L.din("f0_wrT", [D, 16])
    mhT = L.din("f0_mhT", [D, D])
    kfw = L.din("f0_kf", [8, 128, 4096])
    vfw = L.din("f0_vf", [8, 128, 4096])
    mgT = L.din("f0_mgT", [D, D])

    o_xnT_tiles = L.dout("xnT_tiles", [NTILE, 128, D])
    o_qT = L.dout("qT", [D, NTOK], BF16)
    o_kT = L.dout("kT", [D, NTOK], BF16)
    o_v = L.dout("v_blk", [16, 128, 16 * 512], BF16)
    o_g0T = L.dout("g0T", [D, NTOK])

    eyef = L.sb("eyef", [128, 128], F32)
    ones1b = L.sb("ones1b", [1, 128], BF16)
    sel16 = L.sb("sel16", [16, 16 * 128], BF16)
    fcol = L.sb("fcol", [128, len(FC_VALS)], F32)
    bccol = L.sb("bccol", [128, 80], F32)
    bcv = [L.sb(f"bcv{i}", [1, 512], BF16) for i in range(2)]

    xnT_b = L.sb("xnT_b", [128, NCH * NTOK], BF16)
    yTt = L.sb("yTt", [128, NCH * 512], BF16)
    cwbc = L.sb("cwbc", [128, 16 * TT], BF16)
    cwT16 = L.sb("cwT16", [16, NTOK], BF16)

    wbufs = [L.sb(f"wbuf{i}", [128, NCH * 512], BF16) for i in range(2)]
    wres_mh = L.sb("wres_mh", [128, NCH * 1024], BF16)
    wres_mg = L.sb("wres_mg", [128, NCH * 1024], BF16)
    wrouter = L.sb("wrouter", [128, NCH * 16], F32)
    rbuf = L.sb("rbuf", [128, D], F32)
    qk_st = L.sb("qk_st", [128, NTOK], BF16)
    lr_tmp = L.sb("lr_tmp", [128, 512], F32)
    v_st = L.sb("v_st", [128, 16 * 512], BF16)
    qmh_sb = [L.sb(f"qmh_sb{i}", [128, 512], BF16) for i in range(2)]
    cw16 = L.sb("cw16", [128, 16], F32)
    cwt1 = L.sb("cwt1", [128, 16], F32)
    cwt2 = L.sb("cwt2", [128, 16], F32)
    cstat = L.sb("cstat", [128, 2], F32)
    kfb = [L.sb(f"kfb{i}", [128, 4096], BF16) for i in range(2)]
    vfb = [L.sb(f"vfb{i}", [128, 4096], BF16) for i in range(2)]
    gel = [L.sb(f"gel{i}", [128, 512], BF16) for i in range(3)]
    actb = [L.sb(f"actb{i}", [128, 512], BF16) for i in range(3)]
    g0_st = [L.sb(f"g0_st{i}", [128, 512], F32) for i in range(2)]

    pb = [L.ps(f"pb{i}", [128, 512], F32) for i in range(7)]
    pu = L.ps("pu", [128, 512], F32)

    L.dma(L.y, "LD", eyef[:], C["eyef"])
    L.dma(L.y, "LD", bccol[:], bc_col)
    L.dma(L.y, "LD", fcol[:], C["fcol"])
    L.dma(L.g, "GD", ones1b[:], C["ones1b"])
    L.dma(L.g, "GD", sel16[:], C["sel16"])
    ve_ln, ln_st = emit_ln_fm(L, x0, o_xnT_tiles, xnT_b, eyef, pu[:, 0:128], fcol)

    if stop == "ln":
        L.wait(L.y, "ST")
        return L
    ld_wr = L.dma(L.y, "LD", slab_dst(wrouter[:], NCH), wrT.rearrange("(c p) e -> p c e", p=128))
    pu_free_t = 0
    rb_free = 0
    for i in range(NTILE):
        L.wait(L.y, "ST", ln_st[i])
        if rb_free:
            L.wait(L.y, "PE", rb_free)
        ld_rb = L.dma(L.y, "LD", rbuf[:], o_xnT_tiles[i])
        L.wait(L.t, "LD", max(ld_wr, ld_rb))
        L.wait(L.t, "VE", max(ve_ln, pu_free_t))
        for c in range(NCH):
            L.t.matmul(pu[:, 0:16],
                       rbuf[:, c * 128:(c + 1) * 128],
                       wrouter[:, c * 16:(c + 1) * 16],
                       start=(c == 0), stop=(c == NCH - 1))
        pe_r = L.inc(L.t, "PE")
        rb_free = pe_r
        L.wait(L.v, "PE", pe_r)
        L.v.tensor_copy(cw16[:], pu[:, 0:16])
        L.v.reduce_max(cstat[:, 0:1], cw16[:], axis=AX.X)
        L.selfsync(L.v, "VE")
        L.v.tensor_scalar(cwt1[:], cw16[:], cstat[:, 0:1], None, op0=ALU.is_equal)
        L.v.scalar_tensor_tensor(cwt2[:], cwt1[:], -1e30, cw16[:], op0=ALU.mult, op1=ALU.add)
        L.v.reduce_max(cstat[:, 1:2], cwt2[:], axis=AX.X)
        L.selfsync(L.v, "VE")
        L.v.tensor_scalar(cwt2[:], cw16[:], cstat[:, 1:2], None, op0=ALU.is_equal)
        L.v.tensor_add(cwt1[:], cwt1[:], cwt2[:])
        ve_m = L.inc(L.v, "VE")
        L.wait(L.s, "VE", ve_m)
        L.s.activation(cwt2[:], cw16[:], AF.Sigmoid)
        se_g = L.inc(L.s, "SE")
        L.wait(L.v, "SE", se_g)
        L.v.tensor_mul(cw16[:], cwt1[:], cwt2[:])
        ve_cw = L.inc(L.v, "VE")
        L.wait(L.t, "VE", ve_cw)
        L.t.transpose(pu[0:16, 128:256], cw16[:], eyef[:])
        pe_t = L.inc(L.t, "PE")
        L.wait(L.v, "PE", pe_t)
        L.v.tensor_copy(cwT16[:, i * 128:(i + 1) * 128], pu[0:16, 128:256])
        pu_free_t = L.inc(L.v, "VE")

    if stop == "p1":
        L.wait(L.y, "ST")
        return L
    qk_store = {}

    def evac_qk(tag, tt, bank, pe_done):
        fc_abs = tag
        fc = fc_abs // 128
        dst = qk_st[:, tt * TT:(tt + 1) * TT]
        if USE_ACT_LRELU:
            if tt == 0 and fc >= 1 and (fc - 1) in qk_store:
                L.wait(L.s, "ST", qk_store[fc - 1])
            L.wait(L.s, "PE", pe_done)
            L.s.activation(dst, bank[:], AF.Lrelu,
                           bias=bccol[:, fc:fc + 1], alpha=0.01)
            free = L.inc(L.s, "SE")
        else:
            if tt == 0 and fc >= 1 and (fc - 1) in qk_store:
                L.wait(L.v, "ST", qk_store[fc - 1])
            L.wait(L.v, "PE", pe_done)
            L.v.tensor_scalar(lr_tmp[:], bank[:], bccol[:, fc:fc + 1], None, op0=ALU.add)
            L.v.scalar_tensor_tensor(dst, lr_tmp[:], 0.01, lr_tmp[:], op0=ALU.mult, op1=ALU.max)
            free = ("SE", None)
            free = L.inc(L.v, "VE")
            ret = ("VE", free)
            if tt == NTT - 1:
                L.wait(L.y, "VE", free)
                d2 = o_qT if fc_abs < 1024 else o_kT
                r0 = fc_abs % 1024
                qk_store[fc] = L.dma(L.y, "ST", d2[r0:r0 + 128, :], qk_st[:])
            return ret
        if tt == NTT - 1:
            L.wait(L.y, "SE", free)
            dst = o_qT if fc_abs < 1024 else o_kT
            r0 = fc_abs % 1024
            qk_store[fc] = L.dma(L.y, "ST", dst[r0:r0 + 128, :], qk_st[:])
        return ("SE", free)

    bank_free = {}
    pe_blk = {}
    for b in range(4):
        buf = wbufs[b % 2]
        if b >= 2:
            L.wait(L.g, "PE", pe_blk[b - 2])
        ld = L.dma(L.g, "GD", slab_dst(buf[:], NCH), wslab_ap(WcT, b * 512, (b + 1) * 512))
        L.wait(L.t, "GD", ld)
        if b == 0:
            L.wait(L.t, "VE", ve_ln)
        emit_proj_b(L, buf, 512, xnT_b,
                    [(fc * 128, b * 512 + fc * 128) for fc in range(4)],
                    evac_qk, pb[0:4], bank_free)
        pe_blk[b] = L.cnt["PE"]

    if stop == "p2a":
        L.wait(L.y, "ST")
        return L
    vst_done = 0
    vbank_free = {}
    wbuf_free_pe = {0: pe_blk[2], 1: pe_blk[3]}
    for b in range(16):
        buf = wbufs[b % 2]
        L.wait(L.g, "PE", wbuf_free_pe[b % 2] if b >= 2 else wbuf_free_pe[b % 2])
        ld = L.dma(L.g, "GD", slab_dst(buf[:], NCH), wslab_ap(WcT, 2048 + b * 512, 2048 + (b + 1) * 512))
        ld_b = L.dma(L.g, "GD", bcv[b % 2][:], bc_row[0:1, 2048 + b * 512: 2048 + (b + 1) * 512])
        L.wait(L.t, "GD", ld_b)
        for i in range(NTILE):
            bi = 4 + (i % 2)
            sem, cntv = vbank_free.get(bi, (None, 0))
            if sem is not None:
                L.wait(L.t, sem, cntv)
            for k in range(NCH):
                L.t.matmul(pb[bi][:],
                           xnT_b[:, k * NTOK + i * 128: k * NTOK + (i + 1) * 128],
                           buf[:, k * 512:(k + 1) * 512],
                           start=(k == 0), stop=False)
            L.t.matmul(pb[bi][:], ones1b[:], bcv[b % 2][:], start=False, stop=True)
            pe_done = L.inc(L.t, "PE")
            if USE_ACT_LRELU:
                if i == 0 and b >= 1:
                    L.wait(L.s, "ST", vst_done)
                L.wait(L.s, "PE", pe_done)
                L.s.activation(v_st[:, i * 512:(i + 1) * 512], pb[bi][:], AF.Lrelu, alpha=0.01)
                vbank_free[bi] = ("SE", L.inc(L.s, "SE"))
            else:
                if i == 0 and b >= 1:
                    L.wait(L.v, "ST", vst_done)
                L.wait(L.v, "PE", pe_done)
                L.v.scalar_tensor_tensor(v_st[:, i * 512:(i + 1) * 512], pb[bi][:], 0.01,
                                         pb[bi][:], op0=ALU.mult, op1=ALU.max)
                vbank_free[bi] = ("VE", L.inc(L.v, "VE"))
        wbuf_free_pe[b % 2] = L.cnt["PE"]
        if USE_ACT_LRELU:
            L.wait(L.y, "SE")
        else:
            L.wait(L.y, "VE")
        vst_done = L.dma(L.y, "ST", o_v[b], v_st[:])


    if stop == "p2b":
        L.wait(L.y, "ST")
        return L
    L.wait(L.g, "PE")
    L.dma(L.g, "GD", slab_dst(wres_mh[:], NCH), mhT.rearrange("(c p) f -> p c f", p=128))
    ld_w = L.dma(L.g, "GD", slab_dst(wres_mg[:], NCH), mgT.rearrange("(c p) f -> p c f", p=128))

    bank_free2 = {4: ("SE", L.cnt["SE"]), 5: ("SE", L.cnt["SE"])}
    kf_ld = {}
    gelf = {}
    actf = {}
    g0_store = {}
    L.wait(L.t, "GD", ld_w)
    for tt in range(NTT):
        for e in range(16):
            L.wait(L.t, "VE")
            L.t.matmul(pu[:], sel16[:, e * 128:(e + 1) * 128],
                       cwT16[:, tt * TT:(tt + 1) * TT], start=True, stop=True)
            pe_bc = L.inc(L.t, "PE")
            L.wait(L.v, "PE", pe_bc)
            L.v.tensor_copy(cwbc[:, e * TT:(e + 1) * TT], pu[:])
            L.inc(L.v, "VE")
        ve_cwbc = L.cnt["VE"]
        L.wait(L.t, "VE", ve_cwbc)
        for h in range(8):
            it = tt * 8 + h
            bk, bv = kfb[it % 2], vfb[it % 2]
            if it >= 2:
                L.wait(L.g, "PE", kf_ld[it - 2])
            L.dma(L.g, "GD", bk[:], kfw[h])
            gd = L.dma(L.g, "GD", bv[:], vfw[h])
            sem, cntv = bank_free2.get(2, (None, 0))
            if sem is not None:
                L.wait(L.t, sem, cntv)
            for k in range(NCH):
                L.t.matmul(pb[2][:],
                           wres_mh[:, k * 1024 + h * 128: k * 1024 + (h + 1) * 128],
                           xnT_b[:, k * NTOK + tt * TT: k * NTOK + (tt + 1) * TT],
                           start=(k == 0), stop=(k == NCH - 1))
            pe_qmh = L.inc(L.t, "PE")
            qb = qmh_sb[it % 2]
            L.wait(L.s, "PE", pe_qmh)
            L.s.copy(qb[:], pb[2][:])
            se_qmh = L.inc(L.s, "SE")
            bank_free2[2] = ("SE", se_qmh)
            L.wait(L.t, "GD", gd)
            L.wait(L.t, "SE", se_qmh)
            ybank = pb[6]
            sem, cntv = bank_free2.get(6, (None, 0))
            if sem is not None:
                L.wait(L.t, sem, cntv)
            pe_s = {}
            for j in range(33):
                if j < 32:
                    sb_id = (0, 1, 3)[j % 3]
                    sbank = pb[sb_id]
                    sem, cntv = bank_free2.get(sb_id, (None, 0))
                    if sem is not None:
                        L.wait(L.t, sem, cntv)
                    L.t.matmul(sbank[:], bk[:, j * 128:(j + 1) * 128], qb[:],
                               start=True, stop=True)
                    pe_s[j] = L.inc(L.t, "PE")
                if j >= 1:
                    jy = j - 1
                    L.wait(L.t, "VE", gelf[jy % 3])
                    L.t.matmul(ybank[:], bv[:, jy * 128:(jy + 1) * 128], actb[jy % 3][:],
                               start=(jy == 0), stop=(jy == 31))
                    actf[jy % 3] = L.inc(L.t, "PE")
                if j < 32:
                    sb_id = (0, 1, 3)[j % 3]
                    sbank = pb[sb_id]
                    gbuf = gel[j % 3]
                    abuf = actb[j % 3]
                    L.wait(L.s, "PE", pe_s[j])
                    gf = gelf.get(j % 3)
                    if gf is not None:
                        L.wait(L.s, "VE", gf)
                    L.s.activation(gbuf[:], sbank[:], AF.Gelu)
                    se_gel = L.inc(L.s, "SE")
                    bank_free2[sb_id] = ("SE", se_gel)
                    af = actf.get(j % 3)
                    L.wait(L.v, "SE", se_gel)
                    if af is not None:
                        L.wait(L.v, "PE", af)
                    L.v.tensor_mul(abuf[:], gbuf[:], cwbc[:, (j // 2) * TT:(j // 2 + 1) * TT])
                    gelf[j % 3] = L.inc(L.v, "VE")
            kf_ld[it] = L.cnt["PE"]
            L.wait(L.s, "PE")
            L.s.copy(yTt[:, h * 512:(h + 1) * 512], ybank[:])
            bank_free2[6] = ("SE", L.inc(L.s, "SE"))
        L.wait(L.t, "SE")
        for fc in range(NCH):
            bi = 3 + (fc % 2)
            sem, cntv = bank_free2.get(bi, (None, 0))
            if sem is not None:
                L.wait(L.t, sem, cntv)
            for k in range(NCH):
                L.t.matmul(pb[bi][:],
                           wres_mg[:, k * 1024 + fc * 128: k * 1024 + (fc + 1) * 128],
                           yTt[:, k * 512:(k + 1) * 512],
                           start=(k == 0), stop=(k == NCH - 1))
            pe_mg = L.inc(L.t, "PE")
            stg = g0_st[(tt * NCH + fc) % 2]
            gprev = g0_store.get((tt * NCH + fc) % 2)
            if gprev is not None:
                L.wait(L.s, "ST", gprev)
            L.wait(L.s, "PE", pe_mg)
            L.s.activation(stg[:], pb[bi][:], AF.Silu)
            bank_free2[bi] = ("SE", L.inc(L.s, "SE"))
            L.wait(L.y, "SE")
            g0_store[(tt * NCH + fc) % 2] = L.dma(L.y, "ST", o_g0T[fc * 128:(fc + 1) * 128, tt * TT:(tt + 1) * TT], stg[:])

    L.wait(L.y, "ST")
    return L



def build_B0():
    L = Launch("B0")
    C = declare_consts(L)
    qTh = L.din("b0_q", [8, 64, 4096], BF16)
    kTh = L.din("b0_k", [8, 64, 4096], BF16)
    vh = L.din("b0_v", [8, 128, 32 * 512], BF16)
    o_aoT = L.dout("aoT", [8, 512, 4096], BF16)

    fcol = L.sb("fcol", [128, len(FC_VALS)], F32)
    o64 = L.sb("o64", [64, 1], BF16)
    o128 = L.sb("o128", [128, 1], BF16)
    o1f = L.sb("o1f", [1, 128], F32)
    qta = L.sb("qta", [128, 4096], BF16)
    kta = L.sb("kta", [128, 4096], BF16)
    vsb = L.sb("vsb", [128, 32 * 512], BF16)
    sqx = L.sb("sqx", [64, 4096], BF16)
    PT = L.sb("PT", [128, 32 * 512], BF16)
    nq = L.sb("nq", [1, 4096], F32)
    nk = L.sb("nk", [1, 4096], F32)
    mk1 = L.sb("mk1", [1, 1], F32)
    brow = L.sb("brow", [1, 4096], F32)
    rrow = L.sb("rrow", [1, 512], F32)
    rb_bc = L.sb("rb_bc", [128, 512], F32)
    onesb = L.sb("onesb", [128, 128], BF16)
    ao_st = [L.sb(f"ao_st{i}", [128, 512], BF16) for i in range(2)]

    pb = [L.ps(f"pb{i}", [128, 512], F32) for i in range(7)]
    pu = L.ps("pu", [128, 512], F32)

    L.dma(L.y, "LD", fcol[:], C["fcol"])
    L.dma(L.y, "LD", o64[:], C["ones64c"])
    L.dma(L.y, "LD", o128[:], C["ones128c"])
    L.dma(L.y, "LD", o1f[:], C["ones1f"])
    ld_c = L.cnt["LD"]
    L.wait(L.t, "LD", ld_c)
    L.v.memset(onesb[:], 1.0)
    L.v.memset(qta[64:66, :], 0.0)
    L.v.memset(kta[64:66, :], 0.0)
    L.inc(L.v, "VE")

    F8 = FC[0.125]
    sc18 = fcol[:, F8:F8 + 1]
    bank_free = {}
    ao_free = {}
    pe_unit_done = 0
    for u in range(8):
        L.wait(L.g, "PE", pe_unit_done)
        L.dma(L.g, "GD", qta[0:64, :], qTh[u])
        L.dma(L.g, "GD", kta[0:64, :], kTh[u])
        gd_v = L.dma(L.g, "GD", vsb[:], vh[u])
        L.wait(L.t, "GD", gd_v)
        for (src, dstrow) in ((qta, nq), (kta, nk)):
            L.wait(L.s, "GD", gd_v)
            L.wait(L.s, "VE")
            L.s.activation(sqx[:], src[0:64, :], AF.Square)
            se_sq = L.inc(L.s, "SE")
            L.wait(L.t, "SE", se_sq)
            for t8 in range(8):
                sem, cv = bank_free.get(7, (None, 0))
                if sem is not None:
                    L.wait(L.t, sem, cv)
                L.t.matmul(pu[0:1, 0:512], o64[:], sqx[:, t8 * 512:(t8 + 1) * 512],
                           start=True, stop=True)
                pe_n = L.inc(L.t, "PE")
                L.wait(L.s, "PE", pe_n)
                L.s.activation(dstrow[0:1, t8 * 512:(t8 + 1) * 512], pu[0:1, 0:512], AF.Sqrt)
                bank_free[7] = ("SE", L.inc(L.s, "SE"))
        se_rows = L.cnt["SE"]
        L.wait(L.v, "SE", se_rows)
        L.v.reduce_max(mk1[:], nk[:], axis=AX.X)
        L.selfsync(L.v, "VE")
        L.v.tensor_scalar(brow[:], nq[:], mk1[:], -1.0, op0=ALU.mult, op1=ALU.mult)
        L.v.tensor_copy(qta[64:65, :], brow[:])
        L.v.memset(kta[64:65, :], 1.0)
        ve_aug = L.inc(L.v, "VE")
        L.wait(L.t, "VE", ve_aug)
        for st in range(8):
            for kc in range(32):
                bi = kc % 4
                sem, cv = bank_free.get(bi, (None, 0))
                if sem is not None:
                    L.wait(L.t, sem, cv)
                L.t.matmul(pb[bi][:], kta[0:66, kc * 128:(kc + 1) * 128],
                           qta[0:66, st * 512:(st + 1) * 512], start=True, stop=True)
                pe_s = L.inc(L.t, "PE")
                L.wait(L.s, "PE", pe_s)
                L.s.activation(PT[:, kc * 512:(kc + 1) * 512], pb[bi][:], AF.Exp, scale=sc18)
                bank_free[bi] = ("SE", L.inc(L.s, "SE"))
            se_pt = L.cnt["SE"]
            sem, cv = bank_free.get(5, (None, 0))
            if sem is not None:
                L.wait(L.t, sem, cv)
            L.wait(L.t, "SE", se_pt)
            for kc in range(32):
                L.t.matmul(pb[5][:], onesb[:], PT[:, kc * 512:(kc + 1) * 512],
                           start=(kc == 0), stop=(kc == 31))
            pe_sum = L.inc(L.t, "PE")
            L.wait(L.s, "PE", pe_sum)
            L.wait(L.s, "VE")
            L.s.activation(rb_bc[:], pb[5][:], AF.Ln)
            L.s.activation(rb_bc[:], rb_bc[:], AF.Exp, scale=-1.0)
            se_bc = L.inc(L.s, "SE")
            bank_free[5] = ("SE", se_bc)
            for dvc in range(4):
                bi = 4
                sem, cv = bank_free.get(bi, (None, 0))
                if sem is not None:
                    L.wait(L.t, sem, cv)
                for kc in range(32):
                    L.t.matmul(pb[bi][:], vsb[:, kc * 512 + dvc * 128: kc * 512 + (dvc + 1) * 128],
                               PT[:, kc * 512:(kc + 1) * 512],
                               start=(kc == 0), stop=(kc == 31))
                pe_av = L.inc(L.t, "PE")
                stg = ao_st[dvc % 2]
                af = ao_free.get(dvc % 2)
                L.wait(L.v, "PE", pe_av)
                if dvc == 0:
                    L.wait(L.v, "SE", se_bc)
                if af is not None:
                    L.wait(L.v, "ST", af)
                L.v.tensor_mul(stg[:], pb[bi][:], rb_bc[:])
                bank_free[bi] = ("VE", L.inc(L.v, "VE"))
                L.wait(L.y, "VE")
                ao_free[dvc % 2] = L.dma(L.y, "ST",
                                         o_aoT[u, dvc * 128:(dvc + 1) * 128, st * 512:(st + 1) * 512],
                                         stg[:])
        pe_unit_done = L.cnt["PE"]
    L.wait(L.y, "ST")
    return L


def prep_B0(resA):
    ins = []
    for c in range(NCORES):
        b = c // 2
        h0 = (c % 2) * 8
        qs, ks, vs = [], [], []
        for j in range(8):
            h = h0 + j
            q_u = np.concatenate([resA[2 * b]["qT"][h * 64:(h + 1) * 64, :],
                                  resA[2 * b + 1]["qT"][h * 64:(h + 1) * 64, :]], axis=1)
            k_u = np.concatenate([resA[2 * b]["kT"][h * 64:(h + 1) * 64, :],
                                  resA[2 * b + 1]["kT"][h * 64:(h + 1) * 64, :]], axis=1)
            v0 = host_unpack_v_bf(resA[2 * b]["v_blk"])[:, h * 512:(h + 1) * 512]
            v1 = host_unpack_v_bf(resA[2 * b + 1]["v_blk"])[:, h * 512:(h + 1) * 512]
            v_u = np.concatenate([v0, v1], axis=0)
            v_u = v_u.reshape(32, 128, 512).transpose(1, 0, 2).reshape(128, 32 * 512)
            qs.append(q_u); ks.append(k_u); vs.append(v_u)
        ins.append({"b0_q": np.stack(qs), "b0_k": np.stack(ks), "b0_v": np.stack(vs)})
    return ins


_V_CACHE = {}


def host_unpack_v_bf(v_blk):
    key = id(v_blk)
    if key not in _V_CACHE:
        a = v_blk.reshape(16, 128, 16, 512)
        _V_CACHE[key] = np.ascontiguousarray(a.transpose(2, 1, 0, 3).reshape(NTOK, 8192))
    return _V_CACHE[key]


def unpack_B0(resB):
    outs = []
    for c in range(NCORES):
        b = c // 2
        half = c % 2
        aoT = np.empty((8192, NTOK), dtype=resB[0]["aoT"].dtype)
        for h in range(16):
            src_core = 2 * b + (h // 8)
            j = h % 8
            aoT[h * 512:(h + 1) * 512, :] = resB[src_core]["aoT"][j][:, half * NTOK:(half + 1) * NTOK]
        outs.append(aoT)
    return outs



def build_C0():
    L = Launch("C0")
    C = declare_consts(L)
    aoT = L.din("c0_aoT", [8192, NTOK], BF16)
    WoT = L.din("c0_WoT", [8192, D])
    bo_col = L.din("c0_bo_col", [128, 8])
    xnT = L.din("c0_xnT", [D, NTOK])
    x0T = L.din("c0_x0T", [D, NTOK])
    g0T = L.din("c0_g0T", [D, NTOK])
    o_x1T = L.dout("x1T", [D, NTOK])

    bocol = L.sb("bocol", [128, 8], F32)
    wblk = [L.sb(f"wblk{i}", [128, 8 * 1024], BF16) for i in range(2)]
    ablk = [L.sb(f"ablk{i}", [128, 8 * 512], BF16) for i in range(2)]
    xn_sl = L.sb("xn_sl", [128, 512], F32)
    x0_sl = L.sb("x0_sl", [128, 512], F32)
    g0_sl = L.sb("g0_sl", [128, 512], F32)
    pj = L.sb("pj", [128, 512], F32)
    x1_st = [L.sb(f"x1_st{i}", [128, 512], F32) for i in range(2)]
    pb = [L.ps(f"pb{i}", [128, 512], F32) for i in range(8)]

    L.dma(L.y, "LD", bocol[:], bo_col)
    ld_bo = L.cnt["LD"]
    x1_free = {}
    bank_evac = {}
    blk_used = {}
    for tt in range(NTT):
        for kb in range(8):
            it = tt * 8 + kb
            wb, ab = wblk[it % 2], ablk[it % 2]
            if it >= 2:
                L.wait(L.g, "PE", blk_used[it - 2])
            L.dma(L.g, "GD", slab_dst(wb[:], 8),
                  WoT[kb * 1024:(kb + 1) * 1024, :].rearrange("(c p) f -> p c f", p=128))
            gd = L.dma(L.g, "GD", slab_dst(ab[:], 8),
                       aoT[kb * 1024:(kb + 1) * 1024, tt * TT:(tt + 1) * TT]
                       .rearrange("(c p) t -> p c t", p=128))
            L.wait(L.t, "GD", gd)
            if kb == 0:
                for fc in range(8):
                    sem, cv = bank_evac.get(fc, (None, 0))
                    if sem is not None:
                        L.wait(L.t, sem, cv)
            for fc in range(8):
                for k in range(8):
                    L.t.matmul(pb[fc][:],
                               wb[:, k * 1024 + fc * 128: k * 1024 + (fc + 1) * 128],
                               ab[:, k * 512:(k + 1) * 512],
                               start=(kb == 0 and k == 0), stop=(kb == 7 and k == 7))
            blk_used[it] = L.inc(L.t, "PE")
        pe_tt = L.cnt["PE"]
        for fc in range(8):
            L.wait(L.y, "VE")
            ld1 = L.dma(L.y, "LD", xn_sl[:], xnT[fc * 128:(fc + 1) * 128, tt * TT:(tt + 1) * TT])
            ld2 = L.dma(L.y, "LD", x0_sl[:], x0T[fc * 128:(fc + 1) * 128, tt * TT:(tt + 1) * TT])
            ld3 = L.dma(L.y, "LD", g0_sl[:], g0T[fc * 128:(fc + 1) * 128, tt * TT:(tt + 1) * TT])
            L.wait(L.s, "LD", ld_bo)
            L.wait(L.s, "PE", pe_tt)
            L.wait(L.s, "VE")
            if USE_ACT_LRELU:
                L.s.activation(pj[:], pb[fc][:], AF.Lrelu, bias=bocol[:, fc:fc + 1], alpha=0.01)
                se_pj = L.inc(L.s, "SE")
                bank_evac[fc] = ("SE", se_pj)
                L.wait(L.v, "SE", se_pj)
            else:
                L.wait(L.v, "PE", pe_tt)
                L.v.tensor_scalar(pj[:], pb[fc][:], bocol[:, fc:fc + 1], None, op0=ALU.add)
                L.v.scalar_tensor_tensor(pj[:], pj[:], 0.01, pj[:], op0=ALU.mult, op1=ALU.max)
                bank_evac[fc] = ("VE", L.cnt["VE"] + 1)
            stg = x1_st[fc % 2]
            xf = x1_free.get(fc % 2)
            if xf is not None:
                L.wait(L.v, "ST", xf)
            L.wait(L.v, "LD", ld3)
            L.v.tensor_add(pj[:], pj[:], xn_sl[:])
            L.v.tensor_mul(pj[:], pj[:], g0_sl[:])
            L.v.tensor_add(stg[:], pj[:], x0_sl[:])
            ve_x1 = L.inc(L.v, "VE")
            L.wait(L.y, "VE", ve_x1)
            x1_free[fc % 2] = L.dma(L.y, "ST",
                                    o_x1T[fc * 128:(fc + 1) * 128, tt * TT:(tt + 1) * TT], stg[:])
    L.wait(L.y, "ST")
    return L


def prep_C0(inp):
    return {
        "c0_WoT": fmaj(inp["L0_attn_Wo"]),
        "c0_bo_col": np.ascontiguousarray(np.asarray(inp["L0_attn_bo"]).reshape(8, 128).T),
    }



def build_A1():
    L = Launch("A1")
    C = declare_consts(L)
    x1 = L.din("x1", [NTOK, D])
    Wq = L.din("a1_Wq", [D, D])
    Wf = L.din("a1_Wf", [D, D])
    Wi = L.din("a1_Wi", [D, D])
    gk = L.din("a1_gk", [D, 16])
    gbias = L.din("a1_gbias", [1, 16])
    wg = L.din("a1_wg", [16, D, 512])
    wu = L.din("a1_wu", [16, D, 512])
    wd = L.din("a1_wd", [16, 512, D])
    swg = L.din("a1_swg", [D, D])
    swu = L.din("a1_swu", [D, D])
    swd = L.din("a1_swd", [D, D])

    o_xnT_tiles = L.dout("xn1T_tiles", [NTILE, 128, D])
    o_qT = L.dout("glaqT", [D, NTOK], BF16)
    o_kT = L.dout("glakT", [D, NTOK], BF16)
    o_gT = L.dout("glagT", [D, NTOK])
    o_vT = L.dout("glavT", [D, NTOK], BF16)
    o_g1T = L.dout("g1T", [D, NTOK])

    eyef = L.sb("eyef", [128, 128], F32)
    fcol = L.sb("fcol", [128, len(FC_VALS)], F32)
    sel16 = L.sb("sel16", [16, 16 * 128], BF16)
    o1f = L.sb("o1f", [1, 128], F32)
    xnT_b = L.sb("xnT_b", [128, NCH * NTOK], BF16)
    wpool = [L.sb(f"wpool{i}", [128, 12 * 1024], BF16) for i in range(2)]
    y_acc = L.sb("y_acc", [128, 2 * 8 * 512], F32)
    gsb = L.sb("gsb", [128, 8 * 512], BF16)
    usb = L.sb("usb", [128, 8 * 512], BF16)
    isb = L.sb("isb", [128, 8 * 512], BF16)
    cwe = L.sb("cwe", [128, 512], BF16)
    pj = L.sb("pj", [128, 512], F32)
    pj2 = L.sb("pj2", [128, 512], F32)
    pj3 = L.sb("pj3", [128, 512], F32)
    stg_b = [L.sb(f"stg_b{i}", [128, 512], BF16) for i in range(2)]
    stg_f = [L.sb(f"stg_f{i}", [128, 512], F32) for i in range(2)]
    rbuf = L.sb("rbuf", [128, D], F32)
    gkf = L.sb("gkf", [128, NCH * 16], F32)
    gb128 = L.sb("gb128", [128, 16], F32)
    cw16 = L.sb("cw16", [128, 16], F32)
    cwt1 = L.sb("cwt1", [128, 16], F32)
    cwt2 = L.sb("cwt2", [128, 16], F32)
    cwt3 = L.sb("cwt3", [128, 16], F32)
    cwg = L.sb("cwg", [128, 4], F32)
    cwg2 = L.sb("cwg2", [128, 4], F32)
    cstat = L.sb("cstat", [128, 4], F32)
    cwT16 = L.sb("cwT16", [16, NTOK], BF16)

    pb = [L.ps(f"pb{i}", [128, 512], F32) for i in range(6)]
    pu = L.ps("pu", [128, 512], F32)

    L.dma(L.y, "LD", eyef[:], C["eyef"])
    L.dma(L.y, "LD", fcol[:], C["fcol"])
    L.dma(L.y, "LD", o1f[:], C["ones1f"])
    L.dma(L.g, "GD", sel16[:], C["sel16"])
    ve_ln, ln_st = emit_ln_fm(L, x1, o_xnT_tiles, xnT_b, eyef, pu[:, 0:128], fcol)

    QS = FC[0.08838834764831845]
    store_done = {}
    bank_free = {}

    def gla_proj(W, kind, wslot):
        wp = wpool[wslot]
        L.wait(L.g, "PE")
        gd = L.dma(L.g, "GD", slab_dst(wp[:, 0:NCH * 1024], NCH),
                   W.rearrange("(c p) f -> p c f", p=128))
        L.wait(L.t, "GD", gd)
        L.wait(L.t, "VE", ve_ln)
        for fc in range(NCH):
            for tt in range(NTT):
                bi = (fc * NTT + tt) % 2
                sem, cv = bank_free.get(bi, (None, 0))
                if sem is not None:
                    L.wait(L.t, sem, cv)
                for k in range(NCH):
                    L.t.matmul(pb[bi][:],
                               wp[:, k * 1024 + fc * 128: k * 1024 + (fc + 1) * 128],
                               xnT_b[:, k * NTOK + tt * TT: k * NTOK + (tt + 1) * TT],
                               start=(k == 0), stop=(k == NCH - 1))
                pe_d = L.inc(L.t, "PE")
                key = ("b", fc % 2)
                sd = store_done.get(key)
                if kind == "q":
                    L.wait(L.s, "PE", pe_d)
                    L.wait(L.s, "VE")
                    L.s.activation(pj[:], pb[bi][:], AF.Silu)
                    bank_free[bi] = ("SE", L.inc(L.s, "SE"))
                    stg = stg_b[fc % 2]
                    L.wait(L.v, "SE")
                    if sd is not None:
                        L.wait(L.v, "ST", sd)
                    L.v.tensor_scalar(stg[:], pj[:], fcol[:, QS:QS + 1], None, op0=ALU.mult)
                    ve_d = L.inc(L.v, "VE")
                    L.wait(L.y, "VE", ve_d)
                    store_done[key] = L.dma(L.y, "ST", o_qT[fc * 128:(fc + 1) * 128, tt * TT:(tt + 1) * TT], stg[:])
                elif kind == "i":
                    L.wait(L.s, "PE", pe_d)
                    stg = stg_b[fc % 2]
                    if sd is not None:
                        L.wait(L.s, "ST", sd)
                    L.s.copy(stg[:], pb[bi][:])
                    bank_free[bi] = ("SE", (se_d := L.inc(L.s, "SE")))
                    L.wait(L.y, "SE", se_d)
                    store_done[key] = L.dma(L.y, "ST", o_vT[fc * 128:(fc + 1) * 128, tt * TT:(tt + 1) * TT], stg[:])
                else:
                    L.wait(L.s, "PE", pe_d)
                    stg = stg_b[fc % 2]
                    sdk = store_done.get(("b", fc % 2))
                    if sdk is not None:
                        L.wait(L.s, "ST", sdk)
                    L.s.activation(stg[:], pb[bi][:], AF.Sigmoid, scale=-1.0)
                    se_k = L.inc(L.s, "SE")
                    L.wait(L.y, "SE", se_k)
                    store_done[("b", fc % 2)] = L.dma(L.y, "ST", o_kT[fc * 128:(fc + 1) * 128, tt * TT:(tt + 1) * TT], stg[:])
                    L.wait(L.s, "VE")
                    L.s.copy(pj[:], pb[bi][:])
                    bank_free[bi] = ("SE", (se_f := L.inc(L.s, "SE")))
                    L.wait(L.s, "VE")
                    L.s.activation(pj2[:], pj[:], AF.Abs)
                    L.s.activation(pj2[:], pj2[:], AF.Exp, scale=-1.0)
                    L.s.activation(pj2[:], pj2[:], AF.Ln, bias=1.0)
                    se_l = L.inc(L.s, "SE")
                    stgf = stg_f[fc % 2]
                    sdg = store_done.get(("f", fc % 2))
                    L.wait(L.v, "SE", se_l)
                    if sdg is not None:
                        L.wait(L.v, "ST", sdg)
                    L.v.tensor_scalar_min(pj3[:], pj[:], 0.0)
                    L.v.tensor_sub(stgf[:], pj3[:], pj2[:])
                    ve_g = L.inc(L.v, "VE")
                    L.wait(L.y, "VE", ve_g)
                    store_done[("f", fc % 2)] = L.dma(L.y, "ST", o_gT[fc * 128:(fc + 1) * 128, tt * TT:(tt + 1) * TT], stgf[:])

    gla_proj(Wq, "q", 0)
    gla_proj(Wf, "f", 1)
    gla_proj(Wi, "i", 0)
    pe_after_gla = L.cnt["PE"]

    L.wait(L.y, "PE")
    ld_gk = L.dma(L.y, "LD", slab_dst(gkf[:], NCH), gk.rearrange("(c p) e -> p c e", p=128))
    gbrow = L.sb("gbrow", [1, 16], F32)
    ld_gb = L.dma(L.y, "LD", gbrow[:], gbias)
    L.wait(L.t, "LD", ld_gb)
    L.wait(L.t, "VE")
    L.t.matmul(pu[:, 0:16], o1f[:], gbrow[:], start=True, stop=True)
    pe_gb = L.inc(L.t, "PE")
    L.wait(L.v, "PE", pe_gb)
    L.v.tensor_copy(gb128[:], pu[:, 0:16])
    ve_gb = L.inc(L.v, "VE")
    pu_free_t = ve_gb
    rb_free = 0
    for i in range(NTILE):
        L.wait(L.y, "ST", ln_st[i])
        if rb_free:
            L.wait(L.y, "PE", rb_free)
        ld_rb = L.dma(L.y, "LD", rbuf[:], o_xnT_tiles[i])
        L.wait(L.t, "LD", max(ld_gk, ld_rb))
        L.wait(L.t, "VE", pu_free_t)
        for c in range(NCH):
            L.t.matmul(pu[:, 0:16],
                       rbuf[:, c * 128:(c + 1) * 128],
                       gkf[:, c * 16:(c + 1) * 16],
                       start=(c == 0), stop=(c == NCH - 1))
        pe_r = L.inc(L.t, "PE")
        rb_free = pe_r
        L.wait(L.s, "PE", pe_r)
        L.wait(L.s, "VE")
        L.s.activation(cw16[:], pu[:, 0:16], AF.Sigmoid)
        se_sc = L.inc(L.s, "SE")
        L.wait(L.v, "SE", se_sc)
        L.v.tensor_add(cwt1[:], cw16[:], gb128[:])
        m3 = cwt1[:].rearrange("p (g e) -> p g e", g=4)
        L.v.tensor_reduce(cwg[:], m3, axis=AX.X, op=ALU.max)
        bv = cwg[:].unsqueeze(2).broadcast_to([128, 4, 4])
        t3 = cwt2[:].rearrange("p (g e) -> p g e", g=4)
        L.v.tensor_tensor(t3, m3, bv, op=ALU.is_equal)
        L.v.scalar_tensor_tensor(cwt2[:], cwt2[:], -1e30, cwt1[:], op0=ALU.mult, op1=ALU.add)
        t3b = cwt2[:].rearrange("p (g e) -> p g e", g=4)
        L.v.tensor_reduce(cwg2[:], t3b, axis=AX.X, op=ALU.max)
        L.v.tensor_add(cwg[:], cwg[:], cwg2[:])
        L.v.reduce_max(cstat[:, 0:1], cwg[:], axis=AX.X)
        L.v.tensor_scalar(cwg2[:], cwg[:], cstat[:, 0:1], None, op0=ALU.is_equal)
        L.v.scalar_tensor_tensor(cwg2[:], cwg2[:], -1e30, cwg[:], op0=ALU.mult, op1=ALU.add)
        L.v.reduce_max(cstat[:, 1:2], cwg2[:], axis=AX.X)
        L.v.tensor_scalar(cwg2[:], cwg[:], cstat[:, 1:2], None, op0=ALU.is_lt)
        L.v.tensor_scalar(cwg2[:], cwg2[:], -1.0, 1.0, op0=ALU.mult, op1=ALU.add)
        bg = cwg2[:].unsqueeze(2).broadcast_to([128, 4, 4])
        t3c = cwt3[:].rearrange("p (g e) -> p g e", g=4)
        L.v.tensor_copy(t3c, bg)
        L.v.tensor_mul(cwt3[:], cwt3[:], cwt1[:])
        L.v.tensor_copy(cwt2[:], cwt3[:])
        for r in range(4):
            L.v.reduce_max(cstat[:, 2:3], cwt2[:], axis=AX.X)
            if r < 3:
                L.v.tensor_scalar(cw16[:], cwt2[:], cstat[:, 2:3], None, op0=ALU.is_equal)
                L.v.scalar_tensor_tensor(cwt2[:], cw16[:], -1e30, cwt2[:], op0=ALU.mult, op1=ALU.add)
        L.v.tensor_scalar(cw16[:], cwt3[:], cstat[:, 2:3], None, op0=ALU.is_lt)
        L.v.tensor_scalar(cw16[:], cw16[:], -1.0, 1.0, op0=ALU.mult, op1=ALU.add)
        L.v.tensor_mul(cw16[:], cw16[:], cwt3[:])
        L.v.tensor_scalar(cwt2[:], cwt3[:], 0.0, None, op0=ALU.is_le)
        L.v.tensor_scalar(cwt2[:], cwt2[:], -1.0, 1.0, op0=ALU.mult, op1=ALU.add)
        L.v.tensor_mul(cw16[:], cw16[:], cwt2[:])
        L.v.reduce_sum(cstat[:, 3:4], cw16[:], axis=AX.X)
        L.v.tensor_scalar(cstat[:, 3:4], cstat[:, 3:4], fcol[0:128, FC[1e-20]:FC[1e-20] + 1], None, op0=ALU.add)
        ve_den = L.inc(L.v, "VE")
        L.wait(L.v, "VE", ve_den)
        L.v.reciprocal(cstat[:, 3:4], cstat[:, 3:4])
        L.selfsync(L.v, "VE")
        L.v.tensor_scalar(cw16[:], cw16[:], cstat[:, 3:4], fcol[0:128, FC[2.5]:FC[2.5] + 1],
                          op0=ALU.mult, op1=ALU.mult)
        ve_cw = L.inc(L.v, "VE")
        L.wait(L.t, "VE", ve_cw)
        L.t.transpose(pu[0:16, 128:256], cw16[:], eyef[:])
        pe_t = L.inc(L.t, "PE")
        L.wait(L.v, "PE", pe_t)
        L.v.tensor_copy(cwT16[:, i * 128:(i + 1) * 128], pu[0:16, 128:256])
        pu_free_t = L.inc(L.v, "VE")

    gd_we = {-2: pe_after_gla, -1: pe_after_gla}
    for half in range(2):
        tts = (2 * half, 2 * half + 1)
        L.wait(L.v, "PE")
        L.wait(L.v, "SE")
        L.v.memset(y_acc[:], 0.0)
        ve_ya = L.inc(L.v, "VE")
        for e in range(16):
            it = half * 16 + e
            wp = wpool[it % 2]
            L.wait(L.g, "PE", gd_we[it - 2])
            L.dma(L.g, "GD", slab_dst(wp[:, 0:8 * 512], NCH), wg[e].rearrange("(c p) f -> p c f", p=128))
            L.dma(L.g, "GD", slab_dst(wp[:, 8 * 512:16 * 512], NCH), wu[e].rearrange("(c p) f -> p c f", p=128))
            gd = L.dma(L.g, "GD", slab_dst(wp[:, 16 * 512:16 * 512 + 4 * 1024], 4),
                       wd[e].rearrange("(c p) f -> p c f", p=128))
            for tt in tts:
                ts = (tt % 2) * 8
                L.wait(L.t, "VE")
                L.t.matmul(pu[:], sel16[:, e * 128:(e + 1) * 128],
                           cwT16[:, tt * TT:(tt + 1) * TT], start=True, stop=True)
                pe_ce = L.inc(L.t, "PE")
                L.wait(L.v, "PE", pe_ce)
                L.v.tensor_copy(cwe[:], pu[:])
                ve_ce = L.inc(L.v, "VE")
                if tt == tts[0]:
                    L.wait(L.t, "GD", gd)
                for (w0, dst, actk) in ((0, gsb, "silu"), (8 * 512, usb, "copy")):
                    for oc in range(4):
                        bi = oc
                        sem, cv = bank_free.get(bi, (None, 0))
                        if sem is not None:
                            L.wait(L.t, sem, cv)
                        for k in range(NCH):
                            L.t.matmul(pb[bi][:],
                                       wp[:, w0 + k * 512 + oc * 128: w0 + k * 512 + (oc + 1) * 128],
                                       xnT_b[:, k * NTOK + tt * TT: k * NTOK + (tt + 1) * TT],
                                       start=(k == 0), stop=(k == NCH - 1))
                        pe_d = L.inc(L.t, "PE")
                        L.wait(L.s, "PE", pe_d)
                        L.wait(L.s, "VE")
                        if actk == "silu":
                            L.s.activation(dst[:, oc * 512:(oc + 1) * 512], pb[bi][:], AF.Silu)
                        else:
                            L.s.copy(dst[:, oc * 512:(oc + 1) * 512], pb[bi][:])
                        bank_free[bi] = ("SE", L.inc(L.s, "SE"))
                se_gu = L.cnt["SE"]
                L.wait(L.v, "SE", se_gu)
                L.wait(L.v, "VE", ve_ce)
                for oc in range(4):
                    L.v.tensor_mul(isb[:, oc * 512:(oc + 1) * 512], gsb[:, oc * 512:(oc + 1) * 512],
                                   usb[:, oc * 512:(oc + 1) * 512])
                    L.v.tensor_mul(isb[:, oc * 512:(oc + 1) * 512], isb[:, oc * 512:(oc + 1) * 512], cwe[:])
                ve_i = L.inc(L.v, "VE")
                if e == 0 and tt == tts[0]:
                    L.wait(L.t, "VE", max(ve_i, ve_ya))
                else:
                    L.wait(L.t, "VE", ve_i)
                for oc in range(8):
                    bi = 4 + (oc % 2)
                    sem, cv = bank_free.get(bi, (None, 0))
                    if sem is not None:
                        L.wait(L.t, sem, cv)
                    for k in range(4):
                        L.t.matmul(pb[bi][:],
                                   wp[:, 16 * 512 + k * 1024 + oc * 128: 16 * 512 + k * 1024 + (oc + 1) * 128],
                                   isb[:, k * 512:(k + 1) * 512],
                                   start=(k == 0), stop=(k == 3))
                    pe_y = L.inc(L.t, "PE")
                    L.wait(L.v, "PE", pe_y)
                    L.v.tensor_add(y_acc[:, (ts + oc) * 512:(ts + oc + 1) * 512],
                                   y_acc[:, (ts + oc) * 512:(ts + oc + 1) * 512], pb[bi][:])
                    bank_free[bi] = ("VE", L.inc(L.v, "VE"))
            gd_we[it] = L.cnt["PE"]

        wp0, wp1 = wpool[0], wpool[1]
        last_it = half * 16 + 15
        L.wait(L.g, "PE", gd_we[last_it])
        gd_sg = L.dma(L.g, "GD", slab_dst(wp0[:, 0:8 * 1024], NCH), swg.rearrange("(c p) f -> p c f", p=128))
        gd_su = L.dma(L.g, "GD", slab_dst(wp1[:, 0:8 * 1024], NCH), swu.rearrange("(c p) f -> p c f", p=128))
        for tt in tts:
            for (wpx, gdx, dst, actk) in ((wp0, gd_sg, gsb, "silu"), (wp1, gd_su, usb, "copy")):
                L.wait(L.t, "GD", gdx)
                for oc in range(8):
                    bi = oc % 4
                    sem, cv = bank_free.get(bi, (None, 0))
                    if sem is not None:
                        L.wait(L.t, sem, cv)
                    for k in range(NCH):
                        L.t.matmul(pb[bi][:],
                                   wpx[:, k * 1024 + oc * 128: k * 1024 + (oc + 1) * 128],
                                   xnT_b[:, k * NTOK + tt * TT: k * NTOK + (tt + 1) * TT],
                                   start=(k == 0), stop=(k == NCH - 1))
                    pe_d = L.inc(L.t, "PE")
                    L.wait(L.s, "PE", pe_d)
                    L.wait(L.s, "VE")
                    if actk == "silu":
                        L.s.activation(dst[:, oc * 512:(oc + 1) * 512], pb[bi][:], AF.Silu)
                    else:
                        L.s.copy(dst[:, oc * 512:(oc + 1) * 512], pb[bi][:])
                    bank_free[bi] = ("SE", L.inc(L.s, "SE"))
            se_s = L.cnt["SE"]
            L.wait(L.v, "SE", se_s)
            idst = isb if tt == tts[0] else gsb
            for oc in range(8):
                L.v.tensor_mul(idst[:, oc * 512:(oc + 1) * 512], gsb[:, oc * 512:(oc + 1) * 512],
                               usb[:, oc * 512:(oc + 1) * 512])
            L.inc(L.v, "VE")
        ve_si = L.cnt["VE"]
        pe_su = L.cnt["PE"]
        L.wait(L.g, "PE", pe_su)
        gd_sd = L.dma(L.g, "GD", slab_dst(wp0[:, 0:8 * 1024], NCH), swd.rearrange("(c p) f -> p c f", p=128))
        L.wait(L.t, "GD", gd_sd)
        L.wait(L.t, "VE", ve_si)
        for tt in tts:
            src = isb if tt == tts[0] else gsb
            ts = (tt % 2) * 8
            for oc in range(8):
                bi = 4 + (oc % 2)
                sem, cv = bank_free.get(bi, (None, 0))
                if sem is not None:
                    L.wait(L.t, sem, cv)
                for k in range(NCH):
                    L.t.matmul(pb[bi][:],
                               wp0[:, k * 1024 + oc * 128: k * 1024 + (oc + 1) * 128],
                               src[:, k * 512:(k + 1) * 512],
                               start=(k == 0), stop=(k == NCH - 1))
                pe_y = L.inc(L.t, "PE")
                L.wait(L.v, "PE", pe_y)
                L.v.tensor_add(y_acc[:, (ts + oc) * 512:(ts + oc + 1) * 512],
                               y_acc[:, (ts + oc) * 512:(ts + oc + 1) * 512], pb[bi][:])
                bank_free[bi] = ("VE", L.inc(L.v, "VE"))
        gd_we[(half + 1) * 16 - 1] = L.cnt["PE"]
        gd_we[(half + 1) * 16 - 2] = L.cnt["PE"]
        ve_yfin = L.cnt["VE"]
        for tt in tts:
            ts = (tt % 2) * 8
            for oc in range(8):
                stg = stg_f[oc % 2]
                sd = store_done.get(("f", oc % 2))
                L.wait(L.s, "VE", ve_yfin)
                if sd is not None:
                    L.wait(L.s, "ST", sd)
                L.s.activation(stg[:], y_acc[:, (ts + oc) * 512:(ts + oc + 1) * 512], AF.Silu)
                se_g1 = L.inc(L.s, "SE")
                L.wait(L.y, "SE", se_g1)
                store_done[("f", oc % 2)] = L.dma(L.y, "ST",
                    o_g1T[oc * 128:(oc + 1) * 128, tt * TT:(tt + 1) * TT], stg[:])
    L.wait(L.y, "ST")
    return L


def prep_A1_weights(inp):
    return {
        "a1_Wq": np.ascontiguousarray(np.asarray(inp["L1_attn_Wq"], np.float32)),
        "a1_Wf": np.ascontiguousarray(np.asarray(inp["L1_attn_Wf"], np.float32)),
        "a1_Wi": np.ascontiguousarray(np.asarray(inp["L1_attn_Wi"], np.float32)),
        "a1_gk": np.ascontiguousarray(np.asarray(inp["L1_ffn_gate_kernel"], np.float32)),
        "a1_gbias": np.asarray(inp["L1_ffn_gate_bias"], np.float32).reshape(1, 16),
        "a1_wg": np.ascontiguousarray(np.asarray(inp["L1_ffn_w_gate"], np.float32)),
        "a1_wu": np.ascontiguousarray(np.asarray(inp["L1_ffn_w_up"], np.float32)),
        "a1_wd": np.ascontiguousarray(np.asarray(inp["L1_ffn_w_down"], np.float32)),
        "a1_swg": np.ascontiguousarray(np.asarray(inp["L1_ffn_sw_gate"], np.float32)),
        "a1_swu": np.ascontiguousarray(np.asarray(inp["L1_ffn_sw_up"], np.float32)),
        "a1_swd": np.ascontiguousarray(np.asarray(inp["L1_ffn_sw_down"], np.float32)),
    }



def build_B1():
    L = Launch("B1")
    C = declare_consts(L)
    qTi = L.din("b1_qT", [4, 128, 4096], BF16)
    kTi = L.din("b1_kT", [4, 128, 4096], BF16)
    gTi = L.din("b1_gT", [4, 128, 4096])
    vti = L.din("b1_vt", [4, 64, 64 * 128], BF16)
    o_oT = L.dout("oT", [4, 128, 4096])

    eyeb = L.sb("eyeb", [128, 128], BF16)
    utri = L.sb("utri", [32, 32], BF16)
    qT = L.sb("qT", [128, 4096], BF16)
    kT = L.sb("kT", [128, 4096], BF16)
    gT = L.sb("gT", [128, 4096], F32)
    GT = L.sb("GT", [128, 4096], F32)
    E1 = L.sb("E1", [128, 4096], F32)
    etmp = L.sb("etmp", [128, 4096], F32)
    maskc = L.sb("maskc", [128, 4096], F32)
    qe = L.sb("qe", [128, 4096], BF16)
    ke = L.sb("ke", [128, 4096], BF16)
    khT = L.sb("khT", [128, 4096], BF16)
    khtok = L.sb("khtok", [64, 64 * 128], BF16)
    a_sb = L.sb("a_sb", [64, 64 * 32], BF16)
    vt = L.sb("vt", [64, 64 * 128], BF16)
    oT = L.sb("oT_sb", [128, 4096], F32)
    Lc = L.sb("Lc", [128, 128], F32)
    S_f = L.sb("S_f", [128, 128], F32)
    S_b = L.sb("S_b", [128, 128], BF16)

    pO2 = [L.ps(f"pO{i}", [128, 512], F32) for i in range(2)]
    pU = L.ps("pU", [128, 512], F32)
    pS = L.ps("pS", [128, 512], F32)
    pSb = L.ps("pSb", [128, 256], BF16)

    L.dma(L.y, "LD", eyeb[:], C["eyeb"])
    L.dma(L.y, "LD", utri[:], C["utri32"])
    L.wait(L.v, "LD")
    L.v.memset(maskc[:], 1.0)
    mview = maskc[:].rearrange("p (c w) -> p c w", w=32)[:, :, 0:1]
    L.v.memset(mview, 0.0)
    ve_mask = L.inc(L.v, "VE")

    ost_done = 0
    pe_unit = 0
    for u in range(4):
        L.wait(L.g, "PE", pe_unit)
        L.wait(L.g, "VE")
        L.wait(L.g, "SE")
        L.dma(L.g, "GD", qT[:], qTi[u])
        L.dma(L.g, "GD", kT[:], kTi[u])
        L.dma(L.g, "GD", vt[:], vti[u])
        gd_a = L.dma(L.g, "GD", gT[:], gTi[u])
        L.wait(L.v, "GD", gd_a)
        L.wait(L.v, "VE", ve_mask)
        L.wait(L.v, "SE")
        L.v.tensor_tensor_scan(GT[:], maskc[:], gT[:], 0.0, op0=ALU.mult, op1=ALU.add)
        ve_gt = L.inc(L.v, "VE")
        L.wait(L.s, "VE", ve_gt)
        L.s.activation(E1[:], GT[:], AF.Exp)
        L.s.activation(etmp[:], GT[:], AF.Exp, scale=-1.0)
        se_e = L.inc(L.s, "SE")
        L.wait(L.v, "SE", se_e)
        L.v.tensor_mul(qe[:], qT[:], E1[:])
        L.v.tensor_mul(ke[:], kT[:], etmp[:])
        gcb = GT[:].rearrange("p (c w) -> p c w", w=32)[:, :, 31:32].broadcast_to([128, 128, 32])
        gh3 = etmp[:].rearrange("p (c w) -> p c w", w=32)
        gt3 = GT[:].rearrange("p (c w) -> p c w", w=32)
        L.v.tensor_tensor(gh3, gt3, gcb, op=ALU.subtract)
        L.v.tensor_copy(Lc[:], E1[:].rearrange("p (c w) -> p c w", w=32)[:, :, 31:32].squeeze(2))
        ve_gh = L.inc(L.v, "VE")
        L.wait(L.s, "VE", ve_gh)
        L.s.activation(etmp[:], etmp[:], AF.Exp, scale=-1.0)
        se_e3 = L.inc(L.s, "SE")
        L.wait(L.v, "SE", se_e3)
        L.v.tensor_mul(khT[:], kT[:], etmp[:])
        ve_a4 = L.inc(L.v, "VE")
        def khs(c):
            return khtok[(c % 2) * 32:(c % 2) * 32 + 32, (c // 2) * 128:(c // 2) * 128 + 128]

        def vts(c):
            return vt[(c % 2) * 32:(c % 2) * 32 + 32, (c // 2) * 128:(c // 2) * 128 + 128]

        def ass(c):
            return a_sb[(c % 2) * 32:(c % 2) * 32 + 32, (c // 2) * 32:(c // 2) * 32 + 32]

        L.wait(L.t, "VE", ve_a4)
        bank_free = {}
        for c in range(128):
            sem, cv = bank_free.get("pS", (None, 0))
            if sem is not None:
                L.wait(L.t, sem, cv)
            L.t.transpose(pSb[0:32, 0:128], khT[:, c * 32:(c + 1) * 32], eyeb[:])
            pe_t = L.inc(L.t, "PE")
            L.wait(L.v, "PE", pe_t)
            L.v.tensor_copy(khs(c), pSb[0:32, 0:128])
            ve_t = L.inc(L.v, "VE")
            L.wait(L.t, "VE", ve_t)
            L.t.matmul(pS[0:32, 128:160], ke[:, c * 32:(c + 1) * 32], qe[:, c * 32:(c + 1) * 32],
                       start=True, stop=True)
            pe_a = L.inc(L.t, "PE")
            L.wait(L.v, "PE", pe_a)
            L.v.tensor_mul(ass(c), pS[0:32, 128:160], utri[:])
            bank_free["pS"] = ("VE", L.inc(L.v, "VE"))
        ve_a2 = L.cnt["VE"]
        L.wait(L.v, "PE")
        L.wait(L.v, "ST", ost_done)
        L.v.memset(S_f[:], 0.0)
        L.v.memset(S_b[:], 0.0)
        ve_s0 = L.inc(L.v, "VE")
        L.wait(L.t, "VE", max(ve_a2, ve_s0))
        pOfree = {}
        pUfree = 0
        for c in range(128):
            sl = pOfree.get(c % 2)
            if sl is not None:
                L.wait(L.t, "VE", sl)
            pov = pO2[c % 2][:, 0:32]
            L.t.matmul(pov, S_b[:], qe[:, c * 32:(c + 1) * 32], start=True, stop=False)
            L.t.matmul(pov, vts(c), ass(c), start=False, stop=True)
            pe_o = L.inc(L.t, "PE")
            if pUfree:
                L.wait(L.t, "VE", pUfree)
            L.t.matmul(pU[:, 0:128], khs(c), vts(c), start=True, stop=True)
            pe_u = L.inc(L.t, "PE")
            L.wait(L.v, "PE", pe_o)
            L.v.tensor_copy(oT[:, c * 32:(c + 1) * 32], pov)
            pOfree[c % 2] = L.inc(L.v, "VE")
            L.wait(L.v, "PE", pe_u)
            L.v.scalar_tensor_tensor(S_f[:], S_f[:], Lc[:, c:c + 1], pU[:, 0:128], op0=ALU.mult, op1=ALU.add)
            L.v.tensor_copy(S_b[:], S_f[:])
            pUfree = L.inc(L.v, "VE")
            L.wait(L.t, "VE", pUfree)
        ve_fin = L.cnt["VE"]
        L.wait(L.y, "VE", ve_fin)
        L.wait(L.y, "ST", ost_done)
        ost_done = L.dma(L.y, "ST", o_oT[u], oT[:])
        pe_unit = L.cnt["PE"]
    L.wait(L.y, "ST")
    return L


def prep_B1(resA1):
    ins = []
    for c in range(NCORES):
        b = c // 2
        qs, ks, gs, vs = [], [], [], []
        for j in range(4):
            h = (c % 2) * 4 + j
            rows = slice(h * 128, (h + 1) * 128)
            q_u = np.concatenate([resA1[2 * b]["glaqT"][rows], resA1[2 * b + 1]["glaqT"][rows]], axis=1)
            k_u = np.concatenate([resA1[2 * b]["glakT"][rows], resA1[2 * b + 1]["glakT"][rows]], axis=1)
            g_u = np.concatenate([resA1[2 * b]["glagT"][rows], resA1[2 * b + 1]["glagT"][rows]], axis=1)
            v_u = np.concatenate([resA1[2 * b]["glavT"][rows], resA1[2 * b + 1]["glavT"][rows]], axis=1)
            vtok = np.ascontiguousarray(v_u.T)
            vt = vtok.reshape(64, 2, 32, 128).transpose(1, 2, 0, 3).reshape(64, 64 * 128)
            qs.append(q_u); ks.append(k_u); gs.append(g_u); vs.append(np.ascontiguousarray(vt))
        ins.append({"b1_qT": np.stack(qs), "b1_kT": np.stack(ks),
                    "b1_gT": np.stack(gs), "b1_vt": np.stack(vs)})
    return ins


def unpack_B1(resB1):
    outs = []
    for c in range(NCORES):
        b = c // 2
        half = c % 2
        oT = np.empty((D, NTOK), np.float32)
        for h in range(8):
            src_core = 2 * b + (h // 4)
            j = h % 4
            oT[h * 128:(h + 1) * 128, :] = resB1[src_core]["oT"][j][:, half * NTOK:(half + 1) * NTOK]
        outs.append(oT)
    return outs



def build_C1():
    L = Launch("C1")
    C = declare_consts(L)
    oT = L.din("c1_oT", [D, NTOK])
    Wo = L.din("c1_Wo", [D, D])
    gw_col = L.din("c1_gw_col", [128, 8])
    x1T = L.din("c1_x1T", [D, NTOK])
    g1T = L.din("c1_g1T", [D, NTOK])
    o_x2T = L.dout("x2T", [D, NTOK])

    fcol = L.sb("fcol", [128, len(FC_VALS)], F32)
    o128 = L.sb("o128", [128, 1], BF16)
    o1f = L.sb("o1f", [1, 128], F32)
    gwc = L.sb("gwc", [128, 8], F32)
    wres = L.sb("wres", [128, NCH * 1024], BF16)
    oTb = L.sb("oTb", [128, NCH * 512], F32)
    sqb = L.sb("sqb", [128, NCH * 512], BF16)
    onT = L.sb("onT", [128, NCH * 512], BF16)
    rstd = L.sb("rstd", [1, 512], F32)
    rb = L.sb("rb", [128, 512], F32)
    x1_sl = L.sb("x1_sl", [128, 512], F32)
    g1_sl = L.sb("g1_sl", [128, 512], F32)
    tmp = L.sb("tmp", [128, 512], F32)
    x2_st = [L.sb(f"x2_st{i}", [128, 512], F32) for i in range(2)]

    pb = [L.ps(f"pb{i}", [128, 512], F32) for i in range(4)]
    pu = L.ps("pu", [128, 512], F32)

    L.dma(L.y, "LD", fcol[:], C["fcol"])
    L.dma(L.y, "LD", o1f[:], C["ones1f"])
    L.dma(L.y, "LD", gwc[:], gw_col)
    L.dma(L.g, "GD", o128[:], C["ones128c"])
    gd_w = L.dma(L.g, "GD", slab_dst(wres[:], NCH), Wo.rearrange("(c p) f -> p c f", p=128))
    ld_c = L.cnt["LD"]
    x2_free = {}
    bank_free = {}
    for tt in range(NTT):
        L.wait(L.y, "SE")
        L.wait(L.y, "VE")
        ld_o = L.dma(L.y, "LD", oTb[:].rearrange("p (c t) -> p c t", c=NCH),
                     oT[:, tt * TT:(tt + 1) * TT].rearrange("(c p) t -> p c t", p=128))
        L.wait(L.s, "LD", max(ld_o, ld_c))
        L.wait(L.s, "VE")
        L.s.activation(sqb[:], oTb[:], AF.Square)
        se_sq = L.inc(L.s, "SE")
        sem, cv = bank_free.get("u", (None, 0))
        if sem is not None:
            L.wait(L.t, sem, cv)
        L.wait(L.t, "SE", se_sq)
        L.wait(L.t, "GD")
        for k in range(NCH):
            L.t.matmul(pu[0:1, 0:512], o128[:], sqb[:, k * 512:(k + 1) * 512],
                       start=(k == 0), stop=(k == NCH - 1))
        pe_ss = L.inc(L.t, "PE")
        L.wait(L.s, "PE", pe_ss)
        L.s.mul(rstd[:], pu[0:1, 0:512], 1.0 / D)
        L.s.activation(rstd[:], rstd[:], AF.Ln, bias=fcol[0:1, FC[1e-5]:FC[1e-5] + 1])
        L.s.activation(rstd[:], rstd[:], AF.Exp, scale=fcol[0:1, FC[-0.5]:FC[-0.5] + 1])
        bank_free["u"] = ("SE", (se_r := L.inc(L.s, "SE")))
        sem, cv = bank_free.get("b", (None, 0))
        if sem is not None:
            L.wait(L.t, sem, cv)
        L.wait(L.t, "SE", se_r)
        L.t.matmul(pb[3][:], o1f[:], rstd[:], start=True, stop=True)
        pe_bc = L.inc(L.t, "PE")
        L.wait(L.v, "PE", pe_bc)
        L.v.tensor_copy(rb[:], pb[3][:])
        bank_free["b"] = ("VE", L.inc(L.v, "VE"))
        for k in range(NCH):
            L.v.tensor_mul(tmp[:], oTb[:, k * 512:(k + 1) * 512], rb[:])
            L.v.tensor_scalar(onT[:, k * 512:(k + 1) * 512], tmp[:], gwc[:, k:k + 1], None, op0=ALU.mult)
        ve_on = L.inc(L.v, "VE")
        L.wait(L.t, "VE", ve_on)
        for fc in range(NCH):
            bi = fc % 2
            sem, cv = bank_free.get(bi, (None, 0))
            if sem is not None:
                L.wait(L.t, sem, cv)
            for k in range(NCH):
                L.t.matmul(pb[bi][:],
                           wres[:, k * 1024 + fc * 128: k * 1024 + (fc + 1) * 128],
                           onT[:, k * 512:(k + 1) * 512],
                           start=(k == 0), stop=(k == NCH - 1))
            pe_p = L.inc(L.t, "PE")
            L.wait(L.y, "VE")
            ld1 = L.dma(L.y, "LD", x1_sl[:], x1T[fc * 128:(fc + 1) * 128, tt * TT:(tt + 1) * TT])
            ld2 = L.dma(L.y, "LD", g1_sl[:], g1T[fc * 128:(fc + 1) * 128, tt * TT:(tt + 1) * TT])
            stg = x2_st[fc % 2]
            xf = x2_free.get(fc % 2)
            L.wait(L.v, "PE", pe_p)
            L.wait(L.v, "LD", ld2)
            if xf is not None:
                L.wait(L.v, "ST", xf)
            L.v.tensor_mul(tmp[:], pb[bi][:], g1_sl[:])
            bank_free[bi] = ("VE", L.inc(L.v, "VE"))
            L.v.tensor_add(stg[:], tmp[:], x1_sl[:])
            ve_x2 = L.inc(L.v, "VE")
            L.wait(L.y, "VE", ve_x2)
            x2_free[fc % 2] = L.dma(L.y, "ST",
                                    o_x2T[fc * 128:(fc + 1) * 128, tt * TT:(tt + 1) * TT], stg[:])
    L.wait(L.y, "ST")
    return L


def prep_C1(inp):
    return {
        "c1_Wo": np.ascontiguousarray(np.asarray(inp["L1_attn_Wo"], np.float32)),
        "c1_gw_col": np.ascontiguousarray(np.asarray(inp["L1_attn_g_weight"], np.float32).reshape(8, 128).T),
    }



HB = 64
NTOKH = NTOK + 2 * HB


def build_A2():
    L = Launch("A2")
    C = declare_consts(L)
    x2h = L.din("x2h", [NTOKH, D])
    gwT = L.din("a2_gwT", [D, 9216])
    vwT = L.din("a2_vwT", [D, D])
    gamma_c = L.din("a2_gamma", [128, 8])
    beta_c = L.din("a2_beta", [128, 8])
    o_xnT_tiles = L.dout("xn2T_tiles", [NTOKH // 128, 128, D])
    o_s2T = L.dout("s2T", [D, NTOK])

    eyef = L.sb("eyef", [128, 128], F32)
    fcol = L.sb("fcol", [128, len(FC_VALS)], F32)
    o1f = L.sb("o1f", [1, 128], F32)
    o128f = L.sb("o128f", [128, 1], F32)
    o128c = L.sb("o128c", [128, 1], BF16)
    gmc = L.sb("gmc", [128, 8], F32)
    btc = L.sb("btc", [128, 8], F32)
    xnT_b = L.sb("xnT_b", [128, NCH * NTOKH], BF16)
    vT = L.sb("vT", [128, NCH * NTOKH], BF16)
    acc = L.sb("acc", [128, NCH * 512], F32)
    wres = L.sb("wres", [128, NCH * 1024], BF16)
    wstr = [L.sb(f"wstr{i}", [128, NCH * 512], BF16) for i in range(2)]
    gate_sb = [L.sb(f"gate_sb{i}", [128, 512], BF16) for i in range(2)]
    tmp_b = L.sb("tmp_b", [128, 512], BF16)
    sqb = L.sb("sqb", [128, NCH * 512], BF16)
    mrow = L.sb("mrow", [1, 512], F32)
    vrow = L.sb("vrow", [1, 512], F32)
    mbc = L.sb("mbc", [128, 512], F32)
    rbc = L.sb("rbc", [128, 512], F32)
    tmpf = L.sb("tmpf", [128, 512], F32)
    s2_st = [L.sb(f"s2_st{i}", [128, 512], F32) for i in range(2)]

    pb = [L.ps(f"pb{i}", [128, 512], F32) for i in range(4)]
    pu = L.ps("pu", [128, 512], F32)

    L.dma(L.y, "LD", eyef[:], C["eyef"])
    L.dma(L.y, "LD", fcol[:], C["fcol"])
    L.dma(L.y, "LD", o1f[:], C["ones1f"])
    L.dma(L.y, "LD", o128f[:], C["ones128f"])
    L.dma(L.y, "LD", gmc[:], gamma_c)
    ld_c = L.dma(L.y, "LD", btc[:], beta_c)
    L.dma(L.g, "GD", o128c[:], C["ones128c"])
    ve_ln, ln_st = emit_ln_fm(L, x2h, o_xnT_tiles, xnT_b, eyef, pu[:, 0:128], fcol, ntok=NTOKH)

    gd_vw = L.dma(L.g, "GD", slab_dst(wres[:], NCH), vwT.rearrange("(c p) f -> p c f", p=128))
    L.wait(L.t, "GD", gd_vw)
    L.wait(L.t, "VE", ve_ln)
    bank_free = {}
    for fc in range(NCH):
        for i in range(NTOKH // 128):
            bi = (fc * (NTOKH // 128) + i) % 2
            sem, cv = bank_free.get(bi, (None, 0))
            if sem is not None:
                L.wait(L.t, sem, cv)
            for k in range(NCH):
                L.t.matmul(pb[bi][:, 0:128],
                           wres[:, k * 1024 + fc * 128: k * 1024 + (fc + 1) * 128],
                           xnT_b[:, k * NTOKH + i * 128: k * NTOKH + (i + 1) * 128],
                           start=(k == 0), stop=(k == NCH - 1))
            pe_d = L.inc(L.t, "PE")
            L.wait(L.s, "PE", pe_d)
            L.s.copy(vT[:, fc * NTOKH + i * 128: fc * NTOKH + (i + 1) * 128], pb[bi][:, 0:128])
            bank_free[bi] = ("SE", L.inc(L.s, "SE"))
    se_vt = L.cnt["SE"]

    SHL = [0, 32, 16, 4, 1, -1, -4, -16, -32]
    SHL = [0, -32, -16, -4, -1, 1, 4, 16, 32]
    gd_blk = {}
    s2_free = {}
    for tt in range(NTT):
        L.wait(L.v, "SE", se_vt)
        L.wait(L.v, "SE")
        L.wait(L.v, "VE")
        L.v.memset(acc[:], 0.0)
        L.inc(L.v, "VE")
        for b in range(18):
            it = tt * 18 + b
            wb = wstr[it % 2]
            g_guard = gd_blk.get(it - 2)
            if g_guard is not None:
                L.wait(L.g, "PE", g_guard)
            gd = L.dma(L.g, "GD", slab_dst(wb[:], NCH),
                       gwT[:, b * 512:(b + 1) * 512].rearrange("(c p) f -> p c f", p=128))
            L.wait(L.t, "GD", gd)
            n = b // 2
            for fl in range(4):
                fc_abs = (b % 2) * 4 + fl
                bi = fl % 2
                sem, cv = bank_free.get(bi, (None, 0))
                if sem is not None:
                    L.wait(L.t, sem, cv)
                for k in range(NCH):
                    L.t.matmul(pb[bi][:],
                               wb[:, k * 512 + fl * 128: k * 512 + (fl + 1) * 128],
                               xnT_b[:, k * NTOKH + HB + tt * TT: k * NTOKH + HB + (tt + 1) * TT],
                               start=(k == 0), stop=(k == NCH - 1))
                pe_d = L.inc(L.t, "PE")
                gb = gate_sb[fl % 2]
                L.wait(L.s, "PE", pe_d)
                L.wait(L.s, "VE")
                L.s.activation(gb[:], pb[bi][:], AF.Sigmoid)
                bank_free[bi] = ("SE", (se_g := L.inc(L.s, "SE")))
                L.wait(L.v, "SE", se_g)
                off = fc_abs * NTOKH + HB + tt * TT + SHL[n]
                L.v.tensor_mul(tmp_b[:], gb[:], vT[:, off:off + TT])
                L.v.tensor_add(acc[:, fc_abs * 512:(fc_abs + 1) * 512],
                               acc[:, fc_abs * 512:(fc_abs + 1) * 512], tmp_b[:])
                L.inc(L.v, "VE")
            gd_blk[it] = L.cnt["PE"]
        ve_mix = L.cnt["VE"]
        L.wait(L.s, "VE", ve_mix)
        L.s.activation(sqb[:], acc[:], AF.Square)
        se_sq = L.inc(L.s, "SE")
        sem, cv = bank_free.get("u", (None, 0))
        if sem is not None:
            L.wait(L.t, sem, cv)
        L.wait(L.t, "VE", ve_mix)
        for k in range(NCH):
            L.t.matmul(pu[0:1, 0:512], o128f[:], acc[:, k * 512:(k + 1) * 512],
                       start=(k == 0), stop=(k == NCH - 1))
        pe_m = L.inc(L.t, "PE")
        L.wait(L.v, "PE", pe_m)
        L.v.tensor_scalar_mul(mrow[:], pu[0:1, 0:512], 1.0 / D)
        ve_m = L.inc(L.v, "VE")
        L.wait(L.t, "SE", se_sq)
        L.wait(L.t, "VE", ve_m)
        for k in range(NCH):
            L.t.matmul(pu[0:1, 0:512], o128c[:], sqb[:, k * 512:(k + 1) * 512],
                       start=(k == 0), stop=(k == NCH - 1))
        pe_q = L.inc(L.t, "PE")
        L.wait(L.v, "PE", pe_q)
        L.v.tensor_scalar_mul(vrow[:], pu[0:1, 0:512], 1.0 / D)
        L.v.tensor_mul(tmpf[0:1, 0:512], mrow[:], mrow[:])
        L.v.tensor_sub(vrow[:], vrow[:], tmpf[0:1, 0:512])
        ve_v = L.inc(L.v, "VE")
        bank_free["u"] = ("VE", ve_v)
        L.wait(L.s, "VE", ve_v)
        L.s.activation(vrow[:], vrow[:], AF.Ln, bias=fcol[0:1, FC[1e-5]:FC[1e-5] + 1])
        L.s.activation(vrow[:], vrow[:], AF.Exp, scale=fcol[0:1, FC[-0.5]:FC[-0.5] + 1])
        se_r = L.inc(L.s, "SE")
        L.wait(L.t, "SE", se_r)
        L.t.matmul(pb[2][:], o1f[:], mrow[:], start=True, stop=True)
        pe_b1 = L.inc(L.t, "PE")
        L.t.matmul(pb[3][:], o1f[:], vrow[:], start=True, stop=True)
        pe_b2 = L.inc(L.t, "PE")
        L.wait(L.v, "PE", pe_b1)
        L.v.tensor_copy(mbc[:], pb[2][:])
        L.wait(L.v, "PE", pe_b2)
        L.v.tensor_copy(rbc[:], pb[3][:])
        ve_bc = L.inc(L.v, "VE")
        bank_free[2] = ("VE", ve_bc)
        bank_free[3] = ("VE", ve_bc)
        for fc in range(NCH):
            stg = s2_st[fc % 2]
            sf = s2_free.get(fc % 2)
            if sf is not None:
                L.wait(L.v, "ST", sf)
            L.wait(L.v, "LD", ld_c)
            L.v.tensor_sub(tmpf[:], acc[:, fc * 512:(fc + 1) * 512], mbc[:])
            L.v.tensor_mul(tmpf[:], tmpf[:], rbc[:])
            L.v.tensor_scalar(stg[:], tmpf[:], gmc[:, fc:fc + 1], btc[:, fc:fc + 1],
                              op0=ALU.mult, op1=ALU.add)
            ve_s2 = L.inc(L.v, "VE")
            L.wait(L.y, "VE", ve_s2)
            s2_free[fc % 2] = L.dma(L.y, "ST",
                                    o_s2T[fc * 128:(fc + 1) * 128, tt * TT:(tt + 1) * TT], stg[:])
    L.wait(L.y, "ST")
    return L


def prep_A2_weights(inp):
    return {
        "a2_gwT": fmaj(inp["L2_attn_gate_W"]),
        "a2_vwT": fmaj(inp["L2_attn_value_W"]),
        "a2_gamma": np.ascontiguousarray(np.asarray(inp["L2_attn_gamma"], np.float32).reshape(8, 128).T),
        "a2_beta": np.ascontiguousarray(np.asarray(inp["L2_attn_beta"], np.float32).reshape(8, 128).T),
    }


def prep_A2_x2h(x2_full):
    ins = []
    xb = x2_full.reshape(4, 4096, D)
    for c in range(NCORES):
        b = c // 2
        base = (c % 2) * 2048
        idx = (np.arange(base - HB, base + 2048 + HB)) % 4096
        ins.append(np.ascontiguousarray(xb[b][idx]))
    return ins



def build_C2():
    L = Launch("C2")
    y6T = L.din("c2_y6T", [D, NTOK])
    s2T = L.din("c2_s2T", [D, NTOK])
    x2T = L.din("c2_x2T", [D, NTOK])
    o_x3T = L.dout("x3T", [D, NTOK])
    ya = L.sb("ya", [128, 512], F32)
    sa = L.sb("sa", [128, 512], F32)
    xa = L.sb("xa", [128, 512], F32)
    ga = L.sb("ga", [128, 512], F32)
    st = [L.sb(f"st{i}", [128, 512], F32) for i in range(2)]
    st_done = {}
    for i in range(32):
        fc, tt = i % 8, i // 8
        sl = (slice(fc * 128, (fc + 1) * 128), slice(tt * TT, (tt + 1) * TT))
        L.wait(L.y, "SE")
        L.wait(L.y, "VE")
        L.dma(L.y, "LD", ya[:], y6T[sl[0], sl[1]])
        L.dma(L.y, "LD", sa[:], s2T[sl[0], sl[1]])
        ld = L.dma(L.y, "LD", xa[:], x2T[sl[0], sl[1]])
        L.wait(L.s, "LD", ld)
        L.s.activation(ga[:], ya[:], AF.Silu)
        se = L.inc(L.s, "SE")
        buf = st[i % 2]
        sd = st_done.get(i % 2)
        L.wait(L.v, "SE", se)
        if sd is not None:
            L.wait(L.v, "ST", sd)
        L.v.tensor_mul(ga[:], ga[:], sa[:])
        L.v.tensor_add(buf[:], ga[:], xa[:])
        ve = L.inc(L.v, "VE")
        L.wait(L.y, "VE", ve)
        st_done[i % 2] = L.dma(L.y, "ST", o_x3T[sl[0], sl[1]], buf[:])
    L.wait(L.y, "ST")
    return L



def build_C3():
    L = Launch("C3")
    aoT = L.din("c3_aoT", [D, NTOK], BF16)
    woT = L.din("c3_woT", [D, D])
    x3T = L.din("c3_x3T", [D, NTOK])
    g3T = L.din("c3_g3T", [D, NTOK])
    o_x4T = L.dout("x4T", [D, NTOK])
    wres = L.sb("wres", [128, NCH * 1024], BF16)
    aob = L.sb("aob", [128, NCH * NTOK], BF16)
    x3_sl = L.sb("x3_sl", [128, 512], F32)
    g3_sl = L.sb("g3_sl", [128, 512], F32)
    tmp = L.sb("tmp", [128, 512], F32)
    st = [L.sb(f"st{i}", [128, 512], F32) for i in range(2)]
    pb = [L.ps(f"pb{i}", [128, 512], F32) for i in range(2)]
    gd_w = L.dma(L.g, "GD", slab_dst(wres[:], NCH), woT.rearrange("(c p) f -> p c f", p=128))
    gd_a = L.dma(L.g, "GD", slab_dst(aob[:], NCH),
                 aoT.rearrange("(c p) t -> p c t", p=128))
    L.wait(L.t, "GD", gd_a)
    bank_free = {}
    st_done = {}
    for tt in range(NTT):
        for fc in range(NCH):
            bi = fc % 2
            sem, cv = bank_free.get(bi, (None, 0))
            if sem is not None:
                L.wait(L.t, sem, cv)
            for k in range(NCH):
                L.t.matmul(pb[bi][:],
                           wres[:, k * 1024 + fc * 128: k * 1024 + (fc + 1) * 128],
                           aob[:, k * NTOK + tt * TT: k * NTOK + (tt + 1) * TT],
                           start=(k == 0), stop=(k == NCH - 1))
            pe_p = L.inc(L.t, "PE")
            L.wait(L.y, "VE")
            L.dma(L.y, "LD", x3_sl[:], x3T[fc * 128:(fc + 1) * 128, tt * TT:(tt + 1) * TT])
            ld = L.dma(L.y, "LD", g3_sl[:], g3T[fc * 128:(fc + 1) * 128, tt * TT:(tt + 1) * TT])
            buf = st[fc % 2]
            sd = st_done.get(fc % 2)
            L.wait(L.v, "PE", pe_p)
            L.wait(L.v, "LD", ld)
            if sd is not None:
                L.wait(L.v, "ST", sd)
            L.v.tensor_mul(tmp[:], pb[bi][:], g3_sl[:])
            bank_free[bi] = ("VE", L.cnt["VE"] + 1)
            L.v.tensor_add(buf[:], tmp[:], x3_sl[:])
            ve = L.inc(L.v, "VE")
            L.wait(L.y, "VE", ve)
            st_done[fc % 2] = L.dma(L.y, "ST",
                                    o_x4T[fc * 128:(fc + 1) * 128, tt * TT:(tt + 1) * TT], buf[:])
    L.wait(L.y, "ST")
    return L


def prep_A0_weights(inp):
    return {
        "a0_WcT": fmaj(inp["L0_attn_Wc"]),
        "a0_bc_col": np.ascontiguousarray(np.asarray(inp["L0_attn_bc"]).reshape(80, 128).T),
        "a0_bc_row": np.asarray(inp["L0_attn_bc"]).reshape(1, 10240),
        "f0_wrT": fmaj(inp["L0_ffn_w_router"]),
        "f0_mhT": fmaj(inp["L0_ffn_w_mh"]),
        "f0_kf": np.ascontiguousarray(np.asarray(inp["L0_ffn_k_ffwd"]).transpose(0, 3, 1, 2).reshape(8, 128, 4096)),
        "f0_vf": np.ascontiguousarray(np.asarray(inp["L0_ffn_v_ffwd"]).reshape(8, 32, 128, 128)
                                      .transpose(0, 2, 1, 3).reshape(8, 128, 4096)),
        "f0_mgT": fmaj(inp["L0_ffn_w_mg"]),
    }


def host_unpack_xnT(xnT_tiles):
    a = xnT_tiles.reshape(NTILE, 128, NCH, 128)
    return np.ascontiguousarray(a.transpose(2, 1, 0, 3).reshape(D, NTOK))


def host_unpack_v(v_blk):
    a = v_blk.reshape(16, 128, 16, 512)
    return np.ascontiguousarray(a.transpose(2, 1, 0, 3).reshape(NTOK, 8192))





"""L2_ffn inner layer launch (one build; 6 launches with different weight inputs)
plus TBL (rotary tables, run once).

Token layout per core:
  l2_yT [1024, 2816] covers tokens [-768, 2048): yT col of token t = t+768.
  qkv buffer covers [-512, 2048) = 2560: buf col = t+512.  5 tiles of 512.
  attnout/y1 cover [-256, 2048) = 2304: y1 col = t+256.
  windows w = -1..7; queries of win w at buf cols (w+2)*256, y1 cols (w+1)*256.
  keys for win w at buf cols (w+1)*256 .. +512.
"""

NB = 2560
NY1 = 2304
W1COLS = []
_c = 0
for _j in range(22):
    _w = 128 if _j < 21 else 42
    W1COLS.append((_c, _w))
    _c += 2 * _w
assert _c == 5460


def l2_consts():
    o2h = np.zeros((128, 2), np.float32)
    o2h[0:64, 0] = 1.0
    o2h[64:128, 1] = 1.0
    s2h = np.zeros((2, 128), np.float32)
    s2h[0, 0:64] = 1.0
    s2h[1, 64:128] = 1.0
    return {"c_o2h": o2h.astype(BF), "c_s2h": s2h.astype(BF)}


def build_L2F():
    L = Launch("L2F")
    C = declare_consts(L)
    yT = L.din("l2_yT", [D, 2816])
    wqkv = L.din("l2_wqkv", [D, 3072])
    wo = L.din("l2_wo", [D, D])
    w1p = L.din("l2_w1p", [D, 5460])
    w2 = L.din("l2_w2", [2730, D])
    b1p = L.din("l2_b1p", [128, 44])
    b2cI = L.din("l2_b2c", [128, 8])
    lngI = L.din("l2_lngc", [128, 8])
    lnbI = L.din("l2_lnbc", [128, 8])
    qsI = L.din("l2_qsc", [128, 1])
    ksI = L.din("l2_ksc", [128, 1])
    m0I = L.din("l2_m0", [128, 4 * 256], BF16)
    msI = L.din("l2_ms", [128, 4 * 256], BF16)
    csI = L.din("l2_cs", [128, NB], BF16)
    t2I = L.din("l2_t2", [128, NB], BF16)
    ciI = L.din("l2_ci", [128, NB], BF16)
    t2iI = L.din("l2_t2i", [128, NB], BF16)
    kpI = L.din("l2_keep", [128, 1])
    o2hI = L.din("c_o2h", [128, 2], BF16)
    s2hI = L.din("c_s2h", [2, 128], BF16)
    o_oyT = L.dout("oyT", [D, NTOK])
    y1scr = L.nc.dram_tensor("l2_y1scr", [D, NY1], F32, kind="Internal").ap()

    fcol = L.sb("fcol", [128, len(FC_VALS)], F32)
    o1f = L.sb("o1f", [1, 128], F32)
    o128c = L.sb("o128c", [128, 1], BF16)
    eyeb = L.sb("eyeb", [128, 128], BF16)
    o2h = L.sb("o2h", [128, 2], BF16)
    s2h = L.sb("s2h", [2, 128], BF16)
    qscol = L.sb("qscol", [128, 1], F32)
    kscol = L.sb("kscol", [128, 1], F32)
    b1c = L.sb("b1c", [128, 44], F32)
    b2c = L.sb("b2c", [128, 8], F32)
    lng = L.sb("lng", [128, 8], F32)
    lnb = L.sb("lnb", [128, 8], F32)
    kpc = L.sb("kpc", [128, 1], F32)
    m0b = L.sb("m0b", [128, 1024], BF16)
    msb = L.sb("msb", [128, 1024], BF16)
    csb = L.sb("csb", [128, NB], BF16)
    t2b = L.sb("t2b", [128, NB], BF16)
    cib = L.sb("cib", [128, NB], BF16)
    t2ib = L.sb("t2ib", [128, NB], BF16)

    poolA = L.sb("poolA", [128, 22016], BF16)
    ybT = poolA[:, 0:NCH * NB]
    prodb = poolA[:, 0:22 * 512]
    xnb = poolA[:, 11264:11264 + NCH * 512]
    wstr = [poolA[:, 15360 + i * 2048: 15360 + (i + 1) * 2048] for i in range(2)]
    w2s = [poolA[:, 19456 + i * 1024: 19456 + (i + 1) * 1024] for i in range(2)]

    qc = L.sb("qc", [128, NB], BF16)
    kc = L.sb("kc", [128, NB], BF16)
    vc = L.sb("vc", [128, NB], BF16)
    poolB = L.sb("poolB", [128, 3072], BF16)
    vtok = poolB[:, 0:2560]
    aoT = L.sb("aoT", [128, NCH * NY1], BF16)
    y1T = L.sb("y1T", [128, NCH * NY1], BF16)
    wqs = L.sb("wqs", [128, 3 * NCH * 128], BF16)
    wos = L.sb("wos", [128, NCH * 1024], BF16)
    PT = L.sb("PT", [128, 2048], BF16)
    rotA = PT[:, 1024:1536]
    rotB = PT[:, 1536:2048]
    tmpa = L.sb("tmpa", [128, 512], F32)
    tmpb = poolB[:, 2048:3072].bitcast(F32)
    tmpc = L.sb("tmpc", [128, 512], BF16)
    tmpd = L.sb("tmpd", [128, 512], BF16)
    nrm = L.sb("nrm", [2, 512], F32)
    nrmb = L.sb("nrmb", [2, 512], BF16)
    rb2 = L.sb("rb2", [128, 512], F32)
    onesb = L.sb("onesb", [128, 128], BF16)
    bdiag = L.sb("bdiag", [128, 128], BF16)
    rrow = L.sb("rrow", [1, 256], F32)
    ravb = L.sb("ravb", [64, 256], F32)
    mrow = L.sb("mrow", [1, 512], F32)
    vrow = L.sb("vrow", [1, 512], F32)
    mbc = poolB[:, 0:1024].bitcast(F32)
    rbc = poolB[:, 1024:2048].bitcast(F32)
    st_f = [L.sb(f"st_f{i}", [128, 512], F32) for i in range(2)]

    pb = [L.ps(f"pb{i}", [128, 512], F32) for i in range(8)]
    pbf = pb[7][:].bitcast(BF16)

    L.dma(L.g, "GD", o128c[:], C["ones128c"])
    L.dma(L.g, "GD", eyeb[:], C["eyeb"])
    L.dma(L.g, "GD", o2h[:], o2hI)
    L.dma(L.g, "GD", s2h[:], s2hI)
    for (dst, src) in ((fcol, C["fcol"]), (o1f, C["ones1f"]), (qscol, qsI), (kscol, ksI),
                       (b1c, b1p), (b2c, b2cI), (lng, lngI), (lnb, lnbI), (kpc, kpI),
                       (m0b, m0I), (msb, msI)):
        L.dma(L.y, "LD", dst[:], src)
    for (dst, src) in ((csb, csI), (t2b, t2I), (cib, ciI), (t2ib, t2iI)):
        L.dma(L.y, "LD", dst[:], src)
    for ch in range(NCH):
        off = 768 if ch < 4 else 767
        L.dma(L.g, "GD", ybT[:, ch * NB:(ch + 1) * NB],
              yT[ch * 128:(ch + 1) * 128, off - 512: off - 512 + NB])
    gd_y = L.cnt["GD"]
    gd_wo = L.dma(L.g, "GD", slab_dst(wos[:], NCH), wo.rearrange("(c p) f -> p c f", p=128))
    ld_c = L.cnt["LD"]
    L.wait(L.v, "LD", ld_c)
    L.wait(L.s, "LD", ld_c)
    L.v.memset(onesb[:], 1.0)
    L.v.memset(bdiag[:], 0.0)
    L.v.memset(bdiag[0:64, 0:64], 1.0)
    L.v.memset(bdiag[64:128, 64:128], 1.0)
    ve_ones = L.inc(L.v, "VE")

    bank_free = {}
    wqs_free = 0
    for hc in range(8):
        L.wait(L.g, "PE", wqs_free)
        for g in range(3):
            L.dma(L.g, "GD", slab_dst(wqs[:, g * 1024:(g + 1) * 1024], NCH),
                  wqkv[:, g * 1024 + hc * 128: g * 1024 + (hc + 1) * 128]
                  .rearrange("(c p) f -> p c f", p=128))
        gd_w = L.cnt["GD"]
        L.wait(L.t, "GD", max(gd_w, gd_y))
        if hc == 0:
            L.wait(L.t, "LD", ld_c)
        for t5 in range(5):
            tc = t5 * 512
            for g in range(3):
                bi = g % 2
                sem, cv = bank_free.get(bi, (None, 0))
                if sem is not None:
                    L.wait(L.t, sem, cv)
                for k in range(NCH):
                    L.t.matmul(pb[bi][:],
                               wqs[:, g * 1024 + k * 128:g * 1024 + (k + 1) * 128],
                               ybT[:, k * NB + tc: k * NB + tc + 512],
                               start=(k == 0), stop=(k == NCH - 1))
                pe_d = L.inc(L.t, "PE")
                if g == 2:
                    L.wait(L.s, "PE", pe_d)
                    L.wait(L.s, "VE")
                    L.s.copy(vc[:, tc:tc + 512], pb[bi][:])
                    bank_free[bi] = ("SE", L.inc(L.s, "SE"))
                else:
                    dstc = qc if g == 0 else kc
                    scol = qscol if g == 0 else kscol
                    cst = csb if g == 0 else cib
                    t2t = t2b if g == 0 else t2ib
                    L.wait(L.s, "PE", pe_d)
                    L.wait(L.s, "VE")
                    L.s.copy(dstc[:, tc:tc + 512], pb[bi][:])
                    L.s.activation(tmpc[:], pb[bi][:], AF.Square)
                    se_sq = L.inc(L.s, "SE")
                    bank_free[bi] = ("SE", se_sq)
                    sem, cv = bank_free.get(5, (None, 0))
                    if sem is not None:
                        L.wait(L.t, sem, cv)
                    L.wait(L.t, "SE", se_sq)
                    if hc == 0 and t5 == 0:
                        L.wait(L.t, "VE", ve_ones)
                    L.t.matmul(pb[5][:], bdiag[:], tmpc[:], start=True, stop=True)
                    pe_n = L.inc(L.t, "PE")
                    L.wait(L.v, "PE", pe_n)
                    L.v.tensor_scalar_max(rb2[:], pb[5][:],
                                          fcol[:, FC[1e-20]:FC[1e-20] + 1])
                    ve_n0 = L.inc(L.v, "VE")
                    bank_free[5] = ("VE", ve_n0)
                    L.wait(L.s, "VE", ve_n0)
                    L.s.activation(rb2[:], rb2[:], AF.Ln)
                    L.s.activation(rb2[:], rb2[:], AF.Exp, scale=-0.5)
                    se_n = L.inc(L.s, "SE")
                    L.wait(L.v, "SE", se_n)
                    L.v.scalar_tensor_tensor(tmpd[:], dstc[:, tc:tc + 512], scol[:], rb2[:],
                                             op0=ALU.mult, op1=ALU.mult)
                    L.v.tensor_mul(rotA[:], tmpd[:], cst[:, tc:tc + 512])
                    for r in range(4):
                        ro = r * 32
                        so = (r ^ 1) * 32
                        L.v.tensor_mul(rotB[so:so + 32, :], tmpd[ro:ro + 32, :],
                                       t2t[ro:ro + 32, tc:tc + 512])
                    L.v.tensor_add(dstc[:, tc:tc + 512], rotA[:], rotB[:])
                    L.inc(L.v, "VE")
            se_now = L.cnt["SE"]
            L.wait(L.t, "SE", se_now)
            for j4 in range(4):
                jj = t5 * 4 + j4
                sem, cv = bank_free.get("tp", (None, 0))
                if sem is not None:
                    L.wait(L.t, sem, cv)
                L.t.transpose(pbf[:, 0:128], vc[:, jj * 128:(jj + 1) * 128], eyeb[:])
                pe_t = L.inc(L.t, "PE")
                L.wait(L.v, "PE", pe_t)
                L.v.tensor_copy(vtok[:, jj * 128:(jj + 1) * 128], pbf[:, 0:128])
                bank_free["tp"] = ("VE", L.inc(L.v, "VE"))
        ve_qkv = L.cnt["VE"]
        L.wait(L.t, "VE", ve_qkv)
        for head in range(2):
            r0 = head * 64
            for w in range(-1, 8):
                qcol = (w + 2) * 256
                kcol = (w + 1) * 256
                mt = m0b if w == 0 else msb
                for kcn in range(4):
                    sem, cv = bank_free.get(kcn, (None, 0))
                    if sem is not None:
                        L.wait(L.t, sem, cv)
                    L.t.matmul(pb[kcn][:, 0:256],
                               kc[r0:r0 + 64, kcol + kcn * 128: kcol + (kcn + 1) * 128],
                               qc[r0:r0 + 64, qcol:qcol + 256], start=True, stop=True)
                    pe_s = L.inc(L.t, "PE")
                    L.wait(L.v, "PE", pe_s)
                    L.wait(L.v, "SE")
                    L.v.scalar_tensor_tensor(tmpa[:, 0:256], pb[kcn][:, 0:256], 8.0,
                                             mt[:, kcn * 256:(kcn + 1) * 256],
                                             op0=ALU.mult, op1=ALU.add)
                    ve_m = L.inc(L.v, "VE")
                    bank_free[kcn] = ("VE", ve_m)
                    L.wait(L.s, "VE", ve_m)
                    L.s.activation(PT[:, kcn * 256:(kcn + 1) * 256], tmpa[:, 0:256], AF.Exp)
                    L.inc(L.s, "SE")
                se_pt = L.cnt["SE"]
                sem, cv = bank_free.get(5, (None, 0))
                if sem is not None:
                    L.wait(L.t, sem, cv)
                L.wait(L.t, "SE", se_pt)
                if hc == 0 and head == 0 and w == -1:
                    L.wait(L.t, "VE", ve_ones)
                for kcn in range(4):
                    L.t.matmul(pb[5][0:64, 0:256], onesb[:, 0:64],
                               PT[:, kcn * 256:(kcn + 1) * 256],
                               start=(kcn == 0), stop=(kcn == 3))
                pe_sum = L.inc(L.t, "PE")
                L.wait(L.s, "PE", pe_sum)
                L.wait(L.s, "VE")
                L.s.activation(ravb[:], pb[5][0:64, 0:256], AF.Ln)
                L.s.activation(ravb[:], ravb[:], AF.Exp, scale=-1.0)
                se_rav = L.inc(L.s, "SE")
                bank_free[5] = ("SE", se_rav)
                sem, cv = bank_free.get(4, (None, 0))
                if sem is not None:
                    L.wait(L.t, sem, cv)
                for kcn in range(4):
                    jj = kcol // 128 + kcn
                    L.t.matmul(pb[4][0:64, 0:256],
                               vtok[:, jj * 128 + r0: jj * 128 + r0 + 64],
                               PT[:, kcn * 256:(kcn + 1) * 256],
                               start=(kcn == 0), stop=(kcn == 3))
                pe_av = L.inc(L.t, "PE")
                L.wait(L.v, "PE", pe_av)
                L.wait(L.v, "SE", se_rav)
                L.v.tensor_mul(aoT[r0:r0 + 64, hc * NY1 + (w + 1) * 256: hc * NY1 + (w + 2) * 256],
                               pb[4][0:64, 0:256], ravb[:])
                bank_free[4] = ("VE", L.inc(L.v, "VE"))
        wqs_free = L.cnt["PE"]
    ve_attn = L.cnt["VE"]

    T5 = [(0, 512), (512, 512), (1024, 512), (1536, 512), (2048, 256)]
    L.wait(L.t, "GD", gd_wo)
    L.wait(L.t, "VE", ve_attn)
    y1st = {}
    for (tc, tw) in T5:
        for fc in range(NCH):
            bi = fc % 2
            sem, cv = bank_free.get(bi, (None, 0))
            if sem is not None:
                L.wait(L.t, sem, cv)
            for k in range(NCH):
                L.t.matmul(pb[bi][:, 0:tw],
                           wos[:, k * 1024 + fc * 128: k * 1024 + (fc + 1) * 128],
                           aoT[:, k * NY1 + tc: k * NY1 + tc + tw],
                           start=(k == 0), stop=(k == NCH - 1))
            pe_p = L.inc(L.t, "PE")
            L.wait(L.y, "VE")
            ld_y = L.dma(L.y, "LD", tmpa[:, 0:tw],
                         yT[fc * 128:(fc + 1) * 128, 512 + tc: 512 + tc + tw])
            L.wait(L.v, "PE", pe_p)
            L.wait(L.v, "LD", ld_y)
            yp = y1st.get(fc % 2)
            if yp is not None:
                L.wait(L.v, "ST", yp)
            L.v.tensor_add(st_f[fc % 2][:, 0:tw], pb[bi][:, 0:tw], tmpa[:, 0:tw])
            bank_free[bi] = ("VE", L.cnt["VE"] + 1)
            L.v.tensor_copy(y1T[:, fc * NY1 + tc: fc * NY1 + tc + tw], st_f[fc % 2][:, 0:tw])
            ve_y1 = L.inc(L.v, "VE")
            L.wait(L.y, "VE", ve_y1)
            y1st[fc % 2] = L.dma(L.y, "ST", y1scr[fc * 128:(fc + 1) * 128, tc:tc + tw],
                                 st_f[fc % 2][:, 0:tw])
    for ch in range(4, 8):
        L.v.tensor_scalar(y1T[:, ch * NY1 + 255: ch * NY1 + 256],
                          y1T[:, ch * NY1 + 255: ch * NY1 + 256], kpc[:], None, op0=ALU.mult)
    ve_keep = L.inc(L.v, "VE")

    L.wait(L.t, "VE", ve_keep)
    L.wait(L.y, "ST")
    oy_free = {}
    wstr_guard = {0: 0, 1: 0}
    w2s_guard = {0: 0, 1: 0}
    for tt in range(NTT):
        yc = 256 + tt * 512

        def shv(ch, width=512):
            off = yc if ch < 4 else yc - 1
            return y1T[:, ch * NY1 + off: ch * NY1 + off + width]

        sem, cv = bank_free.get(5, (None, 0))
        if sem is not None:
            L.wait(L.t, sem, cv)
        for k in range(NCH):
            L.t.matmul(pb[5][:], onesb[:], shv(k), start=(k == 0), stop=(k == NCH - 1))
        pe_m = L.inc(L.t, "PE")
        L.wait(L.v, "PE", pe_m)
        L.v.tensor_scalar_mul(mbc, pb[5][:], 1.0 / D)
        bank_free[5] = ("VE", L.inc(L.v, "VE"))
        pe_prev = None
        for k in range(NCH):
            L.wait(L.s, "VE")
            if pe_prev is not None:
                L.wait(L.s, "PE", pe_prev)
            L.s.activation(tmpc[:], shv(k), AF.Square)
            se_s = L.inc(L.s, "SE")
            if k == 0:
                sem, cv = bank_free.get(6, (None, 0))
                if sem is not None:
                    L.wait(L.t, sem, cv)
            L.wait(L.t, "SE", se_s)
            L.t.matmul(pb[6][:], onesb[:], tmpc[:], start=(k == 0), stop=(k == NCH - 1))
            pe_prev = L.inc(L.t, "PE")
        pe_q = pe_prev
        L.wait(L.v, "PE", pe_q)
        L.v.tensor_scalar_mul(tmpa[:], pb[6][:], 1.0 / D)
        L.v.tensor_mul(rbc, mbc, mbc)
        L.v.tensor_sub(tmpa[:], tmpa[:], rbc)
        ve_v = L.inc(L.v, "VE")
        bank_free[6] = ("VE", ve_v)
        L.wait(L.s, "VE", ve_v)
        L.s.activation(tmpa[:], tmpa[:], AF.Ln, bias=fcol[:, FC[1e-5]:FC[1e-5] + 1])
        L.s.activation(rbc, tmpa[:], AF.Exp, scale=fcol[:, FC[-0.5]:FC[-0.5] + 1])
        se_bc = L.inc(L.s, "SE")
        L.wait(L.v, "SE", se_bc)
        for ch in range(NCH):
            L.v.tensor_sub(tmpa[:], shv(ch), mbc)
            L.v.tensor_mul(tmpa[:], tmpa[:], rbc)
            L.v.tensor_scalar(xnb[:, ch * 512:(ch + 1) * 512], tmpa[:],
                              lng[:, ch:ch + 1], lnb[:, ch:ch + 1], op0=ALU.mult, op1=ALU.add)
        ve_xn = L.inc(L.v, "VE")
        for j in range(22):
            c0, wdt = W1COLS[j]
            wb = wstr[j % 2]
            if tt == 0 and j < 2:
                L.wait(L.g, "VE", ve_keep)
                L.wait(L.g, "PE")
            else:
                L.wait(L.g, "PE", wstr_guard[j % 2])
            gd = L.dma(L.g, "GD", slab_dst(wb[0:128, 0:NCH * 2 * wdt], NCH),
                       w1p[:, c0:c0 + 2 * wdt].rearrange("(c p) f -> p c f", p=128))
            L.wait(L.t, "GD", gd)
            if j == 0:
                L.wait(L.t, "VE", ve_xn)
            for half in range(2):
                bi = half
                sem, cv = bank_free.get(bi, (None, 0))
                if sem is not None:
                    L.wait(L.t, sem, cv)
                for k in range(NCH):
                    L.t.matmul(pb[bi][0:wdt, 0:512],
                               wb[:, k * 2 * wdt + half * wdt: k * 2 * wdt + (half + 1) * wdt],
                               xnb[:, k * 512:(k + 1) * 512],
                               start=(k == 0), stop=(k == NCH - 1))
                pe_d = L.inc(L.t, "PE")
                L.wait(L.s, "PE", pe_d)
                L.wait(L.s, "VE")
                if half == 0:
                    L.wait(L.v, "PE", pe_d)
                    L.wait(L.v, "SE")
                    L.v.tensor_scalar(tmpc[0:wdt, :], pb[bi][0:wdt, :],
                                      b1c[0:wdt, 2 * j:2 * j + 1], None, op0=ALU.add)
                    bank_free[bi] = ("VE", L.inc(L.v, "VE"))
                else:
                    L.s.activation(tmpd[0:wdt, :], pb[bi][0:wdt, :], AF.Gelu,
                                   bias=b1c[0:wdt, 2 * j + 1:2 * j + 2])
                    bank_free[bi] = ("SE", L.inc(L.s, "SE"))
            se_ag = L.cnt["SE"]
            L.wait(L.v, "SE", se_ag)
            L.v.tensor_mul(prodb[0:wdt, j * 512:(j + 1) * 512], tmpc[0:wdt, :], tmpd[0:wdt, :])
            L.inc(L.v, "VE")
            wstr_guard[j % 2] = L.cnt["PE"]
        ve_prod = L.cnt["VE"]
        for bi in range(8):
            sem, cv = bank_free.get(bi, (None, 0))
            if sem is not None:
                L.wait(L.t, sem, cv)
        L.wait(L.t, "VE", ve_prod)
        for j in range(22):
            c0, wdt = W1COLS[j]
            wb2 = w2s[j % 2]
            L.wait(L.g, "PE", w2s_guard[j % 2] if w2s_guard[j % 2] else L.cnt["PE"] if j < 2 and tt == 0 else w2s_guard[j % 2])
            gd = L.dma(L.g, "GD", wb2[0:wdt, :], w2[j * 128: j * 128 + wdt, :])
            L.wait(L.t, "GD", gd)
            for oc in range(8):
                L.t.matmul(pb[oc][:],
                           wb2[0:wdt, oc * 128:(oc + 1) * 128],
                           prodb[0:wdt, j * 512:(j + 1) * 512],
                           start=(j == 0), stop=(j == 21))
            L.inc(L.t, "PE")
            w2s_guard[j % 2] = L.cnt["PE"]
        pe_w2 = L.cnt["PE"]
        for oc in range(8):
            L.wait(L.y, "VE")
            ld_y1 = L.dma(L.y, "LD", tmpb, y1scr[oc * 128:(oc + 1) * 128, yc:yc + 512])
            of = oy_free.get(oc % 2)
            L.wait(L.v, "PE", pe_w2)
            L.v.tensor_scalar(tmpa[:], pb[oc][:], b2c[:, oc:oc + 1], None, op0=ALU.add)
            L.wait(L.v, "LD", ld_y1)
            if of is not None:
                L.wait(L.v, "ST", of)
            L.v.tensor_add(st_f[oc % 2][:], tmpa[:], tmpb)
            bank_free[oc] = ("VE", (ve_o := L.inc(L.v, "VE")))
            L.wait(L.y, "VE", ve_o)
            oy_free[oc % 2] = L.dma(L.y, "ST",
                                    o_oyT[oc * 128:(oc + 1) * 128, tt * 512:(tt + 1) * 512],
                                    st_f[oc % 2][:])
    L.wait(L.y, "ST")
    return L



def build_TBL():
    L = Launch("TBL")
    C = declare_consts(L)
    pos = L.din("tb_pos", [1, NB])
    invf = L.din("tb_invf", [1, 128])
    lnb_ = L.din("tb_lnbase", [1, 128])
    sgn = L.din("tb_sign", [128, 1])
    o_cs = L.dout("cs", [128, NB], BF16)
    o_t2 = L.dout("t2", [128, NB], BF16)
    o_ci = L.dout("ci", [128, NB], BF16)
    o_t2i = L.dout("t2i", [128, NB], BF16)

    posr = L.sb("posr", [1, NB], F32)
    invr = L.sb("invr", [1, 128], F32)
    lbr = L.sb("lbr", [1, 128], F32)
    sgc = L.sb("sgc", [128, 1], F32)
    prow = L.sb("prow", [1, 512], F32)
    th = L.sb("th", [128, 512], F32)
    ml = L.sb("ml", [128, 512], F32)
    aa = L.sb("aa", [128, 512], F32)
    sinb = L.sb("sinb", [128, 512], F32)
    cosb = L.sb("cosb", [128, 512], F32)
    scl = L.sb("scl", [128, 512], F32)
    iscl = L.sb("iscl", [128, 512], F32)
    t1 = L.sb("t1", [128, 512], F32)
    ui = L.sb("ui", [128, 512], mybir.dt.int32)
    uf = L.sb("uf", [128, 512], F32)
    ob = [L.sb(f"ob{i}", [128, 512], BF16) for i in range(4)]
    pf = L.ps("pf", [128, 512], F32)
    pg = L.ps("pg", [128, 512], F32)

    PI = float(np.pi)
    L.dma(L.y, "LD", posr[:], pos)
    L.dma(L.y, "LD", invr[:], invf)
    L.dma(L.y, "LD", lbr[:], lnb_)
    ldc = L.dma(L.y, "LD", sgc[:], sgn)
    st_done = {}
    pf_free = 0
    pg_free_se = 0
    for t5 in range(5):
        tc = t5 * 512
        L.wait(L.t, "LD", ldc)
        L.wait(L.t, "VE", pf_free)
        L.t.matmul(pf[:], invr[:], posr[0:1, tc:tc + 512], start=True, stop=True)
        pe_f = L.inc(L.t, "PE")
        L.wait(L.v, "PE", pe_f)
        L.v.tensor_scalar(prow[:], posr[0:1, tc:tc + 512], -2048.0, 1.0 / 128.0,
                          op0=ALU.add, op1=ALU.mult)
        ve_pr = L.inc(L.v, "VE")
        L.wait(L.t, "VE", ve_pr)
        L.wait(L.t, "SE", pg_free_se)
        L.t.matmul(pg[:], lbr[:], prow[:], start=True, stop=True)
        pe_g = L.inc(L.t, "PE")
        L.wait(L.s, "PE", pe_g)
        L.wait(L.s, "VE")
        L.s.activation(scl[:], pg[:], AF.Exp)
        L.s.activation(iscl[:], pg[:], AF.Exp, scale=-1.0)
        pg_free_se = L.inc(L.s, "SE")
        def wrap_and_sin(dst, phase_add):
            L.v.tensor_scalar(t1[:], pf[:], phase_add, 1.0 / (2 * PI),
                              op0=ALU.add, op1=ALU.mult)
            L.v.tensor_copy(ui[:], t1[:])
            L.v.tensor_copy(uf[:], ui[:])
            L.v.tensor_sub(t1[:], t1[:], uf[:])
            L.v.tensor_scalar_mul(th[:], t1[:], 2 * PI)
            L.v.tensor_scalar(ml[:], th[:], -PI, None, op0=ALU.is_lt)
            L.v.scalar_tensor_tensor(th[:], ml[:], 2 * PI, th[:], op0=ALU.mult, op1=ALU.add)
            L.v.tensor_scalar(ml[:], th[:], PI, None, op0=ALU.is_lt)
            L.v.tensor_scalar(ml[:], ml[:], -1.0, 1.0, op0=ALU.mult, op1=ALU.add)
            L.v.scalar_tensor_tensor(th[:], ml[:], -2 * PI, th[:], op0=ALU.mult, op1=ALU.add)
            ve_t = L.inc(L.v, "VE")
            L.wait(L.s, "VE", ve_t)
            L.s.activation(dst[:], th[:], AF.Sin)
            return L.inc(L.s, "SE")
        L.wait(L.v, "PE", pe_f)
        se_sin = wrap_and_sin(sinb, 0.0)
        se_cos = wrap_and_sin(cosb, PI / 2)
        pf_free = L.cnt["VE"]
        L.wait(L.v, "ST")
        L.wait(L.v, "SE", se_cos)
        L.v.tensor_mul(ob[0][:], cosb[:], scl[:])
        L.v.tensor_mul(t1[:], sinb[:], scl[:])
        L.v.tensor_scalar(ob[1][:], t1[:], sgc[:], None, op0=ALU.mult)
        L.v.tensor_mul(ob[2][:], cosb[:], iscl[:])
        L.v.tensor_mul(t1[:], sinb[:], iscl[:])
        L.v.tensor_scalar(ob[3][:], t1[:], sgc[:], None, op0=ALU.mult)
        ve_o = L.inc(L.v, "VE")
        L.wait(L.y, "VE", ve_o)
        for (i, dst) in ((0, o_cs), (1, o_t2), (2, o_ci), (3, o_t2i)):
            L.dma(L.y, "ST", dst[:, tc:tc + 512], ob[i][:])
    L.wait(L.y, "ST")
    return L



def prep_L2F_weights(inp, i):
    W1 = np.asarray(inp["L2_ffn_W1"][i], np.float32)
    b1 = np.asarray(inp["L2_ffn_b1"][i], np.float32)
    w1p = np.empty((D, 5460), np.float32)
    b1p = np.zeros((128, 44), np.float32)
    for j in range(22):
        c0, wdt = W1COLS[j]
        a0 = j * 128
        w1p[:, c0:c0 + wdt] = W1[:, a0:a0 + wdt]
        w1p[:, c0 + wdt:c0 + 2 * wdt] = W1[:, 2730 + a0:2730 + a0 + wdt]
        b1p[0:wdt, 2 * j] = b1[a0:a0 + wdt]
        b1p[0:wdt, 2 * j + 1] = b1[2730 + a0:2730 + a0 + wdt]
    qs = np.asarray(inp["L2_ffn_q_scale"][i], np.float32)
    ks = np.asarray(inp["L2_ffn_k_scale"][i], np.float32)
    return {
        "l2_wqkv": np.ascontiguousarray(np.asarray(inp["L2_ffn_Wqkv"][i], np.float32)),
        "l2_wo": np.ascontiguousarray(np.asarray(inp["L2_ffn_Wo"][i], np.float32)),
        "l2_w1p": np.ascontiguousarray(w1p),
        "l2_w2": np.ascontiguousarray(np.asarray(inp["L2_ffn_W2"][i], np.float32)),
        "l2_b1p": b1p,
        "l2_b2c": np.ascontiguousarray(np.asarray(inp["L2_ffn_b2"][i], np.float32).reshape(8, 128).T),
        "l2_lngc": np.ascontiguousarray(np.asarray(inp["L2_ffn_ln_g"][i], np.float32).reshape(8, 128).T),
        "l2_lnbc": np.ascontiguousarray(np.asarray(inp["L2_ffn_ln_b"][i], np.float32).reshape(8, 128).T),
        "l2_qsc": np.tile(qs, 2).reshape(128, 1).astype(np.float32),
        "l2_ksc": np.tile(ks, 2).reshape(128, 1).astype(np.float32),
    }


def l2_masks():
    def build(first_window):
        i = np.arange(256)[:, None]
        jj = np.arange(512)[None, :]
        if first_window:
            jb = jj - 256
            pad = jj < 256
            keep = (~pad) & (i >= jb) & (i <= jb + 256)
        else:
            keep = (jj >= i) & (jj <= i + 256)
        m = np.where(keep, 0.0, -1e30).astype(np.float32)
        mT = m.T
        return np.ascontiguousarray(mT.reshape(4, 128, 256).transpose(1, 0, 2).reshape(128, 1024)).astype(BF)
    return build(False), build(True)


def l2_tbl_inputs(core):
    half = np.arange(0, 64, 2, dtype=np.float64)
    inv_freq = 1.0 / 10000.0 ** (half / 64.0)
    base_scale = (half + 0.4 * 64) / (1.4 * 64)
    dh = np.arange(128) % 64
    invf = inv_freq[np.minimum(dh, 63) % 32]
    lnb_ = np.log(base_scale[dh % 32])
    sgn = np.where(dh < 32, 1.0, -1.0)
    t0 = (core % 2) * 2048 - 512
    pos = np.arange(t0, t0 + NB, dtype=np.float64)
    return {
        "tb_pos": pos.reshape(1, NB).astype(np.float32),
        "tb_invf": invf.reshape(1, 128).astype(np.float32),
        "tb_lnbase": lnb_.reshape(1, 128).astype(np.float32),
        "tb_sign": sgn.reshape(128, 1).astype(np.float32),
    }


def prep_L2F_y(y_full, first):
    ins = []
    yb = y_full.reshape(4, 4096, D)
    for c in range(NCORES):
        b = c // 2
        base = (c % 2) * 2048
        buf = np.zeros((2816, D), np.float32)
        lo = base - 768
        src_lo = max(lo, 0)
        src_hi = min(base + 2048, 4096)
        buf[src_lo - lo: src_hi - lo] = yb[b][src_lo:src_hi]
        ins.append(np.ascontiguousarray(buf.T))
    return ins


def l2_keep(core):
    v = 0.0 if core % 2 == 0 else 1.0
    return {"l2_keep": np.full((128, 1), v, np.float32)}


"""A3: LN3 + L3_attn projections (q,k with rope; v) + L3_ffn (dense experts).
B3: L3 causal attention (head-shard).
TBL3: rope tables for L3 (cos128/sin-sign128 feature-major, per-core tokens)."""


def a3_consts():
    sel8 = np.zeros((8, 8 * 128), np.float32)
    for e in range(8):
        sel8[e, e * 128:(e + 1) * 128] = 1.0
    return {"c_sel8": sel8.astype(BF)}


def build_TBL3():
    L = Launch("TBL3")
    pos = L.din("t3_pos", [1, NTOK])
    invf = L.din("t3_invf", [1, 128])
    sgn = L.din("t3_sign", [128, 1])
    o_cos = L.dout("l3cos", [128, NTOK], BF16)
    o_sin = L.dout("l3sin", [128, NTOK], BF16)
    posr = L.sb("posr", [1, NTOK], F32)
    invr = L.sb("invr", [1, 128], F32)
    sgc = L.sb("sgc", [128, 1], F32)
    th = L.sb("th", [128, 512], F32)
    ml = L.sb("ml", [128, 512], F32)
    t1 = L.sb("t1", [128, 512], F32)
    ui = L.sb("ui", [128, 512], mybir.dt.int32)
    uf = L.sb("uf", [128, 512], F32)
    sinb = L.sb("sinb", [128, 512], F32)
    ob = [L.sb(f"ob{i}", [128, 512], BF16) for i in range(2)]
    pf = L.ps("pf", [128, 512], F32)
    PI = float(np.pi)
    L.dma(L.y, "LD", posr[:], pos)
    L.dma(L.y, "LD", invr[:], invf)
    ldc = L.dma(L.y, "LD", sgc[:], sgn)
    pf_free = 0
    for t4 in range(NTT):
        tc = t4 * 512
        L.wait(L.t, "LD", ldc)
        L.wait(L.t, "VE", pf_free)
        L.t.matmul(pf[:], invr[:], posr[0:1, tc:tc + 512], start=True, stop=True)
        pe_f = L.inc(L.t, "PE")
        L.wait(L.v, "PE", pe_f)
        L.wait(L.v, "ST")

        def wrap_sin(dst, phase):
            L.v.tensor_scalar(t1[:], pf[:], phase, 1.0 / (2 * PI), op0=ALU.add, op1=ALU.mult)
            L.v.tensor_copy(ui[:], t1[:])
            L.v.tensor_copy(uf[:], ui[:])
            L.v.tensor_sub(t1[:], t1[:], uf[:])
            L.v.tensor_scalar_mul(th[:], t1[:], 2 * PI)
            L.v.tensor_scalar(ml[:], th[:], -PI, None, op0=ALU.is_lt)
            L.v.scalar_tensor_tensor(th[:], ml[:], 2 * PI, th[:], op0=ALU.mult, op1=ALU.add)
            L.v.tensor_scalar(ml[:], th[:], PI, None, op0=ALU.is_lt)
            L.v.tensor_scalar(ml[:], ml[:], -1.0, 1.0, op0=ALU.mult, op1=ALU.add)
            L.v.scalar_tensor_tensor(th[:], ml[:], -2 * PI, th[:], op0=ALU.mult, op1=ALU.add)
            ve_t = L.inc(L.v, "VE")
            L.wait(L.s, "VE", ve_t)
            L.s.activation(dst, th[:], AF.Sin)
            return L.inc(L.s, "SE")

        se_c = wrap_sin(ob[0][:], PI / 2)
        L.wait(L.v, "SE", se_c)
        se_s = wrap_sin(sinb[:], 0.0)
        pf_free = L.cnt["VE"]
        L.wait(L.v, "SE", se_s)
        L.v.tensor_scalar(ob[1][:], sinb[:], sgc[:], None, op0=ALU.mult)
        ve_o = L.inc(L.v, "VE")
        L.wait(L.y, "VE", ve_o)
        L.dma(L.y, "ST", o_cos[:, tc:tc + 512], ob[0][:])
        L.dma(L.y, "ST", o_sin[:, tc:tc + 512], ob[1][:])
    L.wait(L.y, "ST")
    return L


def tbl3_inputs(core):
    inv_freq = 10000.0 ** (np.arange(32, dtype=np.float64) * (-1.0 / 32.0))
    p = np.arange(128)
    i32 = p % 32
    invf = inv_freq[i32]
    sgn = np.where((p % 64) < 32, 1.0, -1.0)
    t0 = (core % 2) * 2048
    pos = np.arange(t0, t0 + NTOK, dtype=np.float64)
    return {"t3_pos": pos.reshape(1, NTOK).astype(np.float32),
            "t3_invf": invf.reshape(1, 128).astype(np.float32),
            "t3_sign": sgn.reshape(128, 1).astype(np.float32)}


def l3_perm():
    perm = []
    for h in range(16):
        perm += [h * 64 + 2 * i for i in range(32)]
        perm += [h * 64 + 2 * i + 1 for i in range(32)]
    return np.array(perm)


def build_A3():
    L = Launch("A3")
    C = declare_consts(L)
    x3 = L.din("x3", [NTOK, D])
    wqT = L.din("a3_wqT", [D, D])
    wkT = L.din("a3_wkT", [D, D])
    wvT = L.din("a3_wvT", [D, D])
    l3cos = L.din("l3cos", [128, NTOK], BF16)
    l3sin = L.din("l3sin", [128, NTOK], BF16)
    WrI = L.din("a3_Wr", [D, 8])
    brI = L.din("a3_br", [1, 8])
    W1I = L.din("a3_W1", [8, D, 4096])
    W2I = L.din("a3_W2", [8, 4096, D])
    b1I = L.din("a3_b1c", [128, 8 * 32])
    b2I = L.din("a3_b2c", [128, 8 * 8])
    sel8I = L.din("c_sel8", [8, 8 * 128], BF16)

    o_xnT_tiles = L.dout("xn3T_tiles", [NTILE, 128, D])
    o_qT = L.dout("q3T", [D, NTOK], BF16)
    o_kT = L.dout("k3T", [D, NTOK], BF16)
    o_vT = L.dout("v3T", [D, NTOK], BF16)
    o_g3T = L.dout("g3T", [D, NTOK])

    eyef = L.sb("eyef", [128, 128], F32)
    fcol = L.sb("fcol", [128, len(FC_VALS)], F32)
    o1f = L.sb("o1f", [1, 128], F32)
    sel8 = L.sb("sel8", [8, 8 * 128], BF16)
    cosb = L.sb("cosb", [128, NTOK], BF16)
    sinb = L.sb("sinb", [128, NTOK], BF16)
    xnT_b = L.sb("xnT_b", [128, NCH * NTOK], BF16)
    wres = L.sb("wres", [128, NCH * 1024], BF16)
    wrf = L.sb("wrf", [128, NCH * 8], F32)
    brr = L.sb("brr", [1, 8], F32)
    rbuf = L.sb("rbuf", [128, D], F32)
    b1c = L.sb("b1c", [128, 8 * 32], F32)
    b2c = L.sb("b2c", [128, 8 * 8], F32)
    b2s = L.sb("b2s", [128, 8], F32)
    mk8 = L.sb("mk8", [128, 8], F32)
    mstat = L.sb("mstat", [128, 1], F32)
    mT8 = L.sb("mT8", [8, NTOK], BF16)
    mbce = L.sb("mbce", [128, NTT * 512], BF16)
    rotA = L.sb("rotA", [128, 512], BF16)
    rotB = L.sb("rotB", [128, 512], BF16)
    stg_b = [L.sb(f"stg_b{i}", [128, 512], BF16) for i in range(2)]
    stg_f = [L.sb(f"stg_f{i}", [128, 512], F32) for i in range(2)]
    y_acc = L.sb("y_acc", [128, NTT * NCH * 512], F32)
    hb = L.sb("hb", [128, 4 * 512], BF16)
    tmpf = L.sb("tmpf", [128, 512], F32)
    w1b = [L.sb(f"w1b{i}", [128, NCH * 512], BF16) for i in range(2)]
    w2b = [L.sb(f"w2b{i}", [128, 4 * 1024], BF16) for i in range(2)]

    pb = [L.ps(f"pb{i}", [128, 512], F32) for i in range(6)]
    pu = L.ps("pu", [128, 512], F32)

    L.dma(L.y, "LD", eyef[:], C["eyef"])
    L.dma(L.y, "LD", fcol[:], C["fcol"])
    L.dma(L.y, "LD", o1f[:], C["ones1f"])
    L.dma(L.y, "LD", cosb[:], l3cos)
    L.dma(L.y, "LD", sinb[:], l3sin)
    L.dma(L.y, "LD", b1c[:], b1I)
    L.dma(L.y, "LD", b2c[:], b2I)
    L.dma(L.y, "LD", brr[:], brI)
    ld_c = L.cnt["LD"]
    L.dma(L.g, "GD", sel8[:], sel8I)
    L.wait(L.v, "LD", ld_c)
    L.wait(L.s, "LD", ld_c)
    b2v = b2c[:].rearrange("p (o e) -> p o e", o=8)
    L.v.tensor_reduce(b2s[:], b2v, axis=AX.X, op=ALU.add)
    L.inc(L.v, "VE")

    ve_ln, ln_st = emit_ln_fm(L, x3, o_xnT_tiles, xnT_b, eyef, pu[:, 0:128], fcol)

    bank_free = {}

    def proj(WT, kind, dst):
        L.wait(L.g, "PE")
        gd = L.dma(L.g, "GD", slab_dst(wres[:], NCH), WT.rearrange("(c p) f -> p c f", p=128))
        L.wait(L.t, "GD", gd)
        L.wait(L.t, "VE", ve_ln)
        for fc in range(NCH):
            for tt in range(NTT):
                bi = (fc * NTT + tt) % 2
                sem, cv = bank_free.get(bi, (None, 0))
                if sem is not None:
                    L.wait(L.t, sem, cv)
                for k in range(NCH):
                    L.t.matmul(pb[bi][:],
                               wres[:, k * 1024 + fc * 128: k * 1024 + (fc + 1) * 128],
                               xnT_b[:, k * NTOK + tt * TT: k * NTOK + (tt + 1) * TT],
                               start=(k == 0), stop=(k == NCH - 1))
                pe_d = L.inc(L.t, "PE")
                stg = stg_b[fc % 2]
                sd = bank_free.get(("st", fc % 2))
                L.wait(L.s, "PE", pe_d)
                L.wait(L.s, "VE")
                if sd is not None:
                    L.wait(L.s, "ST", sd[1])
                L.s.copy(stg[:], pb[bi][:])
                se_c = L.inc(L.s, "SE")
                bank_free[bi] = ("SE", se_c)
                if kind == "v":
                    L.wait(L.y, "SE", se_c)
                    bank_free[("st", fc % 2)] = ("ST", L.dma(
                        L.y, "ST", dst[fc * 128:(fc + 1) * 128, tt * TT:(tt + 1) * TT], stg[:]))
                else:
                    L.wait(L.v, "SE", se_c)
                    L.v.tensor_mul(rotA[:], stg[:], cosb[:, tt * TT:(tt + 1) * TT])
                    for r in range(4):
                        ro, so = r * 32, (r ^ 1) * 32
                        L.v.tensor_mul(rotB[so:so + 32, :], stg[ro:ro + 32, :],
                                       sinb[ro:ro + 32, tt * TT:(tt + 1) * TT])
                    L.v.tensor_add(stg[:], rotA[:], rotB[:])
                    ve_r = L.inc(L.v, "VE")
                    L.wait(L.y, "VE", ve_r)
                    bank_free[("st", fc % 2)] = ("ST", L.dma(
                        L.y, "ST", dst[fc * 128:(fc + 1) * 128, tt * TT:(tt + 1) * TT], stg[:]))

    proj(wqT, "q", o_qT)
    proj(wkT, "k", o_kT)
    proj(wvT, "v", o_vT)

    ld_wr = L.dma(L.y, "LD", slab_dst(wrf[:], NCH), WrI.rearrange("(c p) e -> p c e", p=128))
    rb_free = 0
    pu_free_t = 0
    for i in range(NTILE):
        L.wait(L.y, "ST", ln_st[i])
        if rb_free:
            L.wait(L.y, "PE", rb_free)
        ld_rb = L.dma(L.y, "LD", rbuf[:], o_xnT_tiles[i])
        L.wait(L.t, "LD", max(ld_wr, ld_rb))
        L.wait(L.t, "VE", pu_free_t)
        for c in range(NCH):
            L.t.matmul(pu[:, 0:8], rbuf[:, c * 128:(c + 1) * 128],
                       wrf[:, c * 8:(c + 1) * 8], start=(c == 0), stop=False)
        L.t.matmul(pu[:, 0:8], o1f[:], brr[:], start=False, stop=True)
        pe_r = L.inc(L.t, "PE")
        rb_free = pe_r
        L.wait(L.v, "PE", pe_r)
        L.v.tensor_copy(mk8[:], pu[:, 0:8])
        L.v.reduce_max(mstat[:], mk8[:], axis=AX.X)
        L.selfsync(L.v, "VE")
        L.v.tensor_scalar(mk8[:], mk8[:], mstat[:], None, op0=ALU.is_equal)
        ve_m = L.inc(L.v, "VE")
        L.wait(L.t, "VE", ve_m)
        L.t.transpose(pu[0:8, 128:256], mk8[:], eyef[:])
        pe_t = L.inc(L.t, "PE")
        L.wait(L.v, "PE", pe_t)
        L.v.tensor_copy(mT8[:, i * 128:(i + 1) * 128], pu[0:8, 128:256])
        pu_free_t = L.inc(L.v, "VE")

    w1_guard = {0: 0, 1: 0}
    w2_guard = {0: 0, 1: 0}
    g3_done = {}
    L.wait(L.v, "PE")
    L.wait(L.v, "SE")
    L.v.memset(y_acc[:], 0.0)
    ve_y0 = L.inc(L.v, "VE")
    for e in range(8):
        for tt in range(NTT):
            L.wait(L.t, "VE")
            L.t.matmul(pu[:], sel8[:, e * 128:(e + 1) * 128],
                       mT8[:, tt * TT:(tt + 1) * TT], start=True, stop=True)
            pe_mb = L.inc(L.t, "PE")
            L.wait(L.v, "PE", pe_mb)
            L.v.tensor_copy(mbce[:, tt * TT:(tt + 1) * TT], pu[:])
            L.inc(L.v, "VE")
        ve_mb = L.cnt["VE"]
        for blk in range(8):
            it = e * 8 + blk
            wb = w1b[it % 2]
            L.wait(L.g, "PE", w1_guard[it % 2])
            gd1 = L.dma(L.g, "GD", slab_dst(wb[:], NCH),
                        W1I[e][:, blk * 512:(blk + 1) * 512].rearrange("(c p) f -> p c f", p=128))
            wb2 = w2b[it % 2]
            L.wait(L.g, "PE", w2_guard[it % 2])
            gd2 = L.dma(L.g, "GD", slab_dst(wb2[:], 4),
                        W2I[e][blk * 512:(blk + 1) * 512, :].rearrange("(c p) f -> p c f", p=128))
            L.wait(L.t, "GD", gd1)
            for tt in range(NTT):
                for oc in range(4):
                    bi = 4 + (oc % 2)
                    sem, cv = bank_free.get(bi, (None, 0))
                    if sem is not None:
                        L.wait(L.t, sem, cv)
                    for k in range(NCH):
                        L.t.matmul(pb[bi][:],
                                   wb[:, k * 512 + oc * 128: k * 512 + (oc + 1) * 128],
                                   xnT_b[:, k * NTOK + tt * TT: k * NTOK + (tt + 1) * TT],
                                   start=(k == 0), stop=(k == NCH - 1))
                    pe_h = L.inc(L.t, "PE")
                    L.wait(L.v, "PE", pe_h)
                    L.wait(L.v, "SE")
                    if tt == 0 and blk == 0 and oc == 0:
                        L.wait(L.v, "VE", ve_mb)
                    L.v.tensor_mul(tmpf[:], pb[bi][:], mbce[:, tt * TT:(tt + 1) * TT])
                    bank_free[bi] = ("VE", (ve_h := L.inc(L.v, "VE")))
                    L.wait(L.s, "VE", ve_h)
                    L.s.activation(hb[:, oc * 512:(oc + 1) * 512], tmpf[:], AF.Relu,
                                   bias=b1c[:, e * 32 + blk * 4 + oc: e * 32 + blk * 4 + oc + 1])
                    L.inc(L.s, "SE")
                se_h = L.cnt["SE"]
                if tt == 0:
                    L.wait(L.t, "GD", gd2)
                L.wait(L.t, "SE", se_h)
                for oc in range(NCH):
                    bi = oc % 2
                    sem, cv = bank_free.get(bi, (None, 0))
                    if sem is not None:
                        L.wait(L.t, sem, cv)
                    for k in range(4):
                        L.t.matmul(pb[bi][:],
                                   wb2[:, k * 1024 + oc * 128: k * 1024 + (oc + 1) * 128],
                                   hb[:, k * 512:(k + 1) * 512],
                                   start=(k == 0), stop=(k == 3))
                    pe_y = L.inc(L.t, "PE")
                    L.wait(L.v, "PE", pe_y)
                    if e == 0 and blk == 0 and tt == 0 and oc < 2:
                        L.wait(L.v, "VE", ve_y0)
                    yo = (tt * NCH + oc) * 512
                    L.v.tensor_add(y_acc[:, yo:yo + 512],
                                   y_acc[:, yo:yo + 512], pb[bi][:])
                    bank_free[bi] = ("VE", L.inc(L.v, "VE"))
            w1_guard[it % 2] = L.cnt["PE"]
            w2_guard[it % 2] = L.cnt["PE"]
    ve_yf = L.cnt["VE"]
    for tt in range(NTT):
        for oc in range(NCH):
            stg = stg_f[oc % 2]
            gp = g3_done.get(oc % 2)
            L.wait(L.s, "VE", ve_yf)
            if gp is not None:
                L.wait(L.s, "ST", gp)
            yo = (tt * NCH + oc) * 512
            L.s.activation(stg[:], y_acc[:, yo:yo + 512], AF.Silu,
                           bias=b2s[:, oc:oc + 1])
            se_g = L.inc(L.s, "SE")
            L.wait(L.y, "SE", se_g)
            g3_done[oc % 2] = L.dma(L.y, "ST",
                                    o_g3T[oc * 128:(oc + 1) * 128, tt * TT:(tt + 1) * TT], stg[:])
    L.wait(L.y, "ST")
    return L


def prep_A3_weights(inp):
    perm = l3_perm()
    wq = fmaj(inp["L3_attn_wq"])
    wk = fmaj(inp["L3_attn_wk"])
    wv = fmaj(inp["L3_attn_wv"])
    b1 = np.asarray(inp["L3_ffn_b1"], np.float32)
    b2 = np.asarray(inp["L3_ffn_b2"], np.float32)
    b1c = np.zeros((128, 8 * 32), np.float32)
    for e in range(8):
        b1c[:, e * 32:(e + 1) * 32] = b1[e].reshape(32, 128).T
    b2c = np.zeros((128, 8 * 8), np.float32)
    for oc in range(8):
        for e in range(8):
            b2c[:, oc * 8 + e] = b2[e, oc * 128:(oc + 1) * 128]
    return {
        "a3_wqT": np.ascontiguousarray(wq[:, perm]),
        "a3_wkT": np.ascontiguousarray(wk[:, perm]),
        "a3_wvT": np.ascontiguousarray(wv),
        "a3_Wr": np.ascontiguousarray(np.asarray(inp["L3_ffn_Wr"], np.float32)),
        "a3_br": np.asarray(inp["L3_ffn_br"], np.float32).reshape(1, 8),
        "a3_W1": np.ascontiguousarray(np.asarray(inp["L3_ffn_W1"], np.float32)),
        "a3_W2": np.ascontiguousarray(np.asarray(inp["L3_ffn_W2"], np.float32)),
        "a3_b1c": b1c,
        "a3_b2c": b2c,
    }


def b3_consts():
    c = np.arange(1280)[None, :]
    p = np.arange(128)[:, None]
    M = (c >= p + 384).astype(np.float32)
    return {"c_caus": M.astype(BF)}


def build_B3():
    L = Launch("B3")
    C = declare_consts(L)
    qTh = L.din("b3_q", [8, 64, 4096], BF16)
    kTh = L.din("b3_k", [8, 64, 4096], BF16)
    vh = L.din("b3_v", [8, 128, 32 * 64], BF16)
    causI = L.din("c_caus", [128, 1280], BF16)
    o_aoT = L.dout("ao3T", [8, 64, 4096], BF16)

    fcol = L.sb("fcol", [128, len(FC_VALS)], F32)
    o64 = L.sb("o64", [64, 1], BF16)
    o128 = L.sb("o128", [128, 1], BF16)
    o1f = L.sb("o1f", [1, 128], F32)
    caus = L.sb("caus", [128, 1280], BF16)
    qta = L.sb("qta", [128, 4096], BF16)
    kta = L.sb("kta", [128, 4096], BF16)
    vtok = L.sb("vtok", [128, 32 * 64], BF16)
    PT = L.sb("PT", [128, 32 * 512], BF16)
    sqx = L.sb("sqx", [64, 4096], BF16)
    nq = L.sb("nq", [1, 4096], F32)
    nk = L.sb("nk", [1, 4096], F32)
    mk1 = L.sb("mk1", [1, 1], F32)
    brow = L.sb("brow", [1, 4096], F32)
    rrow = L.sb("rrow", [1, 512], F32)
    ravb = L.sb("ravb", [64, 512], F32)
    onesb = L.sb("onesb", [128, 64], BF16)
    ao_st = [L.sb(f"ao_st{i}", [64, 512], BF16) for i in range(2)]

    pb = [L.ps(f"pb{i}", [128, 512], F32) for i in range(7)]
    pu = L.ps("pu", [128, 512], F32)

    L.dma(L.y, "LD", fcol[:], C["fcol"])
    L.dma(L.y, "LD", o64[:], C["ones64c"])
    L.dma(L.y, "LD", o128[:], C["ones128c"])
    L.dma(L.y, "LD", o1f[:], C["ones1f"])
    L.dma(L.y, "LD", caus[:], causI)
    ld_c = L.cnt["LD"]
    L.wait(L.t, "LD", ld_c)
    L.wait(L.v, "LD", ld_c)
    L.wait(L.s, "LD", ld_c)
    L.v.memset(onesb[:], 1.0)
    L.v.memset(qta[64:66, :], 0.0)
    L.v.memset(kta[64:66, :], 0.0)
    L.inc(L.v, "VE")

    F8 = FC[0.125]
    sc18 = fcol[:, F8:F8 + 1]
    bank_free = {}
    ao_free = {}
    pe_unit_done = 0
    for u in range(8):
        L.wait(L.g, "PE", pe_unit_done)
        L.dma(L.g, "GD", qta[0:64, :], qTh[u])
        L.dma(L.g, "GD", kta[0:64, :], kTh[u])
        gd_v = L.dma(L.g, "GD", vtok[:], vh[u])
        L.wait(L.t, "GD", gd_v)
        for (src, dstrow) in ((qta, nq), (kta, nk)):
            L.wait(L.s, "GD", gd_v)
            L.wait(L.s, "VE")
            L.s.activation(sqx[:], src[0:64, :], AF.Square)
            se_sq = L.inc(L.s, "SE")
            L.wait(L.t, "SE", se_sq)
            for t8 in range(8):
                sem, cv = bank_free.get(7, (None, 0))
                if sem is not None:
                    L.wait(L.t, sem, cv)
                L.t.matmul(pu[0:1, 0:512], o64[:], sqx[:, t8 * 512:(t8 + 1) * 512],
                           start=True, stop=True)
                pe_n = L.inc(L.t, "PE")
                L.wait(L.s, "PE", pe_n)
                L.s.activation(dstrow[0:1, t8 * 512:(t8 + 1) * 512], pu[0:1, 0:512], AF.Sqrt)
                bank_free[7] = ("SE", L.inc(L.s, "SE"))
        se_rows = L.cnt["SE"]
        L.wait(L.v, "SE", se_rows)
        L.v.reduce_max(mk1[:], nk[:], axis=AX.X)
        L.selfsync(L.v, "VE")
        L.v.tensor_scalar(brow[:], nq[:], mk1[:], -1.0, op0=ALU.mult, op1=ALU.mult)
        L.v.tensor_copy(qta[64:65, :], brow[:])
        L.v.memset(kta[64:65, :], 1.0)
        ve_aug = L.inc(L.v, "VE")
        L.wait(L.t, "VE", ve_aug)
        for st in range(8):
            q0 = st * 512
            npres = q0 // 128 + 4
            for kc in range(npres):
                bi = kc % 4
                sem, cv = bank_free.get(bi, (None, 0))
                if sem is not None:
                    L.wait(L.t, sem, cv)
                L.t.matmul(pb[bi][:], kta[0:66, kc * 128:(kc + 1) * 128],
                           qta[0:66, q0:q0 + 512], start=True, stop=True)
                pe_s = L.inc(L.t, "PE")
                L.wait(L.s, "PE", pe_s)
                L.s.activation(PT[:, kc * 512:(kc + 1) * 512], pb[bi][:], AF.Exp, scale=sc18)
                se_e = L.inc(L.s, "SE")
                bank_free[bi] = ("SE", se_e)
                o = kc * 128 - q0
                if o >= -384:
                    L.wait(L.v, "SE", se_e)
                    L.v.tensor_mul(PT[:, kc * 512:(kc + 1) * 512],
                                   PT[:, kc * 512:(kc + 1) * 512],
                                   caus[:, 384 - o: 896 - o])
                    L.inc(L.v, "VE")
            se_pt = L.cnt["SE"]
            ve_pt = L.cnt["VE"]
            sem, cv = bank_free.get(5, (None, 0))
            if sem is not None:
                L.wait(L.t, sem, cv)
            L.wait(L.t, "SE", se_pt)
            L.wait(L.t, "VE", ve_pt)
            for kc in range(npres):
                L.t.matmul(pb[5][0:64, :], onesb[:], PT[:, kc * 512:(kc + 1) * 512],
                           start=(kc == 0), stop=(kc == npres - 1))
            pe_sum = L.inc(L.t, "PE")
            L.wait(L.s, "PE", pe_sum)
            L.wait(L.s, "VE")
            L.s.activation(ravb[:], pb[5][0:64, :], AF.Ln)
            L.s.activation(ravb[:], ravb[:], AF.Exp, scale=-1.0)
            se_rav = L.inc(L.s, "SE")
            bank_free[5] = ("SE", se_rav)
            sem, cv = bank_free.get(4, (None, 0))
            if sem is not None:
                L.wait(L.t, sem, cv)
            for kc in range(npres):
                L.t.matmul(pb[4][0:64, :], vtok[:, kc * 64:(kc + 1) * 64],
                           PT[:, kc * 512:(kc + 1) * 512],
                           start=(kc == 0), stop=(kc == npres - 1))
            pe_av = L.inc(L.t, "PE")
            stg = ao_st[st % 2]
            af = ao_free.get(st % 2)
            L.wait(L.v, "PE", pe_av)
            L.wait(L.v, "SE", se_rav)
            if af is not None:
                L.wait(L.v, "ST", af)
            L.v.tensor_mul(stg[:], pb[4][0:64, :], ravb[:])
            bank_free[4] = ("VE", L.inc(L.v, "VE"))
            L.wait(L.y, "VE")
            ao_free[st % 2] = L.dma(L.y, "ST", o_aoT[u, :, q0:q0 + 512], stg[:])
        pe_unit_done = L.cnt["PE"]
    L.wait(L.y, "ST")
    return L


def prep_B3(q3T, k3T, v3T):
    ins = []
    for c in range(NCORES):
        b = c // 2
        qs, ks, vs = [], [], []
        for j in range(8):
            h = (c % 2) * 8 + j
            rows = slice(h * 64, (h + 1) * 64)
            q_u = np.concatenate([q3T[2 * b][rows], q3T[2 * b + 1][rows]], axis=1)
            k_u = np.concatenate([k3T[2 * b][rows], k3T[2 * b + 1][rows]], axis=1)
            v_u = np.concatenate([v3T[2 * b][rows], v3T[2 * b + 1][rows]], axis=1)
            vt = np.ascontiguousarray(v_u.T).reshape(32, 128, 64).transpose(1, 0, 2).reshape(128, 32 * 64)
            qs.append(q_u); ks.append(k_u); vs.append(np.ascontiguousarray(vt))
        ins.append({"b3_q": np.stack(qs), "b3_k": np.stack(ks), "b3_v": np.stack(vs)})
    return ins


def unpack_B3(resB):
    outs = []
    for c in range(NCORES):
        b, half = c // 2, c % 2
        aoT = np.empty((D, NTOK), dtype=resB[0]["ao3T"].dtype)
        for h in range(16):
            src = 2 * b + (h // 8)
            j = h % 8
            aoT[h * 64:(h + 1) * 64, :] = resB[src]["ao3T"][j][:, half * NTOK:(half + 1) * NTOK]
        outs.append(aoT)
    return outs



_BUILT = {}


def _get(name, fn):
    if name not in _BUILT:
        _BUILT[name] = fn()
    return _BUILT[name]


def _unpack_xnT_h(tiles, ntiles):
    a = tiles.reshape(ntiles, 128, NCH, 128)
    return np.ascontiguousarray(a.transpose(2, 1, 0, 3).reshape(D, ntiles * 128))


def kernel(**inputs):
    inp = {k: np.asarray(v) for k, v in inputs.items()}
    consts = dict(const_inputs())
    consts.update(l2_consts())
    consts.update(a3_consts())
    consts.update(b3_consts())

    x0 = np.ascontiguousarray(inp["L0_attn_representations"].reshape(16384, D).astype(np.float32))
    xs = shard_tokens(x0)

    LA0 = _get("A0", build_A0)
    wA0 = prep_A0_weights(inp)
    resA0 = LA0.run([dict(consts, x0=xs[c], **wA0) for c in range(NCORES)])
    LB0 = _get("B0", build_B0)
    resB0 = LB0.run([dict(consts, **m) for m in prep_B0(resA0)])
    aoTs = unpack_B0(resB0)
    LC0 = _get("C0", build_C0)
    wC0 = prep_C0(inp)
    resC0 = LC0.run([dict(consts, **wC0, c0_aoT=aoTs[c],
                          c0_xnT=host_unpack_xnT(resA0[c]["xnT_tiles"]),
                          c0_x0T=fmaj(xs[c]), c0_g0T=resA0[c]["g0T"])
                     for c in range(NCORES)])

    LA1 = _get("A1", build_A1)
    wA1 = prep_A1_weights(inp)
    resA1 = LA1.run([dict(consts, x1=np.ascontiguousarray(resC0[c]["x1T"].T), **wA1)
                     for c in range(NCORES)])
    LB1 = _get("B1", build_B1)
    resB1 = LB1.run([dict(consts, **m) for m in prep_B1(resA1)])
    oTs = unpack_B1(resB1)
    LC1 = _get("C1", build_C1)
    wC1 = prep_C1(inp)
    resC1 = LC1.run([dict(consts, **wC1, c1_oT=oTs[c], c1_x1T=resC0[c]["x1T"],
                          c1_g1T=resA1[c]["g1T"]) for c in range(NCORES)])

    x2_full = np.ascontiguousarray(
        np.concatenate([resC1[c]["x2T"].T for c in range(NCORES)], axis=0))
    LA2 = _get("A2", build_A2)
    wA2 = prep_A2_weights(inp)
    x2hs = prep_A2_x2h(x2_full)
    resA2 = LA2.run([dict(consts, **wA2, x2h=x2hs[c]) for c in range(NCORES)])
    xn2_full = np.ascontiguousarray(np.concatenate(
        [_unpack_xnT_h(resA2[c]["xn2T_tiles"], NTOKH // 128)[:, HB:HB + NTOK].T
         for c in range(NCORES)], axis=0))
    LTB = _get("TBL", build_TBL)
    resTB = LTB.run([dict(consts, **l2_tbl_inputs(c)) for c in range(NCORES)])
    mstd, m0v = l2_masks()
    LF = _get("L2F", build_L2F)
    y_full = xn2_full
    for li in range(6):
        wL = prep_L2F_weights(inp, li)
        ys = prep_L2F_y(y_full, li == 0)
        insF = []
        for c in range(NCORES):
            m0c = m0v if c % 2 == 0 else mstd
            insF.append(dict(consts, **wL, l2_yT=ys[c], l2_m0=m0c, l2_ms=mstd,
                             **l2_keep(c),
                             l2_cs=resTB[c]["cs"], l2_t2=resTB[c]["t2"],
                             l2_ci=resTB[c]["ci"], l2_t2i=resTB[c]["t2i"]))
        resF = LF.run(insF)
        y_full = np.ascontiguousarray(
            np.concatenate([resF[c]["oyT"].T for c in range(NCORES)], axis=0))
    LC2 = _get("C2", build_C2)
    resC2 = LC2.run([dict(consts, c2_y6T=fmaj(y_full[c * NTOK:(c + 1) * NTOK]),
                          c2_s2T=resA2[c]["s2T"], c2_x2T=resC1[c]["x2T"])
                     for c in range(NCORES)])

    LT3 = _get("TBL3", build_TBL3)
    resT3 = LT3.run([dict(consts, **tbl3_inputs(c)) for c in range(NCORES)])
    LA3 = _get("A3", build_A3)
    wA3 = prep_A3_weights(inp)
    resA3 = LA3.run([dict(consts, **wA3, x3=np.ascontiguousarray(resC2[c]["x3T"].T),
                          l3cos=resT3[c]["l3cos"], l3sin=resT3[c]["l3sin"])
                     for c in range(NCORES)])
    LB3 = _get("B3", build_B3)
    insB3 = prep_B3([resA3[c]["q3T"] for c in range(NCORES)],
                    [resA3[c]["k3T"] for c in range(NCORES)],
                    [resA3[c]["v3T"] for c in range(NCORES)])
    resB3 = LB3.run([dict(consts, **m) for m in insB3])
    ao3Ts = unpack_B3(resB3)
    LC3 = _get("C3", build_C3)
    resC3 = LC3.run([dict(consts, c3_aoT=ao3Ts[c], c3_woT=fmaj(inp["L3_attn_wo"]),
                          c3_x3T=resC2[c]["x3T"], c3_g3T=resA3[c]["g3T"])
                     for c in range(NCORES)])

    out = np.concatenate([resC3[c]["x4T"].T for c in range(NCORES)], axis=0)
    return np.ascontiguousarray(out.reshape(4, 4096, D).astype(np.float32))


# version 06 (final)
# speedup vs baseline: 1.1610x; 1.0054x over previous
import math
import numpy as np
from contextlib import ExitStack
from collections import defaultdict
import traceback

import ml_dtypes
import concourse.bass as bass
import concourse.mybir as mybir
from concourse.bass_utils import run_bass_kernel_spmd

F32 = mybir.dt.float32
BF16 = mybir.dt.bfloat16
AF = mybir.ActivationFunctionType
ALU = mybir.AluOpType
AX = mybir.AxisListType
BF = ml_dtypes.bfloat16

USE_ACT_LRELU = True
LN_SKIP_TP = False
LN_SKIP_STATS = False
LN_TP_MODE = 0

FC_VALS = [1e-5, -0.5, 0.125, 1e-12, 2.5, 1e-20, 0.08838834764831845, 1.0 / 1024, 8.0, 0.0]
FC = {v: i for i, v in enumerate(FC_VALS)}

NCORES = 8
NTOK = 2048
D = 1024
NCH = D // 128
NTILE = NTOK // 128
TT = 512
NTT = NTOK // TT

ENGINES = ("tensor", "vector", "scalar", "gpsimd", "sync")


class _Op:
    __slots__ = ("attr", "args", "kwargs", "incs", "src")

    def __init__(self, attr, args, kwargs, src):
        self.attr, self.args, self.kwargs, self.incs, self.src = attr, args, kwargs, [], src


AUTO_SEM = {"vector": "VE", "scalar": "SE", "gpsimd": "GE"}


class _Eng:

    def __init__(self, L, name):
        self._L = L
        self._name = name

    def __getattr__(self, attr):
        L, name = self._L, self._name

        def call(*args, **kwargs):
            fr = traceback.extract_stack(limit=3)[0]
            src = f"{fr.filename}:{fr.lineno}"
            auto = AUTO_SEM.get(name)
            if auto is not None and attr not in ("wait_ge", "nop") \
                    and not attr.startswith("dma_start"):
                q = L.q[name]
                for prev in reversed(q):
                    if prev.attr in ("wait_ge", "nop") or prev.attr.startswith("dma_start"):
                        continue
                    if not prev.incs:
                        prev.incs.append((L.sem(auto), 1))
                        L.cnt[auto] += 1
                    break
                tgt = L.cnt[auto]
                if tgt > 0 and L._autow.get(name, -1) < tgt:
                    q.append(_Op("wait_ge", (L.sem(auto), tgt), {}, src))
                    L._autow[name] = tgt
            L.q[name].append(_Op(attr, args, kwargs, src))
            return self

        return call


class Launch:
    def __init__(self, name):
        self.name = name
        self.nc = bass.Bass("TRN2", target_bir_lowering=False)
        self.stack = ExitStack()
        self.q = {e: [] for e in ENGINES}
        self.t = _Eng(self, "tensor")
        self.v = _Eng(self, "vector")
        self.s = _Eng(self, "scalar")
        self.g = _Eng(self, "gpsimd")
        self.y = _Eng(self, "sync")
        self.sems = {}
        self.cnt = defaultdict(int)
        self.out_names = []
        self.in_names = []
        self._autow = {}
        self._finalized = False

    def din(self, name, shape, dtype=F32):
        self.in_names.append(name)
        return self.nc.dram_tensor(name, list(shape), dtype, kind="ExternalInput").ap()

    def dout(self, name, shape, dtype=F32):
        self.out_names.append(name)
        return self.nc.dram_tensor(name, list(shape), dtype, kind="ExternalOutput").ap()

    def sb(self, name, shape, dtype):
        return self.stack.enter_context(self.nc.sbuf_tensor(name, list(shape), dtype))

    def ps(self, name, shape, dtype=F32):
        return self.stack.enter_context(self.nc.psum_tensor(name, list(shape), dtype))

    def sem(self, name):
        if name not in self.sems:
            self.sems[name] = self.stack.enter_context(self.nc.semaphore(name))
        return self.sems[name]

    def inc(self, eng, semname, n=None):
        sem = self.sem(semname)
        last = self.q[eng._name][-1]
        if last.incs:
            assert last.incs == [(sem, 1)], f"double inc on {last.attr} at {last.src}"
            return self.cnt[semname]
        if n is None:
            n = 16 if last.attr.startswith("dma_start") else 1
        last.incs.append((sem, n))
        self.cnt[semname] += n
        return self.cnt[semname]

    def wait(self, eng, semname, val=None):
        sem = self.sem(semname)
        if val is None:
            val = self.cnt[semname]
        if val <= 0:
            return
        eng.wait_ge(sem, val)

    def dma(self, eng, semname, out, in_):
        self.wait(eng, semname)
        eng.dma_start(out=out, in_=in_)
        return self.inc(eng, semname)

    def selfsync(self, eng, semname):
        self.inc(eng, semname)
        self.wait(eng, semname)

    def finalize(self):
        if self._finalized:
            return self
        self._finalized = True
        nc = self.nc
        with self.stack, nc.Block() as blk:
            for e in ENGINES:
                ops = self.q[e]
                if not ops:
                    continue

                def mk(ops, ename):
                    def f(eng):
                        for op in ops:
                            try:
                                inst = getattr(eng, op.attr)(*op.args, **op.kwargs)
                                for (sem, n) in op.incs:
                                    inst = inst.then_inc(sem, n)
                            except Exception as ex:
                                raise RuntimeError(
                                    f"[{self.name}/{ename}] {op.attr} from {op.src}: {ex}") from ex
                    return f

                getattr(blk, e)(mk(ops, e))
        return self

    def run(self, in_maps, **kw):
        self.finalize()
        filt = [{k: m[k] for k in self.in_names} for m in in_maps]
        res = run_bass_kernel_spmd(self.nc, filt, core_ids=list(range(len(filt))), **kw)
        return [{k: np.asarray(r[k]) for k in self.out_names} for r in res.results]



def shard_tokens(x2d):
    return [np.ascontiguousarray(x2d[c * NTOK:(c + 1) * NTOK]) for c in range(NCORES)]


def fmaj(x2d):
    return np.ascontiguousarray(np.asarray(x2d).T)


def const_inputs():
    return {
        "c_eyeb": np.eye(128, dtype=np.float32).astype(BF),
        "c_eyef": np.eye(128, dtype=np.float32),
        "c_ones1b": np.ones((1, 128), dtype=np.float32).astype(BF),
        "c_sel16": _sel16(),
        "c_fcol": np.tile(np.array([FC_VALS], dtype=np.float32), (128, 1)),
        "c_ones64c": np.ones((64, 1), dtype=np.float32).astype(BF),
        "c_ones128c": np.ones((128, 1), dtype=np.float32).astype(BF),
        "c_ones1f": np.ones((1, 128), dtype=np.float32),
        "c_utri32": np.triu(np.ones((32, 32), dtype=np.float32)).astype(BF),
        "c_ones128f": np.ones((128, 1), dtype=np.float32),
    }


def _sel16():
    m = np.zeros((16, 16 * 128), dtype=np.float32)
    for e in range(16):
        m[e, e * 128:(e + 1) * 128] = 1.0
    return m.astype(BF)


def declare_consts(L):
    return {
        "eyeb": L.din("c_eyeb", [128, 128], BF16),
        "eyef": L.din("c_eyef", [128, 128], F32),
        "ones1b": L.din("c_ones1b", [1, 128], BF16),
        "sel16": L.din("c_sel16", [16, 16 * 128], BF16),
        "fcol": L.din("c_fcol", [128, len(FC_VALS)], F32),
        "ones64c": L.din("c_ones64c", [64, 1], BF16),
        "ones128c": L.din("c_ones128c", [128, 1], BF16),
        "ones1f": L.din("c_ones1f", [1, 128], F32),
        "utri32": L.din("c_utri32", [32, 32], BF16),
        "ones128f": L.din("c_ones128f", [128, 1], F32),
    }


def wslab_ap(W, f0, f1, p=128):
    return W[:, f0:f1].rearrange("(c p) f -> p c f", p=p)


def slab_dst(sb_ap, nk):
    return sb_ap.rearrange("p (c f) -> p c f", c=nk)



def emit_ln_fm(L, x_in, o_xnT_tiles, xnT_b, eyef, pt, fcol, pref="ln", ntok=NTOK):
    xtile = L.sb(pref + "_x", [128, D], F32)
    sq = L.sb(pref + "_sq", [128, D], F32)
    stat = L.sb(pref + "_stat", [128, 8], F32)
    xn_tm = L.sb(pref + "_xn", [128, D], F32)
    st_buf = [L.sb(f"{pref}_st{i}", [128, D], F32) for i in range(2)]
    inv_d = 1.0 / D
    st_done = {}
    prev_xn_done = None
    for i in range(ntok // 128):
        if prev_xn_done is not None:
            L.wait(L.y, "VE", prev_xn_done)
        tgt = L.dma(L.y, "LD", xtile[:], x_in[i * 128:(i + 1) * 128, :])
        L.wait(L.v, "LD", tgt)
        if LN_SKIP_STATS:
            L.v.tensor_copy(xn_tm[:], xtile[:])
            prev_xn_done = L.inc(L.v, "VE")
        else:
            _emit_ln_stats = True
        if not LN_SKIP_STATS:
            L.v.reduce_sum(stat[:, 0:1], xtile[:], axis=AX.X)
            L.v.tensor_mul(sq[:], xtile[:], xtile[:])
            L.v.reduce_sum(stat[:, 1:2], sq[:], axis=AX.X)
            L.v.tensor_scalar_mul(stat[:, 2:3], stat[:, 0:1], inv_d)
            L.v.tensor_scalar_mul(stat[:, 3:4], stat[:, 1:2], inv_d)
            L.v.tensor_mul(stat[:, 4:5], stat[:, 2:3], stat[:, 2:3])
            L.v.tensor_sub(stat[:, 5:6], stat[:, 3:4], stat[:, 4:5])
            ve_var = L.inc(L.v, "VE")
            L.wait(L.s, "VE", ve_var)
            L.s.activation(stat[:, 6:7], stat[:, 5:6], AF.Ln, bias=fcol[:, FC[1e-5]:FC[1e-5] + 1])
            L.s.activation(stat[:, 7:8], stat[:, 6:7], AF.Exp, scale=fcol[:, FC[-0.5]:FC[-0.5] + 1])
            se_rstd = L.inc(L.s, "SE")
            L.wait(L.v, "SE", se_rstd)
            L.v.tensor_scalar(xn_tm[:], xtile[:], stat[:, 2:3], stat[:, 7:8],
                              op0=ALU.subtract, op1=ALU.mult)
            prev_xn_done = L.inc(L.v, "VE")
        buf = st_buf[i % 2]
        if LN_SKIP_TP:
            L.wait(L.y, "VE", prev_xn_done)
            st_done[i] = L.dma(L.y, "ST", o_xnT_tiles[i], xn_tm[:])
            continue
        if i >= 2:
            L.wait(L.t, "ST", st_done[i - 2])
        L.wait(L.t, "VE", prev_xn_done)
        for c in range(NCH):
            L.t.transpose(pt, xn_tm[:, c * 128:(c + 1) * 128], eyef[:])
            pe_tp = L.inc(L.t, "PE")
            if LN_TP_MODE in (0, 2):
                L.wait(L.s, "PE", pe_tp)
                L.s.copy(buf[:, c * 128:(c + 1) * 128], pt)
                se_c = L.inc(L.s, "SE")
                L.wait(L.t, "SE", se_c)
            if LN_TP_MODE in (0, 1):
                L.wait(L.v, "PE", pe_tp)
                if LN_TP_MODE == 0:
                    L.wait(L.v, "SE", se_c)
                L.v.tensor_copy(xnT_b[:, c * ntok + i * 128: c * ntok + (i + 1) * 128], pt)
                ve_c = L.inc(L.v, "VE")
                L.wait(L.t, "VE", ve_c)
        if LN_TP_MODE in (0, 2):
            L.wait(L.y, "SE")
            st_done[i] = L.dma(L.y, "ST", o_xnT_tiles[i], buf[:])
        else:
            L.wait(L.y, "VE", prev_xn_done)
            st_done[i] = L.dma(L.y, "ST", o_xnT_tiles[i], xn_tm[:])
    return L.cnt["VE"], st_done


def emit_proj_b(L, wsb, wstride, xT_sb, fchunks, evac, banks, bank_free, kchunks=NCH,
                pe_wait=None, xstride=NTOK):
    nb = len(banks)
    i = 0
    for (fcol, tag) in fchunks:
        for tt in range(NTT):
            bi = i % nb
            sem, cntv = bank_free.get(bi, (None, 0))
            if sem is not None:
                L.wait(L.t, sem, cntv)
            if pe_wait is not None:
                pe_wait()
                pe_wait = None
            for k in range(kchunks):
                L.t.matmul(banks[bi][:],
                           wsb[:, k * wstride + fcol: k * wstride + fcol + 128],
                           xT_sb[:, k * xstride + tt * TT: k * xstride + (tt + 1) * TT],
                           start=(k == 0), stop=(k == kchunks - 1))
            pe_done = L.inc(L.t, "PE")
            bank_free[bi] = evac(tag, tt, banks[bi], pe_done)
            i += 1



def build_A0(stop=None):
    L = Launch("A0")
    C = declare_consts(L)
    x0 = L.din("x0", [NTOK, D])
    WcT = L.din("a0_WcT", [D, 10240])
    bc_col = L.din("a0_bc_col", [128, 80])
    bc_row = L.din("a0_bc_row", [1, 10240])
    wrT = L.din("f0_wrT", [D, 16])
    mhT = L.din("f0_mhT", [D, D])
    kfw = L.din("f0_kf", [8, 128, 4096])
    vfw = L.din("f0_vf", [8, 128, 4096])
    mgT = L.din("f0_mgT", [D, D])

    o_xnT_tiles = L.dout("xnT_tiles", [NTILE, 128, D])
    o_qT = L.dout("qT", [D, NTOK], BF16)
    o_kT = L.dout("kT", [D, NTOK], BF16)
    o_v = L.dout("v_blk", [16, 128, 16 * 512], BF16)
    o_g0T = L.dout("g0T", [D, NTOK])

    eyef = L.sb("eyef", [128, 128], F32)
    ones1b = L.sb("ones1b", [1, 128], BF16)
    sel16 = L.sb("sel16", [16, 16 * 128], BF16)
    fcol = L.sb("fcol", [128, len(FC_VALS)], F32)
    bccol = L.sb("bccol", [128, 80], F32)
    bcv = [L.sb(f"bcv{i}", [1, 512], BF16) for i in range(2)]

    xnT_b = L.sb("xnT_b", [128, NCH * NTOK], BF16)
    yTt = L.sb("yTt", [128, NCH * 512], BF16)
    cwbc = L.sb("cwbc", [128, 16 * TT], BF16)
    cwT16 = L.sb("cwT16", [16, NTOK], BF16)

    wbufs = [L.sb(f"wbuf{i}", [128, NCH * 512], BF16) for i in range(2)]
    wres_mh = L.sb("wres_mh", [128, NCH * 1024], BF16)
    wres_mg = L.sb("wres_mg", [128, NCH * 1024], BF16)
    wrouter = L.sb("wrouter", [128, NCH * 16], F32)
    rbuf = L.sb("rbuf", [128, D], F32)
    qk_st = L.sb("qk_st", [128, NTOK], BF16)
    lr_tmp = L.sb("lr_tmp", [128, 512], F32)
    v_st = L.sb("v_st", [128, 16 * 512], BF16)
    qmh_sb = [L.sb(f"qmh_sb{i}", [128, 512], BF16) for i in range(2)]
    cw16 = L.sb("cw16", [128, 16], F32)
    cwt1 = L.sb("cwt1", [128, 16], F32)
    cwt2 = L.sb("cwt2", [128, 16], F32)
    cstat = L.sb("cstat", [128, 2], F32)
    kfb = [L.sb(f"kfb{i}", [128, 4096], BF16) for i in range(2)]
    vfb = [L.sb(f"vfb{i}", [128, 4096], BF16) for i in range(2)]
    gel = [L.sb(f"gel{i}", [128, 512], BF16) for i in range(3)]
    actb = [L.sb(f"actb{i}", [128, 512], BF16) for i in range(3)]
    g0_st = [L.sb(f"g0_st{i}", [128, 512], F32) for i in range(2)]

    pb = [L.ps(f"pb{i}", [128, 512], F32) for i in range(7)]
    pu = L.ps("pu", [128, 512], F32)

    L.dma(L.y, "LD", eyef[:], C["eyef"])
    L.dma(L.y, "LD", bccol[:], bc_col)
    L.dma(L.y, "LD", fcol[:], C["fcol"])
    L.dma(L.g, "GD", ones1b[:], C["ones1b"])
    L.dma(L.g, "GD", sel16[:], C["sel16"])
    ve_ln, ln_st = emit_ln_fm(L, x0, o_xnT_tiles, xnT_b, eyef, pu[:, 0:128], fcol)

    if stop == "ln":
        L.wait(L.y, "ST")
        return L
    ld_wr = L.dma(L.y, "LD", slab_dst(wrouter[:], NCH), wrT.rearrange("(c p) e -> p c e", p=128))
    pu_free_t = 0
    rb_free = 0
    for i in range(NTILE):
        L.wait(L.y, "ST", ln_st[i])
        if rb_free:
            L.wait(L.y, "PE", rb_free)
        ld_rb = L.dma(L.y, "LD", rbuf[:], o_xnT_tiles[i])
        L.wait(L.t, "LD", max(ld_wr, ld_rb))
        L.wait(L.t, "VE", max(ve_ln, pu_free_t))
        for c in range(NCH):
            L.t.matmul(pu[:, 0:16],
                       rbuf[:, c * 128:(c + 1) * 128],
                       wrouter[:, c * 16:(c + 1) * 16],
                       start=(c == 0), stop=(c == NCH - 1))
        pe_r = L.inc(L.t, "PE")
        rb_free = pe_r
        L.wait(L.v, "PE", pe_r)
        L.v.tensor_copy(cw16[:], pu[:, 0:16])
        L.v.reduce_max(cstat[:, 0:1], cw16[:], axis=AX.X)
        L.selfsync(L.v, "VE")
        L.v.tensor_scalar(cwt1[:], cw16[:], cstat[:, 0:1], None, op0=ALU.is_equal)
        L.v.scalar_tensor_tensor(cwt2[:], cwt1[:], -1e30, cw16[:], op0=ALU.mult, op1=ALU.add)
        L.v.reduce_max(cstat[:, 1:2], cwt2[:], axis=AX.X)
        L.selfsync(L.v, "VE")
        L.v.tensor_scalar(cwt2[:], cw16[:], cstat[:, 1:2], None, op0=ALU.is_equal)
        L.v.tensor_add(cwt1[:], cwt1[:], cwt2[:])
        ve_m = L.inc(L.v, "VE")
        L.wait(L.s, "VE", ve_m)
        L.s.activation(cwt2[:], cw16[:], AF.Sigmoid)
        se_g = L.inc(L.s, "SE")
        L.wait(L.v, "SE", se_g)
        L.v.tensor_mul(cw16[:], cwt1[:], cwt2[:])
        ve_cw = L.inc(L.v, "VE")
        L.wait(L.t, "VE", ve_cw)
        L.t.transpose(pu[0:16, 128:256], cw16[:], eyef[:])
        pe_t = L.inc(L.t, "PE")
        L.wait(L.v, "PE", pe_t)
        L.v.tensor_copy(cwT16[:, i * 128:(i + 1) * 128], pu[0:16, 128:256])
        pu_free_t = L.inc(L.v, "VE")

    if stop == "p1":
        L.wait(L.y, "ST")
        return L
    qk_store = {}

    def evac_qk(tag, tt, bank, pe_done):
        fc_abs = tag
        fc = fc_abs // 128
        dst = qk_st[:, tt * TT:(tt + 1) * TT]
        if USE_ACT_LRELU:
            if tt == 0 and fc >= 1 and (fc - 1) in qk_store:
                L.wait(L.s, "ST", qk_store[fc - 1])
            L.wait(L.s, "PE", pe_done)
            L.s.activation(dst, bank[:], AF.Lrelu,
                           bias=bccol[:, fc:fc + 1], alpha=0.01)
            free = L.inc(L.s, "SE")
        else:
            if tt == 0 and fc >= 1 and (fc - 1) in qk_store:
                L.wait(L.v, "ST", qk_store[fc - 1])
            L.wait(L.v, "PE", pe_done)
            L.v.tensor_scalar(lr_tmp[:], bank[:], bccol[:, fc:fc + 1], None, op0=ALU.add)
            L.v.scalar_tensor_tensor(dst, lr_tmp[:], 0.01, lr_tmp[:], op0=ALU.mult, op1=ALU.max)
            free = ("SE", None)
            free = L.inc(L.v, "VE")
            ret = ("VE", free)
            if tt == NTT - 1:
                L.wait(L.y, "VE", free)
                d2 = o_qT if fc_abs < 1024 else o_kT
                r0 = fc_abs % 1024
                qk_store[fc] = L.dma(L.y, "ST", d2[r0:r0 + 128, :], qk_st[:])
            return ret
        if tt == NTT - 1:
            L.wait(L.y, "SE", free)
            dst = o_qT if fc_abs < 1024 else o_kT
            r0 = fc_abs % 1024
            qk_store[fc] = L.dma(L.y, "ST", dst[r0:r0 + 128, :], qk_st[:])
        return ("SE", free)

    bank_free = {}
    pe_blk = {}
    for b in range(4):
        buf = wbufs[b % 2]
        if b >= 2:
            L.wait(L.g, "PE", pe_blk[b - 2])
        ld = L.dma(L.g, "GD", slab_dst(buf[:], NCH), wslab_ap(WcT, b * 512, (b + 1) * 512))
        L.wait(L.t, "GD", ld)
        if b == 0:
            L.wait(L.t, "VE", ve_ln)
        emit_proj_b(L, buf, 512, xnT_b,
                    [(fc * 128, b * 512 + fc * 128) for fc in range(4)],
                    evac_qk, pb[0:4], bank_free)
        pe_blk[b] = L.cnt["PE"]

    if stop == "p2a":
        L.wait(L.y, "ST")
        return L
    vst_done = 0
    vbank_free = {}
    wbuf_free_pe = {0: pe_blk[2], 1: pe_blk[3]}
    for b in range(16):
        buf = wbufs[b % 2]
        L.wait(L.g, "PE", wbuf_free_pe[b % 2] if b >= 2 else wbuf_free_pe[b % 2])
        ld = L.dma(L.g, "GD", slab_dst(buf[:], NCH), wslab_ap(WcT, 2048 + b * 512, 2048 + (b + 1) * 512))
        ld_b = L.dma(L.g, "GD", bcv[b % 2][:], bc_row[0:1, 2048 + b * 512: 2048 + (b + 1) * 512])
        L.wait(L.t, "GD", ld_b)
        for i in range(NTILE):
            bi = 4 + (i % 2)
            sem, cntv = vbank_free.get(bi, (None, 0))
            if sem is not None:
                L.wait(L.t, sem, cntv)
            for k in range(NCH):
                L.t.matmul(pb[bi][:],
                           xnT_b[:, k * NTOK + i * 128: k * NTOK + (i + 1) * 128],
                           buf[:, k * 512:(k + 1) * 512],
                           start=(k == 0), stop=False)
            L.t.matmul(pb[bi][:], ones1b[:], bcv[b % 2][:], start=False, stop=True)
            pe_done = L.inc(L.t, "PE")
            if USE_ACT_LRELU:
                if i == 0 and b >= 1:
                    L.wait(L.s, "ST", vst_done)
                L.wait(L.s, "PE", pe_done)
                L.s.activation(v_st[:, i * 512:(i + 1) * 512], pb[bi][:], AF.Lrelu, alpha=0.01)
                vbank_free[bi] = ("SE", L.inc(L.s, "SE"))
            else:
                if i == 0 and b >= 1:
                    L.wait(L.v, "ST", vst_done)
                L.wait(L.v, "PE", pe_done)
                L.v.scalar_tensor_tensor(v_st[:, i * 512:(i + 1) * 512], pb[bi][:], 0.01,
                                         pb[bi][:], op0=ALU.mult, op1=ALU.max)
                vbank_free[bi] = ("VE", L.inc(L.v, "VE"))
        wbuf_free_pe[b % 2] = L.cnt["PE"]
        if USE_ACT_LRELU:
            L.wait(L.y, "SE")
        else:
            L.wait(L.y, "VE")
        vst_done = L.dma(L.y, "ST", o_v[b], v_st[:])


    if stop == "p2b":
        L.wait(L.y, "ST")
        return L
    L.wait(L.g, "PE")
    L.dma(L.g, "GD", slab_dst(wres_mh[:], NCH), mhT.rearrange("(c p) f -> p c f", p=128))
    ld_w = L.dma(L.g, "GD", slab_dst(wres_mg[:], NCH), mgT.rearrange("(c p) f -> p c f", p=128))

    bank_free2 = {4: ("SE", L.cnt["SE"]), 5: ("SE", L.cnt["SE"])}
    kf_ld = {}
    gelf = {}
    actf = {}
    g0_store = {}
    L.wait(L.t, "GD", ld_w)
    for tt in range(NTT):
        for e in range(16):
            L.wait(L.t, "VE")
            L.t.matmul(pu[:], sel16[:, e * 128:(e + 1) * 128],
                       cwT16[:, tt * TT:(tt + 1) * TT], start=True, stop=True)
            pe_bc = L.inc(L.t, "PE")
            L.wait(L.v, "PE", pe_bc)
            L.v.tensor_copy(cwbc[:, e * TT:(e + 1) * TT], pu[:])
            L.inc(L.v, "VE")
        ve_cwbc = L.cnt["VE"]
        L.wait(L.t, "VE", ve_cwbc)
        for h in range(8):
            it = tt * 8 + h
            bk, bv = kfb[it % 2], vfb[it % 2]
            if it >= 2:
                L.wait(L.g, "PE", kf_ld[it - 2])
            L.dma(L.g, "GD", bk[:], kfw[h])
            gd = L.dma(L.g, "GD", bv[:], vfw[h])
            sem, cntv = bank_free2.get(2, (None, 0))
            if sem is not None:
                L.wait(L.t, sem, cntv)
            for k in range(NCH):
                L.t.matmul(pb[2][:],
                           wres_mh[:, k * 1024 + h * 128: k * 1024 + (h + 1) * 128],
                           xnT_b[:, k * NTOK + tt * TT: k * NTOK + (tt + 1) * TT],
                           start=(k == 0), stop=(k == NCH - 1))
            pe_qmh = L.inc(L.t, "PE")
            qb = qmh_sb[it % 2]
            L.wait(L.s, "PE", pe_qmh)
            L.s.copy(qb[:], pb[2][:])
            se_qmh = L.inc(L.s, "SE")
            bank_free2[2] = ("SE", se_qmh)
            L.wait(L.t, "GD", gd)
            L.wait(L.t, "SE", se_qmh)
            ybank = pb[6]
            sem, cntv = bank_free2.get(6, (None, 0))
            if sem is not None:
                L.wait(L.t, sem, cntv)
            pe_s = {}
            for j in range(33):
                if j < 32:
                    sb_id = (0, 1, 3)[j % 3]
                    sbank = pb[sb_id]
                    sem, cntv = bank_free2.get(sb_id, (None, 0))
                    if sem is not None:
                        L.wait(L.t, sem, cntv)
                    L.t.matmul(sbank[:], bk[:, j * 128:(j + 1) * 128], qb[:],
                               start=True, stop=True)
                    pe_s[j] = L.inc(L.t, "PE")
                if j >= 1:
                    jy = j - 1
                    L.wait(L.t, "VE", gelf[jy % 3])
                    L.t.matmul(ybank[:], bv[:, jy * 128:(jy + 1) * 128], actb[jy % 3][:],
                               start=(jy == 0), stop=(jy == 31))
                    actf[jy % 3] = L.inc(L.t, "PE")
                if j < 32:
                    sb_id = (0, 1, 3)[j % 3]
                    sbank = pb[sb_id]
                    gbuf = gel[j % 3]
                    abuf = actb[j % 3]
                    L.wait(L.s, "PE", pe_s[j])
                    gf = gelf.get(j % 3)
                    if gf is not None:
                        L.wait(L.s, "VE", gf)
                    L.s.activation(gbuf[:], sbank[:], AF.Gelu)
                    se_gel = L.inc(L.s, "SE")
                    bank_free2[sb_id] = ("SE", se_gel)
                    af = actf.get(j % 3)
                    L.wait(L.v, "SE", se_gel)
                    if af is not None:
                        L.wait(L.v, "PE", af)
                    L.v.tensor_mul(abuf[:], gbuf[:], cwbc[:, (j // 2) * TT:(j // 2 + 1) * TT])
                    gelf[j % 3] = L.inc(L.v, "VE")
            kf_ld[it] = L.cnt["PE"]
            L.wait(L.s, "PE")
            L.s.copy(yTt[:, h * 512:(h + 1) * 512], ybank[:])
            bank_free2[6] = ("SE", L.inc(L.s, "SE"))
        L.wait(L.t, "SE")
        for fc in range(NCH):
            bi = 3 + (fc % 2)
            sem, cntv = bank_free2.get(bi, (None, 0))
            if sem is not None:
                L.wait(L.t, sem, cntv)
            for k in range(NCH):
                L.t.matmul(pb[bi][:],
                           wres_mg[:, k * 1024 + fc * 128: k * 1024 + (fc + 1) * 128],
                           yTt[:, k * 512:(k + 1) * 512],
                           start=(k == 0), stop=(k == NCH - 1))
            pe_mg = L.inc(L.t, "PE")
            stg = g0_st[(tt * NCH + fc) % 2]
            gprev = g0_store.get((tt * NCH + fc) % 2)
            if gprev is not None:
                L.wait(L.s, "ST", gprev)
            L.wait(L.s, "PE", pe_mg)
            L.s.activation(stg[:], pb[bi][:], AF.Silu)
            bank_free2[bi] = ("SE", L.inc(L.s, "SE"))
            L.wait(L.y, "SE")
            g0_store[(tt * NCH + fc) % 2] = L.dma(L.y, "ST", o_g0T[fc * 128:(fc + 1) * 128, tt * TT:(tt + 1) * TT], stg[:])

    L.wait(L.y, "ST")
    return L



def build_B0():
    L = Launch("B0")
    C = declare_consts(L)
    qTh = L.din("b0_q", [8, 64, 4096], BF16)
    kTh = L.din("b0_k", [8, 64, 4096], BF16)
    vh = L.din("b0_v", [8, 128, 32 * 512], BF16)
    o_aoT = L.dout("aoT", [8, 512, 4096], BF16)

    fcol = L.sb("fcol", [128, len(FC_VALS)], F32)
    o64 = L.sb("o64", [64, 1], BF16)
    o128 = L.sb("o128", [128, 1], BF16)
    o1f = L.sb("o1f", [1, 128], F32)
    qta = L.sb("qta", [128, 4096], BF16)
    kta = L.sb("kta", [128, 4096], BF16)
    vsb = L.sb("vsb", [128, 32 * 512], BF16)
    sqx = L.sb("sqx", [64, 4096], BF16)
    PT = L.sb("PT", [128, 32 * 512], BF16)
    nq = L.sb("nq", [1, 4096], F32)
    nk = L.sb("nk", [1, 4096], F32)
    mk1 = L.sb("mk1", [1, 1], F32)
    brow = L.sb("brow", [1, 4096], F32)
    rrow = L.sb("rrow", [1, 512], F32)
    rb_bc = L.sb("rb_bc", [128, 512], F32)
    onesb = L.sb("onesb", [128, 128], BF16)
    ao_st = [L.sb(f"ao_st{i}", [128, 512], BF16) for i in range(2)]

    pb = [L.ps(f"pb{i}", [128, 512], F32) for i in range(7)]
    pu = L.ps("pu", [128, 512], F32)

    L.dma(L.y, "LD", fcol[:], C["fcol"])
    L.dma(L.y, "LD", o64[:], C["ones64c"])
    L.dma(L.y, "LD", o128[:], C["ones128c"])
    L.dma(L.y, "LD", o1f[:], C["ones1f"])
    ld_c = L.cnt["LD"]
    L.wait(L.t, "LD", ld_c)
    L.v.memset(onesb[:], 1.0)
    L.v.memset(qta[64:66, :], 0.0)
    L.v.memset(kta[64:66, :], 0.0)
    L.inc(L.v, "VE")

    F8 = FC[0.125]
    sc18 = fcol[:, F8:F8 + 1]
    bank_free = {}
    ao_free = {}
    pe_unit_done = 0
    for u in range(8):
        L.wait(L.g, "PE", pe_unit_done)
        L.dma(L.g, "GD", qta[0:64, :], qTh[u])
        L.dma(L.g, "GD", kta[0:64, :], kTh[u])
        gd_v = L.dma(L.g, "GD", vsb[:], vh[u])
        L.wait(L.t, "GD", gd_v)
        for (src, dstrow) in ((qta, nq), (kta, nk)):
            L.wait(L.s, "GD", gd_v)
            L.wait(L.s, "VE")
            L.s.activation(sqx[:], src[0:64, :], AF.Square)
            se_sq = L.inc(L.s, "SE")
            L.wait(L.t, "SE", se_sq)
            for t8 in range(8):
                sem, cv = bank_free.get(7, (None, 0))
                if sem is not None:
                    L.wait(L.t, sem, cv)
                L.t.matmul(pu[0:1, 0:512], o64[:], sqx[:, t8 * 512:(t8 + 1) * 512],
                           start=True, stop=True)
                pe_n = L.inc(L.t, "PE")
                L.wait(L.s, "PE", pe_n)
                L.s.activation(dstrow[0:1, t8 * 512:(t8 + 1) * 512], pu[0:1, 0:512], AF.Sqrt)
                bank_free[7] = ("SE", L.inc(L.s, "SE"))
        se_rows = L.cnt["SE"]
        L.wait(L.v, "SE", se_rows)
        L.v.reduce_max(mk1[:], nk[:], axis=AX.X)
        L.selfsync(L.v, "VE")
        L.v.tensor_scalar(brow[:], nq[:], mk1[:], -1.0, op0=ALU.mult, op1=ALU.mult)
        L.v.tensor_copy(qta[64:65, :], brow[:])
        L.v.memset(kta[64:65, :], 1.0)
        ve_aug = L.inc(L.v, "VE")
        L.wait(L.t, "VE", ve_aug)
        for st in range(8):
            for kc in range(32):
                bi = kc % 4
                sem, cv = bank_free.get(bi, (None, 0))
                if sem is not None:
                    L.wait(L.t, sem, cv)
                L.t.matmul(pb[bi][:], kta[0:66, kc * 128:(kc + 1) * 128],
                           qta[0:66, st * 512:(st + 1) * 512], start=True, stop=True)
                pe_s = L.inc(L.t, "PE")
                L.wait(L.s, "PE", pe_s)
                L.s.activation(PT[:, kc * 512:(kc + 1) * 512], pb[bi][:], AF.Exp, scale=sc18)
                bank_free[bi] = ("SE", L.inc(L.s, "SE"))
            se_pt = L.cnt["SE"]
            sem, cv = bank_free.get(5, (None, 0))
            if sem is not None:
                L.wait(L.t, sem, cv)
            L.wait(L.t, "SE", se_pt)
            for kc in range(32):
                L.t.matmul(pb[5][:], onesb[:], PT[:, kc * 512:(kc + 1) * 512],
                           start=(kc == 0), stop=(kc == 31))
            pe_sum = L.inc(L.t, "PE")
            L.wait(L.s, "PE", pe_sum)
            L.wait(L.s, "VE")
            L.s.activation(rb_bc[:], pb[5][:], AF.Ln)
            L.s.activation(rb_bc[:], rb_bc[:], AF.Exp, scale=-1.0)
            se_bc = L.inc(L.s, "SE")
            bank_free[5] = ("SE", se_bc)
            for dvc in range(4):
                bi = 4
                sem, cv = bank_free.get(bi, (None, 0))
                if sem is not None:
                    L.wait(L.t, sem, cv)
                for kc in range(32):
                    L.t.matmul(pb[bi][:], vsb[:, kc * 512 + dvc * 128: kc * 512 + (dvc + 1) * 128],
                               PT[:, kc * 512:(kc + 1) * 512],
                               start=(kc == 0), stop=(kc == 31))
                pe_av = L.inc(L.t, "PE")
                stg = ao_st[dvc % 2]
                af = ao_free.get(dvc % 2)
                L.wait(L.v, "PE", pe_av)
                if dvc == 0:
                    L.wait(L.v, "SE", se_bc)
                if af is not None:
                    L.wait(L.v, "ST", af)
                L.v.tensor_mul(stg[:], pb[bi][:], rb_bc[:])
                bank_free[bi] = ("VE", L.inc(L.v, "VE"))
                L.wait(L.y, "VE")
                ao_free[dvc % 2] = L.dma(L.y, "ST",
                                         o_aoT[u, dvc * 128:(dvc + 1) * 128, st * 512:(st + 1) * 512],
                                         stg[:])
        pe_unit_done = L.cnt["PE"]
    L.wait(L.y, "ST")
    return L


def prep_B0(resA):
    ins = []
    for c in range(NCORES):
        b = c // 2
        h0 = (c % 2) * 8
        qs, ks, vs = [], [], []
        for j in range(8):
            h = h0 + j
            q_u = np.concatenate([resA[2 * b]["qT"][h * 64:(h + 1) * 64, :],
                                  resA[2 * b + 1]["qT"][h * 64:(h + 1) * 64, :]], axis=1)
            k_u = np.concatenate([resA[2 * b]["kT"][h * 64:(h + 1) * 64, :],
                                  resA[2 * b + 1]["kT"][h * 64:(h + 1) * 64, :]], axis=1)
            v0 = host_unpack_v_bf(resA[2 * b]["v_blk"])[:, h * 512:(h + 1) * 512]
            v1 = host_unpack_v_bf(resA[2 * b + 1]["v_blk"])[:, h * 512:(h + 1) * 512]
            v_u = np.concatenate([v0, v1], axis=0)
            v_u = v_u.reshape(32, 128, 512).transpose(1, 0, 2).reshape(128, 32 * 512)
            qs.append(q_u); ks.append(k_u); vs.append(v_u)
        ins.append({"b0_q": np.stack(qs), "b0_k": np.stack(ks), "b0_v": np.stack(vs)})
    return ins


_V_CACHE = {}


def host_unpack_v_bf(v_blk):
    key = id(v_blk)
    if key not in _V_CACHE:
        a = v_blk.reshape(16, 128, 16, 512)
        _V_CACHE[key] = np.ascontiguousarray(a.transpose(2, 1, 0, 3).reshape(NTOK, 8192))
    return _V_CACHE[key]


def unpack_B0(resB):
    outs = []
    for c in range(NCORES):
        b = c // 2
        half = c % 2
        aoT = np.empty((8192, NTOK), dtype=resB[0]["aoT"].dtype)
        for h in range(16):
            src_core = 2 * b + (h // 8)
            j = h % 8
            aoT[h * 512:(h + 1) * 512, :] = resB[src_core]["aoT"][j][:, half * NTOK:(half + 1) * NTOK]
        outs.append(aoT)
    return outs



def build_C0():
    L = Launch("C0")
    C = declare_consts(L)
    aoT = L.din("c0_aoT", [8192, NTOK], BF16)
    WoT = L.din("c0_WoT", [8192, D])
    bo_col = L.din("c0_bo_col", [128, 8])
    xnT = L.din("c0_xnT", [D, NTOK])
    x0T = L.din("c0_x0T", [D, NTOK])
    g0T = L.din("c0_g0T", [D, NTOK])
    o_x1T = L.dout("x1T", [D, NTOK])

    bocol = L.sb("bocol", [128, 8], F32)
    wblk = [L.sb(f"wblk{i}", [128, 8 * 1024], BF16) for i in range(2)]
    ablk = [L.sb(f"ablk{i}", [128, 8 * 512], BF16) for i in range(2)]
    xn_sl = L.sb("xn_sl", [128, 512], F32)
    x0_sl = L.sb("x0_sl", [128, 512], F32)
    g0_sl = L.sb("g0_sl", [128, 512], F32)
    pj = L.sb("pj", [128, 512], F32)
    x1_st = [L.sb(f"x1_st{i}", [128, 512], F32) for i in range(2)]
    pb = [L.ps(f"pb{i}", [128, 512], F32) for i in range(8)]

    L.dma(L.y, "LD", bocol[:], bo_col)
    ld_bo = L.cnt["LD"]
    x1_free = {}
    bank_evac = {}
    blk_used = {}
    for tt in range(NTT):
        for kb in range(8):
            it = tt * 8 + kb
            wb, ab = wblk[it % 2], ablk[it % 2]
            if it >= 2:
                L.wait(L.g, "PE", blk_used[it - 2])
            L.dma(L.g, "GD", slab_dst(wb[:], 8),
                  WoT[kb * 1024:(kb + 1) * 1024, :].rearrange("(c p) f -> p c f", p=128))
            gd = L.dma(L.g, "GD", slab_dst(ab[:], 8),
                       aoT[kb * 1024:(kb + 1) * 1024, tt * TT:(tt + 1) * TT]
                       .rearrange("(c p) t -> p c t", p=128))
            L.wait(L.t, "GD", gd)
            if kb == 0:
                for fc in range(8):
                    sem, cv = bank_evac.get(fc, (None, 0))
                    if sem is not None:
                        L.wait(L.t, sem, cv)
            for fc in range(8):
                for k in range(8):
                    L.t.matmul(pb[fc][:],
                               wb[:, k * 1024 + fc * 128: k * 1024 + (fc + 1) * 128],
                               ab[:, k * 512:(k + 1) * 512],
                               start=(kb == 0 and k == 0), stop=(kb == 7 and k == 7))
            blk_used[it] = L.inc(L.t, "PE")
        pe_tt = L.cnt["PE"]
        for fc in range(8):
            L.wait(L.y, "VE")
            ld1 = L.dma(L.y, "LD", xn_sl[:], xnT[fc * 128:(fc + 1) * 128, tt * TT:(tt + 1) * TT])
            ld2 = L.dma(L.y, "LD", x0_sl[:], x0T[fc * 128:(fc + 1) * 128, tt * TT:(tt + 1) * TT])
            ld3 = L.dma(L.y, "LD", g0_sl[:], g0T[fc * 128:(fc + 1) * 128, tt * TT:(tt + 1) * TT])
            L.wait(L.s, "LD", ld_bo)
            L.wait(L.s, "PE", pe_tt)
            L.wait(L.s, "VE")
            if USE_ACT_LRELU:
                L.s.activation(pj[:], pb[fc][:], AF.Lrelu, bias=bocol[:, fc:fc + 1], alpha=0.01)
                se_pj = L.inc(L.s, "SE")
                bank_evac[fc] = ("SE", se_pj)
                L.wait(L.v, "SE", se_pj)
            else:
                L.wait(L.v, "PE", pe_tt)
                L.v.tensor_scalar(pj[:], pb[fc][:], bocol[:, fc:fc + 1], None, op0=ALU.add)
                L.v.scalar_tensor_tensor(pj[:], pj[:], 0.01, pj[:], op0=ALU.mult, op1=ALU.max)
                bank_evac[fc] = ("VE", L.cnt["VE"] + 1)
            stg = x1_st[fc % 2]
            xf = x1_free.get(fc % 2)
            if xf is not None:
                L.wait(L.v, "ST", xf)
            L.wait(L.v, "LD", ld3)
            L.v.tensor_add(pj[:], pj[:], xn_sl[:])
            L.v.tensor_mul(pj[:], pj[:], g0_sl[:])
            L.v.tensor_add(stg[:], pj[:], x0_sl[:])
            ve_x1 = L.inc(L.v, "VE")
            L.wait(L.y, "VE", ve_x1)
            x1_free[fc % 2] = L.dma(L.y, "ST",
                                    o_x1T[fc * 128:(fc + 1) * 128, tt * TT:(tt + 1) * TT], stg[:])
    L.wait(L.y, "ST")
    return L


def prep_C0(inp):
    return {
        "c0_WoT": fmaj(inp["L0_attn_Wo"]),
        "c0_bo_col": np.ascontiguousarray(np.asarray(inp["L0_attn_bo"]).reshape(8, 128).T),
    }



def build_A1():
    L = Launch("A1")
    C = declare_consts(L)
    x1 = L.din("x1", [NTOK, D])
    Wq = L.din("a1_Wq", [D, D])
    Wf = L.din("a1_Wf", [D, D])
    Wi = L.din("a1_Wi", [D, D])
    gk = L.din("a1_gk", [D, 16])
    gbias = L.din("a1_gbias", [1, 16])
    wg = L.din("a1_wg", [16, D, 512])
    wu = L.din("a1_wu", [16, D, 512])
    wd = L.din("a1_wd", [16, 512, D])
    swg = L.din("a1_swg", [D, D])
    swu = L.din("a1_swu", [D, D])
    swd = L.din("a1_swd", [D, D])

    o_xnT_tiles = L.dout("xn1T_tiles", [NTILE, 128, D])
    o_qT = L.dout("glaqT", [D, NTOK], BF16)
    o_kT = L.dout("glakT", [D, NTOK], BF16)
    o_gT = L.dout("glagT", [D, NTOK])
    o_vT = L.dout("glavT", [D, NTOK], BF16)
    o_g1T = L.dout("g1T", [D, NTOK])

    eyef = L.sb("eyef", [128, 128], F32)
    fcol = L.sb("fcol", [128, len(FC_VALS)], F32)
    sel16 = L.sb("sel16", [16, 16 * 128], BF16)
    o1f = L.sb("o1f", [1, 128], F32)
    xnT_b = L.sb("xnT_b", [128, NCH * NTOK], BF16)
    wpool = [L.sb(f"wpool{i}", [128, 12 * 1024], BF16) for i in range(2)]
    y_acc = L.sb("y_acc", [128, 2 * 8 * 512], F32)
    gsb = L.sb("gsb", [128, 8 * 512], BF16)
    usb = L.sb("usb", [128, 8 * 512], BF16)
    isb = L.sb("isb", [128, 8 * 512], BF16)
    cwe = L.sb("cwe", [128, 512], BF16)
    pj = L.sb("pj", [128, 512], F32)
    pj2 = L.sb("pj2", [128, 512], F32)
    pj3 = L.sb("pj3", [128, 512], F32)
    stg_b = [L.sb(f"stg_b{i}", [128, 512], BF16) for i in range(2)]
    stg_f = [L.sb(f"stg_f{i}", [128, 512], F32) for i in range(2)]
    rbuf = L.sb("rbuf", [128, D], F32)
    gkf = L.sb("gkf", [128, NCH * 16], F32)
    gb128 = L.sb("gb128", [128, 16], F32)
    cw16 = L.sb("cw16", [128, 16], F32)
    cwt1 = L.sb("cwt1", [128, 16], F32)
    cwt2 = L.sb("cwt2", [128, 16], F32)
    cwt3 = L.sb("cwt3", [128, 16], F32)
    cwg = L.sb("cwg", [128, 4], F32)
    cwg2 = L.sb("cwg2", [128, 4], F32)
    cstat = L.sb("cstat", [128, 4], F32)
    cwT16 = L.sb("cwT16", [16, NTOK], BF16)

    pb = [L.ps(f"pb{i}", [128, 512], F32) for i in range(6)]
    pu = L.ps("pu", [128, 512], F32)

    L.dma(L.y, "LD", eyef[:], C["eyef"])
    L.dma(L.y, "LD", fcol[:], C["fcol"])
    L.dma(L.y, "LD", o1f[:], C["ones1f"])
    L.dma(L.g, "GD", sel16[:], C["sel16"])
    ve_ln, ln_st = emit_ln_fm(L, x1, o_xnT_tiles, xnT_b, eyef, pu[:, 0:128], fcol)

    QS = FC[0.08838834764831845]
    store_done = {}
    bank_free = {}

    def gla_proj(W, kind, wslot):
        wp = wpool[wslot]
        L.wait(L.g, "PE")
        gd = L.dma(L.g, "GD", slab_dst(wp[:, 0:NCH * 1024], NCH),
                   W.rearrange("(c p) f -> p c f", p=128))
        L.wait(L.t, "GD", gd)
        L.wait(L.t, "VE", ve_ln)
        for fc in range(NCH):
            for tt in range(NTT):
                bi = (fc * NTT + tt) % 2
                sem, cv = bank_free.get(bi, (None, 0))
                if sem is not None:
                    L.wait(L.t, sem, cv)
                for k in range(NCH):
                    L.t.matmul(pb[bi][:],
                               wp[:, k * 1024 + fc * 128: k * 1024 + (fc + 1) * 128],
                               xnT_b[:, k * NTOK + tt * TT: k * NTOK + (tt + 1) * TT],
                               start=(k == 0), stop=(k == NCH - 1))
                pe_d = L.inc(L.t, "PE")
                key = ("b", fc % 2)
                sd = store_done.get(key)
                if kind == "q":
                    L.wait(L.s, "PE", pe_d)
                    L.wait(L.s, "VE")
                    L.s.activation(pj[:], pb[bi][:], AF.Silu)
                    bank_free[bi] = ("SE", L.inc(L.s, "SE"))
                    stg = stg_b[fc % 2]
                    L.wait(L.v, "SE")
                    if sd is not None:
                        L.wait(L.v, "ST", sd)
                    L.v.tensor_scalar(stg[:], pj[:], fcol[:, QS:QS + 1], None, op0=ALU.mult)
                    ve_d = L.inc(L.v, "VE")
                    L.wait(L.y, "VE", ve_d)
                    store_done[key] = L.dma(L.y, "ST", o_qT[fc * 128:(fc + 1) * 128, tt * TT:(tt + 1) * TT], stg[:])
                elif kind == "i":
                    L.wait(L.s, "PE", pe_d)
                    stg = stg_b[fc % 2]
                    if sd is not None:
                        L.wait(L.s, "ST", sd)
                    L.s.copy(stg[:], pb[bi][:])
                    bank_free[bi] = ("SE", (se_d := L.inc(L.s, "SE")))
                    L.wait(L.y, "SE", se_d)
                    store_done[key] = L.dma(L.y, "ST", o_vT[fc * 128:(fc + 1) * 128, tt * TT:(tt + 1) * TT], stg[:])
                else:
                    L.wait(L.s, "PE", pe_d)
                    stg = stg_b[fc % 2]
                    sdk = store_done.get(("b", fc % 2))
                    if sdk is not None:
                        L.wait(L.s, "ST", sdk)
                    L.s.activation(stg[:], pb[bi][:], AF.Sigmoid, scale=-1.0)
                    se_k = L.inc(L.s, "SE")
                    L.wait(L.y, "SE", se_k)
                    store_done[("b", fc % 2)] = L.dma(L.y, "ST", o_kT[fc * 128:(fc + 1) * 128, tt * TT:(tt + 1) * TT], stg[:])
                    L.wait(L.s, "VE")
                    L.s.copy(pj[:], pb[bi][:])
                    bank_free[bi] = ("SE", (se_f := L.inc(L.s, "SE")))
                    L.wait(L.s, "VE")
                    L.s.activation(pj2[:], pj[:], AF.Abs)
                    L.s.activation(pj2[:], pj2[:], AF.Exp, scale=-1.0)
                    L.s.activation(pj2[:], pj2[:], AF.Ln, bias=1.0)
                    se_l = L.inc(L.s, "SE")
                    stgf = stg_f[fc % 2]
                    sdg = store_done.get(("f", fc % 2))
                    L.wait(L.v, "SE", se_l)
                    if sdg is not None:
                        L.wait(L.v, "ST", sdg)
                    L.v.tensor_scalar_min(pj3[:], pj[:], 0.0)
                    L.v.tensor_sub(stgf[:], pj3[:], pj2[:])
                    ve_g = L.inc(L.v, "VE")
                    L.wait(L.y, "VE", ve_g)
                    store_done[("f", fc % 2)] = L.dma(L.y, "ST", o_gT[fc * 128:(fc + 1) * 128, tt * TT:(tt + 1) * TT], stgf[:])

    gla_proj(Wq, "q", 0)
    gla_proj(Wf, "f", 1)
    gla_proj(Wi, "i", 0)
    pe_after_gla = L.cnt["PE"]

    L.wait(L.y, "PE")
    ld_gk = L.dma(L.y, "LD", slab_dst(gkf[:], NCH), gk.rearrange("(c p) e -> p c e", p=128))
    gbrow = L.sb("gbrow", [1, 16], F32)
    ld_gb = L.dma(L.y, "LD", gbrow[:], gbias)
    L.wait(L.t, "LD", ld_gb)
    L.wait(L.t, "VE")
    L.t.matmul(pu[:, 0:16], o1f[:], gbrow[:], start=True, stop=True)
    pe_gb = L.inc(L.t, "PE")
    L.wait(L.v, "PE", pe_gb)
    L.v.tensor_copy(gb128[:], pu[:, 0:16])
    ve_gb = L.inc(L.v, "VE")
    pu_free_t = ve_gb
    rb_free = 0
    for i in range(NTILE):
        L.wait(L.y, "ST", ln_st[i])
        if rb_free:
            L.wait(L.y, "PE", rb_free)
        ld_rb = L.dma(L.y, "LD", rbuf[:], o_xnT_tiles[i])
        L.wait(L.t, "LD", max(ld_gk, ld_rb))
        L.wait(L.t, "VE", pu_free_t)
        for c in range(NCH):
            L.t.matmul(pu[:, 0:16],
                       rbuf[:, c * 128:(c + 1) * 128],
                       gkf[:, c * 16:(c + 1) * 16],
                       start=(c == 0), stop=(c == NCH - 1))
        pe_r = L.inc(L.t, "PE")
        rb_free = pe_r
        L.wait(L.s, "PE", pe_r)
        L.wait(L.s, "VE")
        L.s.activation(cw16[:], pu[:, 0:16], AF.Sigmoid)
        se_sc = L.inc(L.s, "SE")
        L.wait(L.v, "SE", se_sc)
        L.v.tensor_add(cwt1[:], cw16[:], gb128[:])
        m3 = cwt1[:].rearrange("p (g e) -> p g e", g=4)
        L.v.tensor_reduce(cwg[:], m3, axis=AX.X, op=ALU.max)
        bv = cwg[:].unsqueeze(2).broadcast_to([128, 4, 4])
        t3 = cwt2[:].rearrange("p (g e) -> p g e", g=4)
        L.v.tensor_tensor(t3, m3, bv, op=ALU.is_equal)
        L.v.scalar_tensor_tensor(cwt2[:], cwt2[:], -1e30, cwt1[:], op0=ALU.mult, op1=ALU.add)
        t3b = cwt2[:].rearrange("p (g e) -> p g e", g=4)
        L.v.tensor_reduce(cwg2[:], t3b, axis=AX.X, op=ALU.max)
        L.v.tensor_add(cwg[:], cwg[:], cwg2[:])
        L.v.reduce_max(cstat[:, 0:1], cwg[:], axis=AX.X)
        L.v.tensor_scalar(cwg2[:], cwg[:], cstat[:, 0:1], None, op0=ALU.is_equal)
        L.v.scalar_tensor_tensor(cwg2[:], cwg2[:], -1e30, cwg[:], op0=ALU.mult, op1=ALU.add)
        L.v.reduce_max(cstat[:, 1:2], cwg2[:], axis=AX.X)
        L.v.tensor_scalar(cwg2[:], cwg[:], cstat[:, 1:2], None, op0=ALU.is_lt)
        L.v.tensor_scalar(cwg2[:], cwg2[:], -1.0, 1.0, op0=ALU.mult, op1=ALU.add)
        bg = cwg2[:].unsqueeze(2).broadcast_to([128, 4, 4])
        t3c = cwt3[:].rearrange("p (g e) -> p g e", g=4)
        L.v.tensor_copy(t3c, bg)
        L.v.tensor_mul(cwt3[:], cwt3[:], cwt1[:])
        L.v.tensor_copy(cwt2[:], cwt3[:])
        for r in range(4):
            L.v.reduce_max(cstat[:, 2:3], cwt2[:], axis=AX.X)
            if r < 3:
                L.v.tensor_scalar(cw16[:], cwt2[:], cstat[:, 2:3], None, op0=ALU.is_equal)
                L.v.scalar_tensor_tensor(cwt2[:], cw16[:], -1e30, cwt2[:], op0=ALU.mult, op1=ALU.add)
        L.v.tensor_scalar(cw16[:], cwt3[:], cstat[:, 2:3], None, op0=ALU.is_lt)
        L.v.tensor_scalar(cw16[:], cw16[:], -1.0, 1.0, op0=ALU.mult, op1=ALU.add)
        L.v.tensor_mul(cw16[:], cw16[:], cwt3[:])
        L.v.tensor_scalar(cwt2[:], cwt3[:], 0.0, None, op0=ALU.is_le)
        L.v.tensor_scalar(cwt2[:], cwt2[:], -1.0, 1.0, op0=ALU.mult, op1=ALU.add)
        L.v.tensor_mul(cw16[:], cw16[:], cwt2[:])
        L.v.reduce_sum(cstat[:, 3:4], cw16[:], axis=AX.X)
        L.v.tensor_scalar(cstat[:, 3:4], cstat[:, 3:4], fcol[0:128, FC[1e-20]:FC[1e-20] + 1], None, op0=ALU.add)
        ve_den = L.inc(L.v, "VE")
        L.wait(L.v, "VE", ve_den)
        L.v.reciprocal(cstat[:, 3:4], cstat[:, 3:4])
        L.selfsync(L.v, "VE")
        L.v.tensor_scalar(cw16[:], cw16[:], cstat[:, 3:4], fcol[0:128, FC[2.5]:FC[2.5] + 1],
                          op0=ALU.mult, op1=ALU.mult)
        ve_cw = L.inc(L.v, "VE")
        L.wait(L.t, "VE", ve_cw)
        L.t.transpose(pu[0:16, 128:256], cw16[:], eyef[:])
        pe_t = L.inc(L.t, "PE")
        L.wait(L.v, "PE", pe_t)
        L.v.tensor_copy(cwT16[:, i * 128:(i + 1) * 128], pu[0:16, 128:256])
        pu_free_t = L.inc(L.v, "VE")

    gd_we = {-2: pe_after_gla, -1: pe_after_gla}
    for half in range(2):
        tts = (2 * half, 2 * half + 1)
        L.wait(L.v, "PE")
        L.wait(L.v, "SE")
        L.v.memset(y_acc[:], 0.0)
        ve_ya = L.inc(L.v, "VE")
        for e in range(16):
            it = half * 16 + e
            wp = wpool[it % 2]
            L.wait(L.g, "PE", gd_we[it - 2])
            L.dma(L.g, "GD", slab_dst(wp[:, 0:8 * 512], NCH), wg[e].rearrange("(c p) f -> p c f", p=128))
            L.dma(L.g, "GD", slab_dst(wp[:, 8 * 512:16 * 512], NCH), wu[e].rearrange("(c p) f -> p c f", p=128))
            gd = L.dma(L.g, "GD", slab_dst(wp[:, 16 * 512:16 * 512 + 4 * 1024], 4),
                       wd[e].rearrange("(c p) f -> p c f", p=128))
            for tt in tts:
                ts = (tt % 2) * 8
                L.wait(L.t, "VE")
                L.t.matmul(pu[:], sel16[:, e * 128:(e + 1) * 128],
                           cwT16[:, tt * TT:(tt + 1) * TT], start=True, stop=True)
                pe_ce = L.inc(L.t, "PE")
                L.wait(L.v, "PE", pe_ce)
                L.v.tensor_copy(cwe[:], pu[:])
                ve_ce = L.inc(L.v, "VE")
                if tt == tts[0]:
                    L.wait(L.t, "GD", gd)
                for (w0, dst, actk) in ((0, gsb, "silu"), (8 * 512, usb, "copy")):
                    for oc in range(4):
                        bi = oc
                        sem, cv = bank_free.get(bi, (None, 0))
                        if sem is not None:
                            L.wait(L.t, sem, cv)
                        for k in range(NCH):
                            L.t.matmul(pb[bi][:],
                                       wp[:, w0 + k * 512 + oc * 128: w0 + k * 512 + (oc + 1) * 128],
                                       xnT_b[:, k * NTOK + tt * TT: k * NTOK + (tt + 1) * TT],
                                       start=(k == 0), stop=(k == NCH - 1))
                        pe_d = L.inc(L.t, "PE")
                        L.wait(L.s, "PE", pe_d)
                        L.wait(L.s, "VE")
                        if actk == "silu":
                            L.s.activation(dst[:, oc * 512:(oc + 1) * 512], pb[bi][:], AF.Silu)
                        else:
                            L.s.copy(dst[:, oc * 512:(oc + 1) * 512], pb[bi][:])
                        bank_free[bi] = ("SE", L.inc(L.s, "SE"))
                se_gu = L.cnt["SE"]
                L.wait(L.v, "SE", se_gu)
                L.wait(L.v, "VE", ve_ce)
                for oc in range(4):
                    L.v.tensor_mul(isb[:, oc * 512:(oc + 1) * 512], gsb[:, oc * 512:(oc + 1) * 512],
                                   usb[:, oc * 512:(oc + 1) * 512])
                    L.v.tensor_mul(isb[:, oc * 512:(oc + 1) * 512], isb[:, oc * 512:(oc + 1) * 512], cwe[:])
                ve_i = L.inc(L.v, "VE")
                if e == 0 and tt == tts[0]:
                    L.wait(L.t, "VE", max(ve_i, ve_ya))
                else:
                    L.wait(L.t, "VE", ve_i)
                for oc in range(8):
                    bi = 4 + (oc % 2)
                    sem, cv = bank_free.get(bi, (None, 0))
                    if sem is not None:
                        L.wait(L.t, sem, cv)
                    for k in range(4):
                        L.t.matmul(pb[bi][:],
                                   wp[:, 16 * 512 + k * 1024 + oc * 128: 16 * 512 + k * 1024 + (oc + 1) * 128],
                                   isb[:, k * 512:(k + 1) * 512],
                                   start=(k == 0), stop=(k == 3))
                    pe_y = L.inc(L.t, "PE")
                    L.wait(L.v, "PE", pe_y)
                    L.v.tensor_add(y_acc[:, (ts + oc) * 512:(ts + oc + 1) * 512],
                                   y_acc[:, (ts + oc) * 512:(ts + oc + 1) * 512], pb[bi][:])
                    bank_free[bi] = ("VE", L.inc(L.v, "VE"))
            gd_we[it] = L.cnt["PE"]

        wp0, wp1 = wpool[0], wpool[1]
        last_it = half * 16 + 15
        L.wait(L.g, "PE", gd_we[last_it])
        gd_sg = L.dma(L.g, "GD", slab_dst(wp0[:, 0:8 * 1024], NCH), swg.rearrange("(c p) f -> p c f", p=128))
        gd_su = L.dma(L.g, "GD", slab_dst(wp1[:, 0:8 * 1024], NCH), swu.rearrange("(c p) f -> p c f", p=128))
        for tt in tts:
            for (wpx, gdx, dst, actk) in ((wp0, gd_sg, gsb, "silu"), (wp1, gd_su, usb, "copy")):
                L.wait(L.t, "GD", gdx)
                for oc in range(8):
                    bi = oc % 4
                    sem, cv = bank_free.get(bi, (None, 0))
                    if sem is not None:
                        L.wait(L.t, sem, cv)
                    for k in range(NCH):
                        L.t.matmul(pb[bi][:],
                                   wpx[:, k * 1024 + oc * 128: k * 1024 + (oc + 1) * 128],
                                   xnT_b[:, k * NTOK + tt * TT: k * NTOK + (tt + 1) * TT],
                                   start=(k == 0), stop=(k == NCH - 1))
                    pe_d = L.inc(L.t, "PE")
                    L.wait(L.s, "PE", pe_d)
                    L.wait(L.s, "VE")
                    if actk == "silu":
                        L.s.activation(dst[:, oc * 512:(oc + 1) * 512], pb[bi][:], AF.Silu)
                    else:
                        L.s.copy(dst[:, oc * 512:(oc + 1) * 512], pb[bi][:])
                    bank_free[bi] = ("SE", L.inc(L.s, "SE"))
            se_s = L.cnt["SE"]
            L.wait(L.v, "SE", se_s)
            idst = isb if tt == tts[0] else gsb
            for oc in range(8):
                L.v.tensor_mul(idst[:, oc * 512:(oc + 1) * 512], gsb[:, oc * 512:(oc + 1) * 512],
                               usb[:, oc * 512:(oc + 1) * 512])
            L.inc(L.v, "VE")
        ve_si = L.cnt["VE"]
        pe_su = L.cnt["PE"]
        L.wait(L.g, "PE", pe_su)
        gd_sd = L.dma(L.g, "GD", slab_dst(wp0[:, 0:8 * 1024], NCH), swd.rearrange("(c p) f -> p c f", p=128))
        L.wait(L.t, "GD", gd_sd)
        L.wait(L.t, "VE", ve_si)
        for tt in tts:
            src = isb if tt == tts[0] else gsb
            ts = (tt % 2) * 8
            for oc in range(8):
                bi = 4 + (oc % 2)
                sem, cv = bank_free.get(bi, (None, 0))
                if sem is not None:
                    L.wait(L.t, sem, cv)
                for k in range(NCH):
                    L.t.matmul(pb[bi][:],
                               wp0[:, k * 1024 + oc * 128: k * 1024 + (oc + 1) * 128],
                               src[:, k * 512:(k + 1) * 512],
                               start=(k == 0), stop=(k == NCH - 1))
                pe_y = L.inc(L.t, "PE")
                L.wait(L.v, "PE", pe_y)
                L.v.tensor_add(y_acc[:, (ts + oc) * 512:(ts + oc + 1) * 512],
                               y_acc[:, (ts + oc) * 512:(ts + oc + 1) * 512], pb[bi][:])
                bank_free[bi] = ("VE", L.inc(L.v, "VE"))
        gd_we[(half + 1) * 16 - 1] = L.cnt["PE"]
        gd_we[(half + 1) * 16 - 2] = L.cnt["PE"]
        ve_yfin = L.cnt["VE"]
        for tt in tts:
            ts = (tt % 2) * 8
            for oc in range(8):
                stg = stg_f[oc % 2]
                sd = store_done.get(("f", oc % 2))
                L.wait(L.s, "VE", ve_yfin)
                if sd is not None:
                    L.wait(L.s, "ST", sd)
                L.s.activation(stg[:], y_acc[:, (ts + oc) * 512:(ts + oc + 1) * 512], AF.Silu)
                se_g1 = L.inc(L.s, "SE")
                L.wait(L.y, "SE", se_g1)
                store_done[("f", oc % 2)] = L.dma(L.y, "ST",
                    o_g1T[oc * 128:(oc + 1) * 128, tt * TT:(tt + 1) * TT], stg[:])
    L.wait(L.y, "ST")
    return L


def prep_A1_weights(inp):
    return {
        "a1_Wq": np.ascontiguousarray(np.asarray(inp["L1_attn_Wq"], np.float32)),
        "a1_Wf": np.ascontiguousarray(np.asarray(inp["L1_attn_Wf"], np.float32)),
        "a1_Wi": np.ascontiguousarray(np.asarray(inp["L1_attn_Wi"], np.float32)),
        "a1_gk": np.ascontiguousarray(np.asarray(inp["L1_ffn_gate_kernel"], np.float32)),
        "a1_gbias": np.asarray(inp["L1_ffn_gate_bias"], np.float32).reshape(1, 16),
        "a1_wg": np.ascontiguousarray(np.asarray(inp["L1_ffn_w_gate"], np.float32)),
        "a1_wu": np.ascontiguousarray(np.asarray(inp["L1_ffn_w_up"], np.float32)),
        "a1_wd": np.ascontiguousarray(np.asarray(inp["L1_ffn_w_down"], np.float32)),
        "a1_swg": np.ascontiguousarray(np.asarray(inp["L1_ffn_sw_gate"], np.float32)),
        "a1_swu": np.ascontiguousarray(np.asarray(inp["L1_ffn_sw_up"], np.float32)),
        "a1_swd": np.ascontiguousarray(np.asarray(inp["L1_ffn_sw_down"], np.float32)),
    }



def build_B1():
    L = Launch("B1")
    C = declare_consts(L)
    qTi = L.din("b1_qT", [4, 128, 4096], BF16)
    kTi = L.din("b1_kT", [4, 128, 4096], BF16)
    gTi = L.din("b1_gT", [4, 128, 4096])
    vti = L.din("b1_vt", [4, 64, 64 * 128], BF16)
    o_oT = L.dout("oT", [4, 128, 4096])

    eyeb = L.sb("eyeb", [128, 128], BF16)
    utri = L.sb("utri", [32, 32], BF16)
    qT = L.sb("qT", [128, 4096], BF16)
    kT = L.sb("kT", [128, 4096], BF16)
    gT = L.sb("gT", [128, 4096], F32)
    GT = L.sb("GT", [128, 4096], F32)
    E1 = L.sb("E1", [128, 4096], F32)
    etmp = L.sb("etmp", [128, 4096], F32)
    maskc = L.sb("maskc", [128, 4096], F32)
    qe = L.sb("qe", [128, 4096], BF16)
    ke = L.sb("ke", [128, 4096], BF16)
    khT = L.sb("khT", [128, 4096], BF16)
    khtok = L.sb("khtok", [64, 64 * 128], BF16)
    a_sb = L.sb("a_sb", [64, 64 * 32], BF16)
    vt = L.sb("vt", [64, 64 * 128], BF16)
    oT = L.sb("oT_sb", [128, 4096], F32)
    Lc = L.sb("Lc", [128, 128], F32)
    S_f = L.sb("S_f", [128, 128], F32)
    S_b = L.sb("S_b", [128, 128], BF16)

    pO2 = [L.ps(f"pO{i}", [128, 512], F32) for i in range(2)]
    pU = L.ps("pU", [128, 512], F32)
    pS = L.ps("pS", [128, 512], F32)
    pSb = L.ps("pSb", [128, 256], BF16)

    L.dma(L.y, "LD", eyeb[:], C["eyeb"])
    L.dma(L.y, "LD", utri[:], C["utri32"])
    L.wait(L.v, "LD")
    L.v.memset(maskc[:], 1.0)
    mview = maskc[:].rearrange("p (c w) -> p c w", w=32)[:, :, 0:1]
    L.v.memset(mview, 0.0)
    ve_mask = L.inc(L.v, "VE")

    ost_done = 0
    pe_unit = 0
    for u in range(4):
        L.wait(L.g, "PE", pe_unit)
        L.wait(L.g, "VE")
        L.wait(L.g, "SE")
        L.dma(L.g, "GD", qT[:], qTi[u])
        L.dma(L.g, "GD", kT[:], kTi[u])
        L.dma(L.g, "GD", vt[:], vti[u])
        gd_a = L.dma(L.g, "GD", gT[:], gTi[u])
        L.wait(L.v, "GD", gd_a)
        L.wait(L.v, "VE", ve_mask)
        L.wait(L.v, "SE")
        L.v.tensor_tensor_scan(GT[:], maskc[:], gT[:], 0.0, op0=ALU.mult, op1=ALU.add)
        ve_gt = L.inc(L.v, "VE")
        L.wait(L.s, "VE", ve_gt)
        L.s.activation(E1[:], GT[:], AF.Exp)
        L.s.activation(etmp[:], GT[:], AF.Exp, scale=-1.0)
        se_e = L.inc(L.s, "SE")
        L.wait(L.v, "SE", se_e)
        L.v.tensor_mul(qe[:], qT[:], E1[:])
        L.v.tensor_mul(ke[:], kT[:], etmp[:])
        gcb = GT[:].rearrange("p (c w) -> p c w", w=32)[:, :, 31:32].broadcast_to([128, 128, 32])
        gh3 = etmp[:].rearrange("p (c w) -> p c w", w=32)
        gt3 = GT[:].rearrange("p (c w) -> p c w", w=32)
        L.v.tensor_tensor(gh3, gt3, gcb, op=ALU.subtract)
        L.v.tensor_copy(Lc[:], E1[:].rearrange("p (c w) -> p c w", w=32)[:, :, 31:32].squeeze(2))
        ve_gh = L.inc(L.v, "VE")
        L.wait(L.s, "VE", ve_gh)
        L.s.activation(etmp[:], etmp[:], AF.Exp, scale=-1.0)
        se_e3 = L.inc(L.s, "SE")
        L.wait(L.v, "SE", se_e3)
        L.v.tensor_mul(khT[:], kT[:], etmp[:])
        ve_a4 = L.inc(L.v, "VE")
        def khs(c):
            return khtok[(c % 2) * 32:(c % 2) * 32 + 32, (c // 2) * 128:(c // 2) * 128 + 128]

        def vts(c):
            return vt[(c % 2) * 32:(c % 2) * 32 + 32, (c // 2) * 128:(c // 2) * 128 + 128]

        def ass(c):
            return a_sb[(c % 2) * 32:(c % 2) * 32 + 32, (c // 2) * 32:(c // 2) * 32 + 32]

        L.wait(L.t, "VE", ve_a4)
        bank_free = {}
        for c in range(128):
            sem, cv = bank_free.get("pS", (None, 0))
            if sem is not None:
                L.wait(L.t, sem, cv)
            L.t.transpose(pSb[0:32, 0:128], khT[:, c * 32:(c + 1) * 32], eyeb[:])
            pe_t = L.inc(L.t, "PE")
            L.wait(L.v, "PE", pe_t)
            L.v.tensor_copy(khs(c), pSb[0:32, 0:128])
            ve_t = L.inc(L.v, "VE")
            L.wait(L.t, "VE", ve_t)
            L.t.matmul(pS[0:32, 128:160], ke[:, c * 32:(c + 1) * 32], qe[:, c * 32:(c + 1) * 32],
                       start=True, stop=True)
            pe_a = L.inc(L.t, "PE")
            L.wait(L.v, "PE", pe_a)
            L.v.tensor_mul(ass(c), pS[0:32, 128:160], utri[:])
            bank_free["pS"] = ("VE", L.inc(L.v, "VE"))
        ve_a2 = L.cnt["VE"]
        L.wait(L.v, "PE")
        L.wait(L.v, "ST", ost_done)
        L.v.memset(S_f[:], 0.0)
        L.v.memset(S_b[:], 0.0)
        ve_s0 = L.inc(L.v, "VE")
        L.wait(L.t, "VE", max(ve_a2, ve_s0))
        pOfree = {}
        pUfree = 0
        for c in range(128):
            sl = pOfree.get(c % 2)
            if sl is not None:
                L.wait(L.t, "VE", sl)
            pov = pO2[c % 2][:, 0:32]
            L.t.matmul(pov, S_b[:], qe[:, c * 32:(c + 1) * 32], start=True, stop=False)
            L.t.matmul(pov, vts(c), ass(c), start=False, stop=True)
            pe_o = L.inc(L.t, "PE")
            if pUfree:
                L.wait(L.t, "VE", pUfree)
            L.t.matmul(pU[:, 0:128], khs(c), vts(c), start=True, stop=True)
            pe_u = L.inc(L.t, "PE")
            L.wait(L.v, "PE", pe_o)
            L.v.tensor_copy(oT[:, c * 32:(c + 1) * 32], pov)
            pOfree[c % 2] = L.inc(L.v, "VE")
            L.wait(L.v, "PE", pe_u)
            L.v.scalar_tensor_tensor(S_f[:], S_f[:], Lc[:, c:c + 1], pU[:, 0:128], op0=ALU.mult, op1=ALU.add)
            L.v.tensor_copy(S_b[:], S_f[:])
            pUfree = L.inc(L.v, "VE")
            L.wait(L.t, "VE", pUfree)
        ve_fin = L.cnt["VE"]
        L.wait(L.y, "VE", ve_fin)
        L.wait(L.y, "ST", ost_done)
        ost_done = L.dma(L.y, "ST", o_oT[u], oT[:])
        pe_unit = L.cnt["PE"]
    L.wait(L.y, "ST")
    return L


def prep_B1(resA1):
    ins = []
    for c in range(NCORES):
        b = c // 2
        qs, ks, gs, vs = [], [], [], []
        for j in range(4):
            h = (c % 2) * 4 + j
            rows = slice(h * 128, (h + 1) * 128)
            q_u = np.concatenate([resA1[2 * b]["glaqT"][rows], resA1[2 * b + 1]["glaqT"][rows]], axis=1)
            k_u = np.concatenate([resA1[2 * b]["glakT"][rows], resA1[2 * b + 1]["glakT"][rows]], axis=1)
            g_u = np.concatenate([resA1[2 * b]["glagT"][rows], resA1[2 * b + 1]["glagT"][rows]], axis=1)
            v_u = np.concatenate([resA1[2 * b]["glavT"][rows], resA1[2 * b + 1]["glavT"][rows]], axis=1)
            vtok = np.ascontiguousarray(v_u.T)
            vt = vtok.reshape(64, 2, 32, 128).transpose(1, 2, 0, 3).reshape(64, 64 * 128)
            qs.append(q_u); ks.append(k_u); gs.append(g_u); vs.append(np.ascontiguousarray(vt))
        ins.append({"b1_qT": np.stack(qs), "b1_kT": np.stack(ks),
                    "b1_gT": np.stack(gs), "b1_vt": np.stack(vs)})
    return ins


def unpack_B1(resB1):
    outs = []
    for c in range(NCORES):
        b = c // 2
        half = c % 2
        oT = np.empty((D, NTOK), np.float32)
        for h in range(8):
            src_core = 2 * b + (h // 4)
            j = h % 4
            oT[h * 128:(h + 1) * 128, :] = resB1[src_core]["oT"][j][:, half * NTOK:(half + 1) * NTOK]
        outs.append(oT)
    return outs



def build_C1():
    L = Launch("C1")
    C = declare_consts(L)
    oT = L.din("c1_oT", [D, NTOK])
    Wo = L.din("c1_Wo", [D, D])
    gw_col = L.din("c1_gw_col", [128, 8])
    x1T = L.din("c1_x1T", [D, NTOK])
    g1T = L.din("c1_g1T", [D, NTOK])
    o_x2T = L.dout("x2T", [D, NTOK])

    fcol = L.sb("fcol", [128, len(FC_VALS)], F32)
    o128 = L.sb("o128", [128, 1], BF16)
    o1f = L.sb("o1f", [1, 128], F32)
    gwc = L.sb("gwc", [128, 8], F32)
    wres = L.sb("wres", [128, NCH * 1024], BF16)
    oTb = L.sb("oTb", [128, NCH * 512], F32)
    sqb = L.sb("sqb", [128, NCH * 512], BF16)
    onT = L.sb("onT", [128, NCH * 512], BF16)
    rstd = L.sb("rstd", [1, 512], F32)
    rb = L.sb("rb", [128, 512], F32)
    x1_sl = L.sb("x1_sl", [128, 512], F32)
    g1_sl = L.sb("g1_sl", [128, 512], F32)
    tmp = L.sb("tmp", [128, 512], F32)
    x2_st = [L.sb(f"x2_st{i}", [128, 512], F32) for i in range(2)]

    pb = [L.ps(f"pb{i}", [128, 512], F32) for i in range(4)]
    pu = L.ps("pu", [128, 512], F32)

    L.dma(L.y, "LD", fcol[:], C["fcol"])
    L.dma(L.y, "LD", o1f[:], C["ones1f"])
    L.dma(L.y, "LD", gwc[:], gw_col)
    L.dma(L.g, "GD", o128[:], C["ones128c"])
    gd_w = L.dma(L.g, "GD", slab_dst(wres[:], NCH), Wo.rearrange("(c p) f -> p c f", p=128))
    ld_c = L.cnt["LD"]
    x2_free = {}
    bank_free = {}
    for tt in range(NTT):
        L.wait(L.y, "SE")
        L.wait(L.y, "VE")
        ld_o = L.dma(L.y, "LD", oTb[:].rearrange("p (c t) -> p c t", c=NCH),
                     oT[:, tt * TT:(tt + 1) * TT].rearrange("(c p) t -> p c t", p=128))
        L.wait(L.s, "LD", max(ld_o, ld_c))
        L.wait(L.s, "VE")
        L.s.activation(sqb[:], oTb[:], AF.Square)
        se_sq = L.inc(L.s, "SE")
        sem, cv = bank_free.get("u", (None, 0))
        if sem is not None:
            L.wait(L.t, sem, cv)
        L.wait(L.t, "SE", se_sq)
        L.wait(L.t, "GD")
        for k in range(NCH):
            L.t.matmul(pu[0:1, 0:512], o128[:], sqb[:, k * 512:(k + 1) * 512],
                       start=(k == 0), stop=(k == NCH - 1))
        pe_ss = L.inc(L.t, "PE")
        L.wait(L.s, "PE", pe_ss)
        L.s.mul(rstd[:], pu[0:1, 0:512], 1.0 / D)
        L.s.activation(rstd[:], rstd[:], AF.Ln, bias=fcol[0:1, FC[1e-5]:FC[1e-5] + 1])
        L.s.activation(rstd[:], rstd[:], AF.Exp, scale=fcol[0:1, FC[-0.5]:FC[-0.5] + 1])
        bank_free["u"] = ("SE", (se_r := L.inc(L.s, "SE")))
        sem, cv = bank_free.get("b", (None, 0))
        if sem is not None:
            L.wait(L.t, sem, cv)
        L.wait(L.t, "SE", se_r)
        L.t.matmul(pb[3][:], o1f[:], rstd[:], start=True, stop=True)
        pe_bc = L.inc(L.t, "PE")
        L.wait(L.v, "PE", pe_bc)
        L.v.tensor_copy(rb[:], pb[3][:])
        bank_free["b"] = ("VE", L.inc(L.v, "VE"))
        for k in range(NCH):
            L.v.tensor_mul(tmp[:], oTb[:, k * 512:(k + 1) * 512], rb[:])
            L.v.tensor_scalar(onT[:, k * 512:(k + 1) * 512], tmp[:], gwc[:, k:k + 1], None, op0=ALU.mult)
        ve_on = L.inc(L.v, "VE")
        L.wait(L.t, "VE", ve_on)
        for fc in range(NCH):
            bi = fc % 2
            sem, cv = bank_free.get(bi, (None, 0))
            if sem is not None:
                L.wait(L.t, sem, cv)
            for k in range(NCH):
                L.t.matmul(pb[bi][:],
                           wres[:, k * 1024 + fc * 128: k * 1024 + (fc + 1) * 128],
                           onT[:, k * 512:(k + 1) * 512],
                           start=(k == 0), stop=(k == NCH - 1))
            pe_p = L.inc(L.t, "PE")
            L.wait(L.y, "VE")
            ld1 = L.dma(L.y, "LD", x1_sl[:], x1T[fc * 128:(fc + 1) * 128, tt * TT:(tt + 1) * TT])
            ld2 = L.dma(L.y, "LD", g1_sl[:], g1T[fc * 128:(fc + 1) * 128, tt * TT:(tt + 1) * TT])
            stg = x2_st[fc % 2]
            xf = x2_free.get(fc % 2)
            L.wait(L.v, "PE", pe_p)
            L.wait(L.v, "LD", ld2)
            if xf is not None:
                L.wait(L.v, "ST", xf)
            L.v.tensor_mul(tmp[:], pb[bi][:], g1_sl[:])
            bank_free[bi] = ("VE", L.inc(L.v, "VE"))
            L.v.tensor_add(stg[:], tmp[:], x1_sl[:])
            ve_x2 = L.inc(L.v, "VE")
            L.wait(L.y, "VE", ve_x2)
            x2_free[fc % 2] = L.dma(L.y, "ST",
                                    o_x2T[fc * 128:(fc + 1) * 128, tt * TT:(tt + 1) * TT], stg[:])
    L.wait(L.y, "ST")
    return L


def prep_C1(inp):
    return {
        "c1_Wo": np.ascontiguousarray(np.asarray(inp["L1_attn_Wo"], np.float32)),
        "c1_gw_col": np.ascontiguousarray(np.asarray(inp["L1_attn_g_weight"], np.float32).reshape(8, 128).T),
    }



HB = 64
NTOKH = NTOK + 2 * HB


def build_A2():
    L = Launch("A2")
    C = declare_consts(L)
    x2h = L.din("x2h", [NTOKH, D])
    gwT = L.din("a2_gwT", [D, 9216])
    vwT = L.din("a2_vwT", [D, D])
    gamma_c = L.din("a2_gamma", [128, 8])
    beta_c = L.din("a2_beta", [128, 8])
    o_xnT_tiles = L.dout("xn2T_tiles", [NTOKH // 128, 128, D])
    o_s2T = L.dout("s2T", [D, NTOK])

    eyef = L.sb("eyef", [128, 128], F32)
    fcol = L.sb("fcol", [128, len(FC_VALS)], F32)
    o1f = L.sb("o1f", [1, 128], F32)
    o128f = L.sb("o128f", [128, 1], F32)
    o128c = L.sb("o128c", [128, 1], BF16)
    gmc = L.sb("gmc", [128, 8], F32)
    btc = L.sb("btc", [128, 8], F32)
    xnT_b = L.sb("xnT_b", [128, NCH * NTOKH], BF16)
    vT = L.sb("vT", [128, NCH * NTOKH], BF16)
    acc = L.sb("acc", [128, NCH * 512], F32)
    wres = L.sb("wres", [128, NCH * 1024], BF16)
    wstr = [L.sb(f"wstr{i}", [128, NCH * 512], BF16) for i in range(2)]
    gate_sb = [L.sb(f"gate_sb{i}", [128, 512], BF16) for i in range(2)]
    tmp_b = L.sb("tmp_b", [128, 512], BF16)
    sqb = L.sb("sqb", [128, NCH * 512], BF16)
    mrow = L.sb("mrow", [1, 512], F32)
    vrow = L.sb("vrow", [1, 512], F32)
    mbc = L.sb("mbc", [128, 512], F32)
    rbc = L.sb("rbc", [128, 512], F32)
    tmpf = L.sb("tmpf", [128, 512], F32)
    s2_st = [L.sb(f"s2_st{i}", [128, 512], F32) for i in range(2)]

    pb = [L.ps(f"pb{i}", [128, 512], F32) for i in range(4)]
    pu = L.ps("pu", [128, 512], F32)

    L.dma(L.y, "LD", eyef[:], C["eyef"])
    L.dma(L.y, "LD", fcol[:], C["fcol"])
    L.dma(L.y, "LD", o1f[:], C["ones1f"])
    L.dma(L.y, "LD", o128f[:], C["ones128f"])
    L.dma(L.y, "LD", gmc[:], gamma_c)
    ld_c = L.dma(L.y, "LD", btc[:], beta_c)
    L.dma(L.g, "GD", o128c[:], C["ones128c"])
    ve_ln, ln_st = emit_ln_fm(L, x2h, o_xnT_tiles, xnT_b, eyef, pu[:, 0:128], fcol, ntok=NTOKH)

    gd_vw = L.dma(L.g, "GD", slab_dst(wres[:], NCH), vwT.rearrange("(c p) f -> p c f", p=128))
    L.wait(L.t, "GD", gd_vw)
    L.wait(L.t, "VE", ve_ln)
    bank_free = {}
    for fc in range(NCH):
        for i in range(NTOKH // 128):
            bi = (fc * (NTOKH // 128) + i) % 2
            sem, cv = bank_free.get(bi, (None, 0))
            if sem is not None:
                L.wait(L.t, sem, cv)
            for k in range(NCH):
                L.t.matmul(pb[bi][:, 0:128],
                           wres[:, k * 1024 + fc * 128: k * 1024 + (fc + 1) * 128],
                           xnT_b[:, k * NTOKH + i * 128: k * NTOKH + (i + 1) * 128],
                           start=(k == 0), stop=(k == NCH - 1))
            pe_d = L.inc(L.t, "PE")
            L.wait(L.s, "PE", pe_d)
            L.s.copy(vT[:, fc * NTOKH + i * 128: fc * NTOKH + (i + 1) * 128], pb[bi][:, 0:128])
            bank_free[bi] = ("SE", L.inc(L.s, "SE"))
    se_vt = L.cnt["SE"]

    SHL = [0, 32, 16, 4, 1, -1, -4, -16, -32]
    SHL = [0, -32, -16, -4, -1, 1, 4, 16, 32]
    gd_blk = {}
    s2_free = {}
    for tt in range(NTT):
        L.wait(L.v, "SE", se_vt)
        L.wait(L.v, "SE")
        L.wait(L.v, "VE")
        L.v.memset(acc[:], 0.0)
        L.inc(L.v, "VE")
        for b in range(18):
            it = tt * 18 + b
            wb = wstr[it % 2]
            g_guard = gd_blk.get(it - 2)
            if g_guard is not None:
                L.wait(L.g, "PE", g_guard)
            gd = L.dma(L.g, "GD", slab_dst(wb[:], NCH),
                       gwT[:, b * 512:(b + 1) * 512].rearrange("(c p) f -> p c f", p=128))
            L.wait(L.t, "GD", gd)
            n = b // 2
            for fl in range(4):
                fc_abs = (b % 2) * 4 + fl
                bi = fl % 2
                sem, cv = bank_free.get(bi, (None, 0))
                if sem is not None:
                    L.wait(L.t, sem, cv)
                for k in range(NCH):
                    L.t.matmul(pb[bi][:],
                               wb[:, k * 512 + fl * 128: k * 512 + (fl + 1) * 128],
                               xnT_b[:, k * NTOKH + HB + tt * TT: k * NTOKH + HB + (tt + 1) * TT],
                               start=(k == 0), stop=(k == NCH - 1))
                pe_d = L.inc(L.t, "PE")
                gb = gate_sb[fl % 2]
                L.wait(L.s, "PE", pe_d)
                L.wait(L.s, "VE")
                L.s.activation(gb[:], pb[bi][:], AF.Sigmoid)
                bank_free[bi] = ("SE", (se_g := L.inc(L.s, "SE")))
                L.wait(L.v, "SE", se_g)
                off = fc_abs * NTOKH + HB + tt * TT + SHL[n]
                L.v.tensor_mul(tmp_b[:], gb[:], vT[:, off:off + TT])
                L.v.tensor_add(acc[:, fc_abs * 512:(fc_abs + 1) * 512],
                               acc[:, fc_abs * 512:(fc_abs + 1) * 512], tmp_b[:])
                L.inc(L.v, "VE")
            gd_blk[it] = L.cnt["PE"]
        ve_mix = L.cnt["VE"]
        L.wait(L.s, "VE", ve_mix)
        L.s.activation(sqb[:], acc[:], AF.Square)
        se_sq = L.inc(L.s, "SE")
        sem, cv = bank_free.get("u", (None, 0))
        if sem is not None:
            L.wait(L.t, sem, cv)
        L.wait(L.t, "VE", ve_mix)
        for k in range(NCH):
            L.t.matmul(pu[0:1, 0:512], o128f[:], acc[:, k * 512:(k + 1) * 512],
                       start=(k == 0), stop=(k == NCH - 1))
        pe_m = L.inc(L.t, "PE")
        L.wait(L.v, "PE", pe_m)
        L.v.tensor_scalar_mul(mrow[:], pu[0:1, 0:512], 1.0 / D)
        ve_m = L.inc(L.v, "VE")
        L.wait(L.t, "SE", se_sq)
        L.wait(L.t, "VE", ve_m)
        for k in range(NCH):
            L.t.matmul(pu[0:1, 0:512], o128c[:], sqb[:, k * 512:(k + 1) * 512],
                       start=(k == 0), stop=(k == NCH - 1))
        pe_q = L.inc(L.t, "PE")
        L.wait(L.v, "PE", pe_q)
        L.v.tensor_scalar_mul(vrow[:], pu[0:1, 0:512], 1.0 / D)
        L.v.tensor_mul(tmpf[0:1, 0:512], mrow[:], mrow[:])
        L.v.tensor_sub(vrow[:], vrow[:], tmpf[0:1, 0:512])
        ve_v = L.inc(L.v, "VE")
        bank_free["u"] = ("VE", ve_v)
        L.wait(L.s, "VE", ve_v)
        L.s.activation(vrow[:], vrow[:], AF.Ln, bias=fcol[0:1, FC[1e-5]:FC[1e-5] + 1])
        L.s.activation(vrow[:], vrow[:], AF.Exp, scale=fcol[0:1, FC[-0.5]:FC[-0.5] + 1])
        se_r = L.inc(L.s, "SE")
        L.wait(L.t, "SE", se_r)
        L.t.matmul(pb[2][:], o1f[:], mrow[:], start=True, stop=True)
        pe_b1 = L.inc(L.t, "PE")
        L.t.matmul(pb[3][:], o1f[:], vrow[:], start=True, stop=True)
        pe_b2 = L.inc(L.t, "PE")
        L.wait(L.v, "PE", pe_b1)
        L.v.tensor_copy(mbc[:], pb[2][:])
        L.wait(L.v, "PE", pe_b2)
        L.v.tensor_copy(rbc[:], pb[3][:])
        ve_bc = L.inc(L.v, "VE")
        bank_free[2] = ("VE", ve_bc)
        bank_free[3] = ("VE", ve_bc)
        for fc in range(NCH):
            stg = s2_st[fc % 2]
            sf = s2_free.get(fc % 2)
            if sf is not None:
                L.wait(L.v, "ST", sf)
            L.wait(L.v, "LD", ld_c)
            L.v.tensor_sub(tmpf[:], acc[:, fc * 512:(fc + 1) * 512], mbc[:])
            L.v.tensor_mul(tmpf[:], tmpf[:], rbc[:])
            L.v.tensor_scalar(stg[:], tmpf[:], gmc[:, fc:fc + 1], btc[:, fc:fc + 1],
                              op0=ALU.mult, op1=ALU.add)
            ve_s2 = L.inc(L.v, "VE")
            L.wait(L.y, "VE", ve_s2)
            s2_free[fc % 2] = L.dma(L.y, "ST",
                                    o_s2T[fc * 128:(fc + 1) * 128, tt * TT:(tt + 1) * TT], stg[:])
    L.wait(L.y, "ST")
    return L


def prep_A2_weights(inp):
    return {
        "a2_gwT": fmaj(inp["L2_attn_gate_W"]),
        "a2_vwT": fmaj(inp["L2_attn_value_W"]),
        "a2_gamma": np.ascontiguousarray(np.asarray(inp["L2_attn_gamma"], np.float32).reshape(8, 128).T),
        "a2_beta": np.ascontiguousarray(np.asarray(inp["L2_attn_beta"], np.float32).reshape(8, 128).T),
    }


def prep_A2_x2h(x2_full):
    ins = []
    xb = x2_full.reshape(4, 4096, D)
    for c in range(NCORES):
        b = c // 2
        base = (c % 2) * 2048
        idx = (np.arange(base - HB, base + 2048 + HB)) % 4096
        ins.append(np.ascontiguousarray(xb[b][idx]))
    return ins



def build_C2():
    L = Launch("C2")
    y6T = L.din("c2_y6T", [D, NTOK])
    s2T = L.din("c2_s2T", [D, NTOK])
    x2T = L.din("c2_x2T", [D, NTOK])
    o_x3T = L.dout("x3T", [D, NTOK])
    ya = L.sb("ya", [128, 512], F32)
    sa = L.sb("sa", [128, 512], F32)
    xa = L.sb("xa", [128, 512], F32)
    ga = L.sb("ga", [128, 512], F32)
    st = [L.sb(f"st{i}", [128, 512], F32) for i in range(2)]
    st_done = {}
    for i in range(32):
        fc, tt = i % 8, i // 8
        sl = (slice(fc * 128, (fc + 1) * 128), slice(tt * TT, (tt + 1) * TT))
        L.wait(L.y, "SE")
        L.wait(L.y, "VE")
        L.dma(L.y, "LD", ya[:], y6T[sl[0], sl[1]])
        L.dma(L.y, "LD", sa[:], s2T[sl[0], sl[1]])
        ld = L.dma(L.y, "LD", xa[:], x2T[sl[0], sl[1]])
        L.wait(L.s, "LD", ld)
        L.s.activation(ga[:], ya[:], AF.Silu)
        se = L.inc(L.s, "SE")
        buf = st[i % 2]
        sd = st_done.get(i % 2)
        L.wait(L.v, "SE", se)
        if sd is not None:
            L.wait(L.v, "ST", sd)
        L.v.tensor_mul(ga[:], ga[:], sa[:])
        L.v.tensor_add(buf[:], ga[:], xa[:])
        ve = L.inc(L.v, "VE")
        L.wait(L.y, "VE", ve)
        st_done[i % 2] = L.dma(L.y, "ST", o_x3T[sl[0], sl[1]], buf[:])
    L.wait(L.y, "ST")
    return L



def build_C3():
    L = Launch("C3")
    aoT = L.din("c3_aoT", [D, NTOK], BF16)
    woT = L.din("c3_woT", [D, D])
    x3T = L.din("c3_x3T", [D, NTOK])
    g3T = L.din("c3_g3T", [D, NTOK])
    o_x4T = L.dout("x4T", [D, NTOK])
    wres = L.sb("wres", [128, NCH * 1024], BF16)
    aob = L.sb("aob", [128, NCH * NTOK], BF16)
    x3_sl = L.sb("x3_sl", [128, 512], F32)
    g3_sl = L.sb("g3_sl", [128, 512], F32)
    tmp = L.sb("tmp", [128, 512], F32)
    st = [L.sb(f"st{i}", [128, 512], F32) for i in range(2)]
    pb = [L.ps(f"pb{i}", [128, 512], F32) for i in range(2)]
    gd_w = L.dma(L.g, "GD", slab_dst(wres[:], NCH), woT.rearrange("(c p) f -> p c f", p=128))
    gd_a = L.dma(L.g, "GD", slab_dst(aob[:], NCH),
                 aoT.rearrange("(c p) t -> p c t", p=128))
    L.wait(L.t, "GD", gd_a)
    bank_free = {}
    st_done = {}
    for tt in range(NTT):
        for fc in range(NCH):
            bi = fc % 2
            sem, cv = bank_free.get(bi, (None, 0))
            if sem is not None:
                L.wait(L.t, sem, cv)
            for k in range(NCH):
                L.t.matmul(pb[bi][:],
                           wres[:, k * 1024 + fc * 128: k * 1024 + (fc + 1) * 128],
                           aob[:, k * NTOK + tt * TT: k * NTOK + (tt + 1) * TT],
                           start=(k == 0), stop=(k == NCH - 1))
            pe_p = L.inc(L.t, "PE")
            L.wait(L.y, "VE")
            L.dma(L.y, "LD", x3_sl[:], x3T[fc * 128:(fc + 1) * 128, tt * TT:(tt + 1) * TT])
            ld = L.dma(L.y, "LD", g3_sl[:], g3T[fc * 128:(fc + 1) * 128, tt * TT:(tt + 1) * TT])
            buf = st[fc % 2]
            sd = st_done.get(fc % 2)
            L.wait(L.v, "PE", pe_p)
            L.wait(L.v, "LD", ld)
            if sd is not None:
                L.wait(L.v, "ST", sd)
            L.v.tensor_mul(tmp[:], pb[bi][:], g3_sl[:])
            bank_free[bi] = ("VE", L.cnt["VE"] + 1)
            L.v.tensor_add(buf[:], tmp[:], x3_sl[:])
            ve = L.inc(L.v, "VE")
            L.wait(L.y, "VE", ve)
            st_done[fc % 2] = L.dma(L.y, "ST",
                                    o_x4T[fc * 128:(fc + 1) * 128, tt * TT:(tt + 1) * TT], buf[:])
    L.wait(L.y, "ST")
    return L


def prep_A0_weights(inp):
    return {
        "a0_WcT": fmaj(inp["L0_attn_Wc"]),
        "a0_bc_col": np.ascontiguousarray(np.asarray(inp["L0_attn_bc"]).reshape(80, 128).T),
        "a0_bc_row": np.asarray(inp["L0_attn_bc"]).reshape(1, 10240),
        "f0_wrT": fmaj(inp["L0_ffn_w_router"]),
        "f0_mhT": fmaj(inp["L0_ffn_w_mh"]),
        "f0_kf": np.ascontiguousarray(np.asarray(inp["L0_ffn_k_ffwd"]).transpose(0, 3, 1, 2).reshape(8, 128, 4096)),
        "f0_vf": np.ascontiguousarray(np.asarray(inp["L0_ffn_v_ffwd"]).reshape(8, 32, 128, 128)
                                      .transpose(0, 2, 1, 3).reshape(8, 128, 4096)),
        "f0_mgT": fmaj(inp["L0_ffn_w_mg"]),
    }


def host_unpack_xnT(xnT_tiles):
    a = xnT_tiles.reshape(NTILE, 128, NCH, 128)
    return np.ascontiguousarray(a.transpose(2, 1, 0, 3).reshape(D, NTOK))


def host_unpack_v(v_blk):
    a = v_blk.reshape(16, 128, 16, 512)
    return np.ascontiguousarray(a.transpose(2, 1, 0, 3).reshape(NTOK, 8192))





"""L2_ffn inner layer launch (one build; 6 launches with different weight inputs)
plus TBL (rotary tables, run once).

Token layout per core:
  l2_yT [1024, 2816] covers tokens [-768, 2048): yT col of token t = t+768.
  qkv buffer covers [-512, 2048) = 2560: buf col = t+512.  5 tiles of 512.
  attnout/y1 cover [-256, 2048) = 2304: y1 col = t+256.
  windows w = -1..7; queries of win w at buf cols (w+2)*256, y1 cols (w+1)*256.
  keys for win w at buf cols (w+1)*256 .. +512.
"""

NB = 2560
NY1 = 2304
W1COLS = []
_c = 0
for _j in range(22):
    _w = 128 if _j < 21 else 42
    W1COLS.append((_c, _w))
    _c += 2 * _w
assert _c == 5460


def l2_consts():
    o2h = np.zeros((128, 2), np.float32)
    o2h[0:64, 0] = 1.0
    o2h[64:128, 1] = 1.0
    s2h = np.zeros((2, 128), np.float32)
    s2h[0, 0:64] = 1.0
    s2h[1, 64:128] = 1.0
    return {"c_o2h": o2h.astype(BF), "c_s2h": s2h.astype(BF)}


def build_L2F():
    L = Launch("L2F")
    C = declare_consts(L)
    yT = L.din("l2_yT", [D, 2816])
    wqkv = L.din("l2_wqkv", [D, 3072])
    wo = L.din("l2_wo", [D, D])
    w1p = L.din("l2_w1p", [D, 5460])
    w2 = L.din("l2_w2", [2730, D])
    b1p = L.din("l2_b1p", [128, 44])
    b2cI = L.din("l2_b2c", [128, 8])
    lngI = L.din("l2_lngc", [128, 8])
    lnbI = L.din("l2_lnbc", [128, 8])
    qsI = L.din("l2_qsc", [128, 1])
    ksI = L.din("l2_ksc", [128, 1])
    m0I = L.din("l2_m0", [128, 4 * 256], BF16)
    msI = L.din("l2_ms", [128, 4 * 256], BF16)
    csI = L.din("l2_cs", [128, NB], BF16)
    t2I = L.din("l2_t2", [128, NB], BF16)
    ciI = L.din("l2_ci", [128, NB], BF16)
    t2iI = L.din("l2_t2i", [128, NB], BF16)
    kpI = L.din("l2_keep", [128, 1])
    o2hI = L.din("c_o2h", [128, 2], BF16)
    s2hI = L.din("c_s2h", [2, 128], BF16)
    o_oyT = L.dout("oyT", [D, NTOK])
    y1scr = L.nc.dram_tensor("l2_y1scr", [D, NY1], F32, kind="Internal").ap()

    fcol = L.sb("fcol", [128, len(FC_VALS)], F32)
    o1f = L.sb("o1f", [1, 128], F32)
    o128c = L.sb("o128c", [128, 1], BF16)
    eyeb = L.sb("eyeb", [128, 128], BF16)
    o2h = L.sb("o2h", [128, 2], BF16)
    s2h = L.sb("s2h", [2, 128], BF16)
    qscol = L.sb("qscol", [128, 1], F32)
    kscol = L.sb("kscol", [128, 1], F32)
    b1c = L.sb("b1c", [128, 44], F32)
    b2c = L.sb("b2c", [128, 8], F32)
    lng = L.sb("lng", [128, 8], F32)
    lnb = L.sb("lnb", [128, 8], F32)
    kpc = L.sb("kpc", [128, 1], F32)
    m0b = L.sb("m0b", [128, 1024], BF16)
    msb = L.sb("msb", [128, 1024], BF16)
    csb = L.sb("csb", [128, NB], BF16)
    t2b = L.sb("t2b", [128, NB], BF16)
    cib = L.sb("cib", [128, NB], BF16)
    t2ib = L.sb("t2ib", [128, NB], BF16)

    poolA = L.sb("poolA", [128, 22016], BF16)
    ybT = poolA[:, 0:NCH * NB]
    prodb = poolA[:, 0:22 * 512]
    xnb = poolA[:, 11264:11264 + NCH * 512]
    wstr = [poolA[:, 15360 + i * 2048: 15360 + (i + 1) * 2048] for i in range(2)]
    w2s = [poolA[:, 19456 + i * 1024: 19456 + (i + 1) * 1024] for i in range(2)]

    qc = L.sb("qc", [128, NB], BF16)
    kc = L.sb("kc", [128, NB], BF16)
    vc = L.sb("vc", [128, NB], BF16)
    poolB = L.sb("poolB", [128, 3072], BF16)
    vtok = poolB[:, 0:2560]
    aoT = L.sb("aoT", [128, NCH * NY1], BF16)
    y1T = L.sb("y1T", [128, NCH * NY1], BF16)
    wqs = L.sb("wqs", [128, 3 * NCH * 128], BF16)
    wos = L.sb("wos", [128, NCH * 1024], BF16)
    PT = L.sb("PT", [128, 2048], BF16)
    rotA = PT[:, 1024:1536]
    rotB = PT[:, 1536:2048]
    tmpa = L.sb("tmpa", [128, 512], F32)
    tmpb = poolB[:, 2048:3072].bitcast(F32)
    tmpc = L.sb("tmpc", [128, 512], BF16)
    tmpd = L.sb("tmpd", [128, 512], BF16)
    nrm = L.sb("nrm", [2, 512], F32)
    nrmb = L.sb("nrmb", [2, 512], BF16)
    rb2 = L.sb("rb2", [128, 512], F32)
    onesb = L.sb("onesb", [128, 128], BF16)
    bdiag = L.sb("bdiag", [128, 128], BF16)
    rrow = L.sb("rrow", [1, 256], F32)
    ravb = L.sb("ravb", [64, 256], F32)
    mrow = L.sb("mrow", [1, 512], F32)
    vrow = L.sb("vrow", [1, 512], F32)
    mbc = poolB[:, 0:1024].bitcast(F32)
    rbc = poolB[:, 1024:2048].bitcast(F32)
    st_f = [L.sb(f"st_f{i}", [128, 512], F32) for i in range(2)]

    pb = [L.ps(f"pb{i}", [128, 512], F32) for i in range(8)]
    pbf = pb[7][:].bitcast(BF16)

    L.dma(L.g, "GD", o128c[:], C["ones128c"])
    L.dma(L.g, "GD", eyeb[:], C["eyeb"])
    L.dma(L.g, "GD", o2h[:], o2hI)
    L.dma(L.g, "GD", s2h[:], s2hI)
    for (dst, src) in ((fcol, C["fcol"]), (o1f, C["ones1f"]), (qscol, qsI), (kscol, ksI),
                       (b1c, b1p), (b2c, b2cI), (lng, lngI), (lnb, lnbI), (kpc, kpI),
                       (m0b, m0I), (msb, msI)):
        L.dma(L.y, "LD", dst[:], src)
    for (dst, src) in ((csb, csI), (t2b, t2I), (cib, ciI), (t2ib, t2iI)):
        L.dma(L.y, "LD", dst[:], src)
    for ch in range(NCH):
        off = 768 if ch < 4 else 767
        L.dma(L.g, "GD", ybT[:, ch * NB:(ch + 1) * NB],
              yT[ch * 128:(ch + 1) * 128, off - 512: off - 512 + NB])
    gd_y = L.cnt["GD"]
    gd_wo = L.dma(L.g, "GD", slab_dst(wos[:], NCH), wo.rearrange("(c p) f -> p c f", p=128))
    ld_c = L.cnt["LD"]
    L.wait(L.v, "LD", ld_c)
    L.wait(L.s, "LD", ld_c)
    L.v.memset(onesb[:], 1.0)
    L.v.memset(bdiag[:], 0.0)
    L.v.memset(bdiag[0:64, 0:64], 1.0)
    L.v.memset(bdiag[64:128, 64:128], 1.0)
    ve_ones = L.inc(L.v, "VE")

    bank_free = {}
    wqs_free = 0
    for hc in range(8):
        L.wait(L.g, "PE", wqs_free)
        for g in range(3):
            L.dma(L.g, "GD", slab_dst(wqs[:, g * 1024:(g + 1) * 1024], NCH),
                  wqkv[:, g * 1024 + hc * 128: g * 1024 + (hc + 1) * 128]
                  .rearrange("(c p) f -> p c f", p=128))
        gd_w = L.cnt["GD"]
        L.wait(L.t, "GD", max(gd_w, gd_y))
        if hc == 0:
            L.wait(L.t, "LD", ld_c)
        for t5 in range(5):
            tc = t5 * 512
            for g in range(3):
                bi = g % 2
                sem, cv = bank_free.get(bi, (None, 0))
                if sem is not None:
                    L.wait(L.t, sem, cv)
                for k in range(NCH):
                    L.t.matmul(pb[bi][:],
                               wqs[:, g * 1024 + k * 128:g * 1024 + (k + 1) * 128],
                               ybT[:, k * NB + tc: k * NB + tc + 512],
                               start=(k == 0), stop=(k == NCH - 1))
                pe_d = L.inc(L.t, "PE")
                if g == 2:
                    L.wait(L.s, "PE", pe_d)
                    L.wait(L.s, "VE")
                    L.s.copy(vc[:, tc:tc + 512], pb[bi][:])
                    bank_free[bi] = ("SE", L.inc(L.s, "SE"))
                else:
                    dstc = qc if g == 0 else kc
                    scol = qscol if g == 0 else kscol
                    cst = csb if g == 0 else cib
                    t2t = t2b if g == 0 else t2ib
                    L.wait(L.s, "PE", pe_d)
                    L.wait(L.s, "VE")
                    L.s.copy(dstc[:, tc:tc + 512], pb[bi][:])
                    L.s.activation(tmpc[:], pb[bi][:], AF.Square)
                    se_sq = L.inc(L.s, "SE")
                    bank_free[bi] = ("SE", se_sq)
                    sem, cv = bank_free.get(5, (None, 0))
                    if sem is not None:
                        L.wait(L.t, sem, cv)
                    L.wait(L.t, "SE", se_sq)
                    if hc == 0 and t5 == 0:
                        L.wait(L.t, "VE", ve_ones)
                    L.t.matmul(pb[5][:], bdiag[:], tmpc[:], start=True, stop=True)
                    pe_n = L.inc(L.t, "PE")
                    L.wait(L.v, "PE", pe_n)
                    L.v.tensor_scalar_max(rb2[:], pb[5][:],
                                          fcol[:, FC[1e-20]:FC[1e-20] + 1])
                    ve_n0 = L.inc(L.v, "VE")
                    bank_free[5] = ("VE", ve_n0)
                    L.wait(L.s, "VE", ve_n0)
                    L.s.activation(rb2[:], rb2[:], AF.Ln)
                    L.s.activation(rb2[:], rb2[:], AF.Exp, scale=-0.5)
                    se_n = L.inc(L.s, "SE")
                    L.wait(L.v, "SE", se_n)
                    L.v.scalar_tensor_tensor(tmpd[:], dstc[:, tc:tc + 512], scol[:], rb2[:],
                                             op0=ALU.mult, op1=ALU.mult)
                    L.v.tensor_mul(rotA[:], tmpd[:], cst[:, tc:tc + 512])
                    for r in range(4):
                        ro = r * 32
                        so = (r ^ 1) * 32
                        L.v.tensor_mul(rotB[so:so + 32, :], tmpd[ro:ro + 32, :],
                                       t2t[ro:ro + 32, tc:tc + 512])
                    L.v.tensor_add(dstc[:, tc:tc + 512], rotA[:], rotB[:])
                    L.inc(L.v, "VE")
            se_now = L.cnt["SE"]
            L.wait(L.t, "SE", se_now)
            for j4 in range(4):
                jj = t5 * 4 + j4
                sem, cv = bank_free.get("tp", (None, 0))
                if sem is not None:
                    L.wait(L.t, sem, cv)
                L.t.transpose(pbf[:, 0:128], vc[:, jj * 128:(jj + 1) * 128], eyeb[:])
                pe_t = L.inc(L.t, "PE")
                L.wait(L.v, "PE", pe_t)
                L.v.tensor_copy(vtok[:, jj * 128:(jj + 1) * 128], pbf[:, 0:128])
                bank_free["tp"] = ("VE", L.inc(L.v, "VE"))
        ve_qkv = L.cnt["VE"]
        L.wait(L.t, "VE", ve_qkv)
        for head in range(2):
            r0 = head * 64
            for w in range(-1, 8):
                qcol = (w + 2) * 256
                kcol = (w + 1) * 256
                mt = m0b if w == 0 else msb
                for kcn in range(4):
                    sem, cv = bank_free.get(kcn, (None, 0))
                    if sem is not None:
                        L.wait(L.t, sem, cv)
                    L.t.matmul(pb[kcn][:, 0:256],
                               kc[r0:r0 + 64, kcol + kcn * 128: kcol + (kcn + 1) * 128],
                               qc[r0:r0 + 64, qcol:qcol + 256], start=True, stop=True)
                    pe_s = L.inc(L.t, "PE")
                    L.wait(L.v, "PE", pe_s)
                    L.wait(L.v, "SE")
                    L.v.scalar_tensor_tensor(tmpa[:, 0:256], pb[kcn][:, 0:256], 8.0,
                                             mt[:, kcn * 256:(kcn + 1) * 256],
                                             op0=ALU.mult, op1=ALU.add)
                    ve_m = L.inc(L.v, "VE")
                    bank_free[kcn] = ("VE", ve_m)
                    L.wait(L.s, "VE", ve_m)
                    L.s.activation(PT[:, kcn * 256:(kcn + 1) * 256], tmpa[:, 0:256], AF.Exp)
                    L.inc(L.s, "SE")
                se_pt = L.cnt["SE"]
                sem, cv = bank_free.get(5, (None, 0))
                if sem is not None:
                    L.wait(L.t, sem, cv)
                L.wait(L.t, "SE", se_pt)
                if hc == 0 and head == 0 and w == -1:
                    L.wait(L.t, "VE", ve_ones)
                for kcn in range(4):
                    L.t.matmul(pb[5][0:64, 0:256], onesb[:, 0:64],
                               PT[:, kcn * 256:(kcn + 1) * 256],
                               start=(kcn == 0), stop=(kcn == 3))
                pe_sum = L.inc(L.t, "PE")
                L.wait(L.s, "PE", pe_sum)
                L.wait(L.s, "VE")
                L.s.activation(ravb[:], pb[5][0:64, 0:256], AF.Ln)
                L.s.activation(ravb[:], ravb[:], AF.Exp, scale=-1.0)
                se_rav = L.inc(L.s, "SE")
                bank_free[5] = ("SE", se_rav)
                sem, cv = bank_free.get(4, (None, 0))
                if sem is not None:
                    L.wait(L.t, sem, cv)
                for kcn in range(4):
                    jj = kcol // 128 + kcn
                    L.t.matmul(pb[4][0:64, 0:256],
                               vtok[:, jj * 128 + r0: jj * 128 + r0 + 64],
                               PT[:, kcn * 256:(kcn + 1) * 256],
                               start=(kcn == 0), stop=(kcn == 3))
                pe_av = L.inc(L.t, "PE")
                L.wait(L.v, "PE", pe_av)
                L.wait(L.v, "SE", se_rav)
                L.v.tensor_mul(aoT[r0:r0 + 64, hc * NY1 + (w + 1) * 256: hc * NY1 + (w + 2) * 256],
                               pb[4][0:64, 0:256], ravb[:])
                bank_free[4] = ("VE", L.inc(L.v, "VE"))
        wqs_free = L.cnt["PE"]
    ve_attn = L.cnt["VE"]

    T5 = [(0, 512), (512, 512), (1024, 512), (1536, 512), (2048, 256)]
    L.wait(L.t, "GD", gd_wo)
    L.wait(L.t, "VE", ve_attn)
    y1st = {}
    for (tc, tw) in T5:
        for fc in range(NCH):
            bi = fc % 2
            sem, cv = bank_free.get(bi, (None, 0))
            if sem is not None:
                L.wait(L.t, sem, cv)
            for k in range(NCH):
                L.t.matmul(pb[bi][:, 0:tw],
                           wos[:, k * 1024 + fc * 128: k * 1024 + (fc + 1) * 128],
                           aoT[:, k * NY1 + tc: k * NY1 + tc + tw],
                           start=(k == 0), stop=(k == NCH - 1))
            pe_p = L.inc(L.t, "PE")
            L.wait(L.y, "VE")
            ld_y = L.dma(L.y, "LD", tmpa[:, 0:tw],
                         yT[fc * 128:(fc + 1) * 128, 512 + tc: 512 + tc + tw])
            L.wait(L.v, "PE", pe_p)
            L.wait(L.v, "LD", ld_y)
            yp = y1st.get(fc % 2)
            if yp is not None:
                L.wait(L.v, "ST", yp)
            L.v.tensor_add(st_f[fc % 2][:, 0:tw], pb[bi][:, 0:tw], tmpa[:, 0:tw])
            bank_free[bi] = ("VE", L.cnt["VE"] + 1)
            L.v.tensor_copy(y1T[:, fc * NY1 + tc: fc * NY1 + tc + tw], st_f[fc % 2][:, 0:tw])
            ve_y1 = L.inc(L.v, "VE")
            L.wait(L.y, "VE", ve_y1)
            y1st[fc % 2] = L.dma(L.y, "ST", y1scr[fc * 128:(fc + 1) * 128, tc:tc + tw],
                                 st_f[fc % 2][:, 0:tw])
    for ch in range(4, 8):
        L.v.tensor_scalar(y1T[:, ch * NY1 + 255: ch * NY1 + 256],
                          y1T[:, ch * NY1 + 255: ch * NY1 + 256], kpc[:], None, op0=ALU.mult)
    ve_keep = L.inc(L.v, "VE")

    L.wait(L.t, "VE", ve_keep)
    L.wait(L.y, "ST")
    oy_free = {}
    pe_y1 = L.cnt["PE"]
    wstr.append(wos[:, 0:2048])
    w2s.append(wos[:, 2048:3072])
    wstr_guard = {0: 0, 1: 0, 2: pe_y1}
    w2s_guard = {0: 0, 1: 0, 2: pe_y1}
    for tt in range(NTT):
        yc = 256 + tt * 512

        def shv(ch, width=512):
            off = yc if ch < 4 else yc - 1
            return y1T[:, ch * NY1 + off: ch * NY1 + off + width]

        sem, cv = bank_free.get(5, (None, 0))
        if sem is not None:
            L.wait(L.t, sem, cv)
        for k in range(NCH):
            L.t.matmul(pb[5][:], onesb[:], shv(k), start=(k == 0), stop=(k == NCH - 1))
        pe_m = L.inc(L.t, "PE")
        L.wait(L.v, "PE", pe_m)
        L.v.tensor_scalar_mul(mbc, pb[5][:], 1.0 / D)
        bank_free[5] = ("VE", L.inc(L.v, "VE"))
        pe_prev = None
        for k in range(NCH):
            L.wait(L.s, "VE")
            if pe_prev is not None:
                L.wait(L.s, "PE", pe_prev)
            L.s.activation(tmpc[:], shv(k), AF.Square)
            se_s = L.inc(L.s, "SE")
            if k == 0:
                sem, cv = bank_free.get(6, (None, 0))
                if sem is not None:
                    L.wait(L.t, sem, cv)
            L.wait(L.t, "SE", se_s)
            L.t.matmul(pb[6][:], onesb[:], tmpc[:], start=(k == 0), stop=(k == NCH - 1))
            pe_prev = L.inc(L.t, "PE")
        pe_q = pe_prev
        L.wait(L.v, "PE", pe_q)
        L.v.tensor_scalar_mul(tmpa[:], pb[6][:], 1.0 / D)
        L.v.tensor_mul(rbc, mbc, mbc)
        L.v.tensor_sub(tmpa[:], tmpa[:], rbc)
        ve_v = L.inc(L.v, "VE")
        bank_free[6] = ("VE", ve_v)
        L.wait(L.s, "VE", ve_v)
        L.s.activation(tmpa[:], tmpa[:], AF.Ln, bias=fcol[:, FC[1e-5]:FC[1e-5] + 1])
        L.s.activation(rbc, tmpa[:], AF.Exp, scale=fcol[:, FC[-0.5]:FC[-0.5] + 1])
        se_bc = L.inc(L.s, "SE")
        L.wait(L.v, "SE", se_bc)
        for ch in range(NCH):
            L.v.tensor_sub(tmpa[:], shv(ch), mbc)
            L.v.tensor_mul(tmpa[:], tmpa[:], rbc)
            L.v.tensor_scalar(xnb[:, ch * 512:(ch + 1) * 512], tmpa[:],
                              lng[:, ch:ch + 1], lnb[:, ch:ch + 1], op0=ALU.mult, op1=ALU.add)
        ve_xn = L.inc(L.v, "VE")
        for j in range(22):
            c0, wdt = W1COLS[j]
            wb = wstr[j % 3]
            if tt == 0 and j < 2:
                L.wait(L.g, "VE", ve_keep)
                L.wait(L.g, "PE")
            else:
                L.wait(L.g, "PE", wstr_guard[j % 3])
            gd = L.dma(L.g, "GD", slab_dst(wb[0:128, 0:NCH * 2 * wdt], NCH),
                       w1p[:, c0:c0 + 2 * wdt].rearrange("(c p) f -> p c f", p=128))
            L.wait(L.t, "GD", gd)
            if j == 0:
                L.wait(L.t, "VE", ve_xn)
            for half in range(2):
                bi = half
                sem, cv = bank_free.get(bi, (None, 0))
                if sem is not None:
                    L.wait(L.t, sem, cv)
                for k in range(NCH):
                    L.t.matmul(pb[bi][0:wdt, 0:512],
                               wb[:, k * 2 * wdt + half * wdt: k * 2 * wdt + (half + 1) * wdt],
                               xnb[:, k * 512:(k + 1) * 512],
                               start=(k == 0), stop=(k == NCH - 1))
                pe_d = L.inc(L.t, "PE")
                L.wait(L.s, "PE", pe_d)
                L.wait(L.s, "VE")
                if half == 0:
                    L.wait(L.v, "PE", pe_d)
                    L.wait(L.v, "SE")
                    L.v.tensor_scalar(tmpc[0:wdt, :], pb[bi][0:wdt, :],
                                      b1c[0:wdt, 2 * j:2 * j + 1], None, op0=ALU.add)
                    bank_free[bi] = ("VE", L.inc(L.v, "VE"))
                else:
                    L.s.activation(tmpd[0:wdt, :], pb[bi][0:wdt, :], AF.Gelu,
                                   bias=b1c[0:wdt, 2 * j + 1:2 * j + 2])
                    bank_free[bi] = ("SE", L.inc(L.s, "SE"))
            se_ag = L.cnt["SE"]
            L.wait(L.v, "SE", se_ag)
            L.v.tensor_mul(prodb[0:wdt, j * 512:(j + 1) * 512], tmpc[0:wdt, :], tmpd[0:wdt, :])
            L.inc(L.v, "VE")
            wstr_guard[j % 3] = L.cnt["PE"]
        ve_prod = L.cnt["VE"]
        for bi in range(8):
            sem, cv = bank_free.get(bi, (None, 0))
            if sem is not None:
                L.wait(L.t, sem, cv)
        L.wait(L.t, "VE", ve_prod)
        for j in range(22):
            c0, wdt = W1COLS[j]
            wb2 = w2s[j % 3]
            g2 = w2s_guard[j % 3]
            if g2:
                L.wait(L.g, "PE", g2)
            elif tt == 0 and j < 2:
                L.wait(L.g, "PE")
            gd = L.dma(L.g, "GD", wb2[0:wdt, :], w2[j * 128: j * 128 + wdt, :])
            L.wait(L.t, "GD", gd)
            for oc in range(8):
                L.t.matmul(pb[oc][:],
                           wb2[0:wdt, oc * 128:(oc + 1) * 128],
                           prodb[0:wdt, j * 512:(j + 1) * 512],
                           start=(j == 0), stop=(j == 21))
            L.inc(L.t, "PE")
            w2s_guard[j % 3] = L.cnt["PE"]
        pe_w2 = L.cnt["PE"]
        for oc in range(8):
            L.wait(L.y, "VE")
            ld_y1 = L.dma(L.y, "LD", tmpb, y1scr[oc * 128:(oc + 1) * 128, yc:yc + 512])
            of = oy_free.get(oc % 2)
            L.wait(L.v, "PE", pe_w2)
            L.v.tensor_scalar(tmpa[:], pb[oc][:], b2c[:, oc:oc + 1], None, op0=ALU.add)
            L.wait(L.v, "LD", ld_y1)
            if of is not None:
                L.wait(L.v, "ST", of)
            L.v.tensor_add(st_f[oc % 2][:], tmpa[:], tmpb)
            bank_free[oc] = ("VE", (ve_o := L.inc(L.v, "VE")))
            L.wait(L.y, "VE", ve_o)
            oy_free[oc % 2] = L.dma(L.y, "ST",
                                    o_oyT[oc * 128:(oc + 1) * 128, tt * 512:(tt + 1) * 512],
                                    st_f[oc % 2][:])
    L.wait(L.y, "ST")
    return L



def build_TBL():
    L = Launch("TBL")
    C = declare_consts(L)
    pos = L.din("tb_pos", [1, NB])
    invf = L.din("tb_invf", [1, 128])
    lnb_ = L.din("tb_lnbase", [1, 128])
    sgn = L.din("tb_sign", [128, 1])
    o_cs = L.dout("cs", [128, NB], BF16)
    o_t2 = L.dout("t2", [128, NB], BF16)
    o_ci = L.dout("ci", [128, NB], BF16)
    o_t2i = L.dout("t2i", [128, NB], BF16)

    posr = L.sb("posr", [1, NB], F32)
    invr = L.sb("invr", [1, 128], F32)
    lbr = L.sb("lbr", [1, 128], F32)
    sgc = L.sb("sgc", [128, 1], F32)
    prow = L.sb("prow", [1, 512], F32)
    th = L.sb("th", [128, 512], F32)
    ml = L.sb("ml", [128, 512], F32)
    aa = L.sb("aa", [128, 512], F32)
    sinb = L.sb("sinb", [128, 512], F32)
    cosb = L.sb("cosb", [128, 512], F32)
    scl = L.sb("scl", [128, 512], F32)
    iscl = L.sb("iscl", [128, 512], F32)
    t1 = L.sb("t1", [128, 512], F32)
    ui = L.sb("ui", [128, 512], mybir.dt.int32)
    uf = L.sb("uf", [128, 512], F32)
    ob = [L.sb(f"ob{i}", [128, 512], BF16) for i in range(4)]
    pf = L.ps("pf", [128, 512], F32)
    pg = L.ps("pg", [128, 512], F32)

    PI = float(np.pi)
    L.dma(L.y, "LD", posr[:], pos)
    L.dma(L.y, "LD", invr[:], invf)
    L.dma(L.y, "LD", lbr[:], lnb_)
    ldc = L.dma(L.y, "LD", sgc[:], sgn)
    st_done = {}
    pf_free = 0
    pg_free_se = 0
    for t5 in range(5):
        tc = t5 * 512
        L.wait(L.t, "LD", ldc)
        L.wait(L.t, "VE", pf_free)
        L.t.matmul(pf[:], invr[:], posr[0:1, tc:tc + 512], start=True, stop=True)
        pe_f = L.inc(L.t, "PE")
        L.wait(L.v, "PE", pe_f)
        L.v.tensor_scalar(prow[:], posr[0:1, tc:tc + 512], -2048.0, 1.0 / 128.0,
                          op0=ALU.add, op1=ALU.mult)
        ve_pr = L.inc(L.v, "VE")
        L.wait(L.t, "VE", ve_pr)
        L.wait(L.t, "SE", pg_free_se)
        L.t.matmul(pg[:], lbr[:], prow[:], start=True, stop=True)
        pe_g = L.inc(L.t, "PE")
        L.wait(L.s, "PE", pe_g)
        L.wait(L.s, "VE")
        L.s.activation(scl[:], pg[:], AF.Exp)
        L.s.activation(iscl[:], pg[:], AF.Exp, scale=-1.0)
        pg_free_se = L.inc(L.s, "SE")
        def wrap_and_sin(dst, phase_add):
            L.v.tensor_scalar(t1[:], pf[:], phase_add, 1.0 / (2 * PI),
                              op0=ALU.add, op1=ALU.mult)
            L.v.tensor_copy(ui[:], t1[:])
            L.v.tensor_copy(uf[:], ui[:])
            L.v.tensor_sub(t1[:], t1[:], uf[:])
            L.v.tensor_scalar_mul(th[:], t1[:], 2 * PI)
            L.v.tensor_scalar(ml[:], th[:], -PI, None, op0=ALU.is_lt)
            L.v.scalar_tensor_tensor(th[:], ml[:], 2 * PI, th[:], op0=ALU.mult, op1=ALU.add)
            L.v.tensor_scalar(ml[:], th[:], PI, None, op0=ALU.is_lt)
            L.v.tensor_scalar(ml[:], ml[:], -1.0, 1.0, op0=ALU.mult, op1=ALU.add)
            L.v.scalar_tensor_tensor(th[:], ml[:], -2 * PI, th[:], op0=ALU.mult, op1=ALU.add)
            ve_t = L.inc(L.v, "VE")
            L.wait(L.s, "VE", ve_t)
            L.s.activation(dst[:], th[:], AF.Sin)
            return L.inc(L.s, "SE")
        L.wait(L.v, "PE", pe_f)
        se_sin = wrap_and_sin(sinb, 0.0)
        se_cos = wrap_and_sin(cosb, PI / 2)
        pf_free = L.cnt["VE"]
        L.wait(L.v, "ST")
        L.wait(L.v, "SE", se_cos)
        L.v.tensor_mul(ob[0][:], cosb[:], scl[:])
        L.v.tensor_mul(t1[:], sinb[:], scl[:])
        L.v.tensor_scalar(ob[1][:], t1[:], sgc[:], None, op0=ALU.mult)
        L.v.tensor_mul(ob[2][:], cosb[:], iscl[:])
        L.v.tensor_mul(t1[:], sinb[:], iscl[:])
        L.v.tensor_scalar(ob[3][:], t1[:], sgc[:], None, op0=ALU.mult)
        ve_o = L.inc(L.v, "VE")
        L.wait(L.y, "VE", ve_o)
        for (i, dst) in ((0, o_cs), (1, o_t2), (2, o_ci), (3, o_t2i)):
            L.dma(L.y, "ST", dst[:, tc:tc + 512], ob[i][:])
    L.wait(L.y, "ST")
    return L



def prep_L2F_weights(inp, i):
    W1 = np.asarray(inp["L2_ffn_W1"][i], np.float32)
    b1 = np.asarray(inp["L2_ffn_b1"][i], np.float32)
    w1p = np.empty((D, 5460), np.float32)
    b1p = np.zeros((128, 44), np.float32)
    for j in range(22):
        c0, wdt = W1COLS[j]
        a0 = j * 128
        w1p[:, c0:c0 + wdt] = W1[:, a0:a0 + wdt]
        w1p[:, c0 + wdt:c0 + 2 * wdt] = W1[:, 2730 + a0:2730 + a0 + wdt]
        b1p[0:wdt, 2 * j] = b1[a0:a0 + wdt]
        b1p[0:wdt, 2 * j + 1] = b1[2730 + a0:2730 + a0 + wdt]
    qs = np.asarray(inp["L2_ffn_q_scale"][i], np.float32)
    ks = np.asarray(inp["L2_ffn_k_scale"][i], np.float32)
    return {
        "l2_wqkv": np.ascontiguousarray(np.asarray(inp["L2_ffn_Wqkv"][i], np.float32)),
        "l2_wo": np.ascontiguousarray(np.asarray(inp["L2_ffn_Wo"][i], np.float32)),
        "l2_w1p": np.ascontiguousarray(w1p),
        "l2_w2": np.ascontiguousarray(np.asarray(inp["L2_ffn_W2"][i], np.float32)),
        "l2_b1p": b1p,
        "l2_b2c": np.ascontiguousarray(np.asarray(inp["L2_ffn_b2"][i], np.float32).reshape(8, 128).T),
        "l2_lngc": np.ascontiguousarray(np.asarray(inp["L2_ffn_ln_g"][i], np.float32).reshape(8, 128).T),
        "l2_lnbc": np.ascontiguousarray(np.asarray(inp["L2_ffn_ln_b"][i], np.float32).reshape(8, 128).T),
        "l2_qsc": np.tile(qs, 2).reshape(128, 1).astype(np.float32),
        "l2_ksc": np.tile(ks, 2).reshape(128, 1).astype(np.float32),
    }


def l2_masks():
    def build(first_window):
        i = np.arange(256)[:, None]
        jj = np.arange(512)[None, :]
        if first_window:
            jb = jj - 256
            pad = jj < 256
            keep = (~pad) & (i >= jb) & (i <= jb + 256)
        else:
            keep = (jj >= i) & (jj <= i + 256)
        m = np.where(keep, 0.0, -1e30).astype(np.float32)
        mT = m.T
        return np.ascontiguousarray(mT.reshape(4, 128, 256).transpose(1, 0, 2).reshape(128, 1024)).astype(BF)
    return build(False), build(True)


def l2_tbl_inputs(core):
    half = np.arange(0, 64, 2, dtype=np.float64)
    inv_freq = 1.0 / 10000.0 ** (half / 64.0)
    base_scale = (half + 0.4 * 64) / (1.4 * 64)
    dh = np.arange(128) % 64
    invf = inv_freq[np.minimum(dh, 63) % 32]
    lnb_ = np.log(base_scale[dh % 32])
    sgn = np.where(dh < 32, 1.0, -1.0)
    t0 = (core % 2) * 2048 - 512
    pos = np.arange(t0, t0 + NB, dtype=np.float64)
    return {
        "tb_pos": pos.reshape(1, NB).astype(np.float32),
        "tb_invf": invf.reshape(1, 128).astype(np.float32),
        "tb_lnbase": lnb_.reshape(1, 128).astype(np.float32),
        "tb_sign": sgn.reshape(128, 1).astype(np.float32),
    }


def prep_L2F_y(y_full, first):
    ins = []
    yb = y_full.reshape(4, 4096, D)
    for c in range(NCORES):
        b = c // 2
        base = (c % 2) * 2048
        buf = np.zeros((2816, D), np.float32)
        lo = base - 768
        src_lo = max(lo, 0)
        src_hi = min(base + 2048, 4096)
        buf[src_lo - lo: src_hi - lo] = yb[b][src_lo:src_hi]
        ins.append(np.ascontiguousarray(buf.T))
    return ins


def l2_keep(core):
    v = 0.0 if core % 2 == 0 else 1.0
    return {"l2_keep": np.full((128, 1), v, np.float32)}


"""A3: LN3 + L3_attn projections (q,k with rope; v) + L3_ffn (dense experts).
B3: L3 causal attention (head-shard).
TBL3: rope tables for L3 (cos128/sin-sign128 feature-major, per-core tokens)."""


def a3_consts():
    sel8 = np.zeros((8, 8 * 128), np.float32)
    for e in range(8):
        sel8[e, e * 128:(e + 1) * 128] = 1.0
    return {"c_sel8": sel8.astype(BF)}


def build_TBL3():
    L = Launch("TBL3")
    pos = L.din("t3_pos", [1, NTOK])
    invf = L.din("t3_invf", [1, 128])
    sgn = L.din("t3_sign", [128, 1])
    o_cos = L.dout("l3cos", [128, NTOK], BF16)
    o_sin = L.dout("l3sin", [128, NTOK], BF16)
    posr = L.sb("posr", [1, NTOK], F32)
    invr = L.sb("invr", [1, 128], F32)
    sgc = L.sb("sgc", [128, 1], F32)
    th = L.sb("th", [128, 512], F32)
    ml = L.sb("ml", [128, 512], F32)
    t1 = L.sb("t1", [128, 512], F32)
    ui = L.sb("ui", [128, 512], mybir.dt.int32)
    uf = L.sb("uf", [128, 512], F32)
    sinb = L.sb("sinb", [128, 512], F32)
    ob = [L.sb(f"ob{i}", [128, 512], BF16) for i in range(2)]
    pf = L.ps("pf", [128, 512], F32)
    PI = float(np.pi)
    L.dma(L.y, "LD", posr[:], pos)
    L.dma(L.y, "LD", invr[:], invf)
    ldc = L.dma(L.y, "LD", sgc[:], sgn)
    pf_free = 0
    for t4 in range(NTT):
        tc = t4 * 512
        L.wait(L.t, "LD", ldc)
        L.wait(L.t, "VE", pf_free)
        L.t.matmul(pf[:], invr[:], posr[0:1, tc:tc + 512], start=True, stop=True)
        pe_f = L.inc(L.t, "PE")
        L.wait(L.v, "PE", pe_f)
        L.wait(L.v, "ST")

        def wrap_sin(dst, phase):
            L.v.tensor_scalar(t1[:], pf[:], phase, 1.0 / (2 * PI), op0=ALU.add, op1=ALU.mult)
            L.v.tensor_copy(ui[:], t1[:])
            L.v.tensor_copy(uf[:], ui[:])
            L.v.tensor_sub(t1[:], t1[:], uf[:])
            L.v.tensor_scalar_mul(th[:], t1[:], 2 * PI)
            L.v.tensor_scalar(ml[:], th[:], -PI, None, op0=ALU.is_lt)
            L.v.scalar_tensor_tensor(th[:], ml[:], 2 * PI, th[:], op0=ALU.mult, op1=ALU.add)
            L.v.tensor_scalar(ml[:], th[:], PI, None, op0=ALU.is_lt)
            L.v.tensor_scalar(ml[:], ml[:], -1.0, 1.0, op0=ALU.mult, op1=ALU.add)
            L.v.scalar_tensor_tensor(th[:], ml[:], -2 * PI, th[:], op0=ALU.mult, op1=ALU.add)
            ve_t = L.inc(L.v, "VE")
            L.wait(L.s, "VE", ve_t)
            L.s.activation(dst, th[:], AF.Sin)
            return L.inc(L.s, "SE")

        se_c = wrap_sin(ob[0][:], PI / 2)
        L.wait(L.v, "SE", se_c)
        se_s = wrap_sin(sinb[:], 0.0)
        pf_free = L.cnt["VE"]
        L.wait(L.v, "SE", se_s)
        L.v.tensor_scalar(ob[1][:], sinb[:], sgc[:], None, op0=ALU.mult)
        ve_o = L.inc(L.v, "VE")
        L.wait(L.y, "VE", ve_o)
        L.dma(L.y, "ST", o_cos[:, tc:tc + 512], ob[0][:])
        L.dma(L.y, "ST", o_sin[:, tc:tc + 512], ob[1][:])
    L.wait(L.y, "ST")
    return L


def tbl3_inputs(core):
    inv_freq = 10000.0 ** (np.arange(32, dtype=np.float64) * (-1.0 / 32.0))
    p = np.arange(128)
    i32 = p % 32
    invf = inv_freq[i32]
    sgn = np.where((p % 64) < 32, 1.0, -1.0)
    t0 = (core % 2) * 2048
    pos = np.arange(t0, t0 + NTOK, dtype=np.float64)
    return {"t3_pos": pos.reshape(1, NTOK).astype(np.float32),
            "t3_invf": invf.reshape(1, 128).astype(np.float32),
            "t3_sign": sgn.reshape(128, 1).astype(np.float32)}


def l3_perm():
    perm = []
    for h in range(16):
        perm += [h * 64 + 2 * i for i in range(32)]
        perm += [h * 64 + 2 * i + 1 for i in range(32)]
    return np.array(perm)


def build_A3():
    L = Launch("A3")
    C = declare_consts(L)
    x3 = L.din("x3", [NTOK, D])
    wqT = L.din("a3_wqT", [D, D])
    wkT = L.din("a3_wkT", [D, D])
    wvT = L.din("a3_wvT", [D, D])
    l3cos = L.din("l3cos", [128, NTOK], BF16)
    l3sin = L.din("l3sin", [128, NTOK], BF16)
    WrI = L.din("a3_Wr", [D, 8])
    brI = L.din("a3_br", [1, 8])
    W1I = L.din("a3_W1", [8, D, 4096])
    W2I = L.din("a3_W2", [8, 4096, D])
    b1I = L.din("a3_b1c", [128, 8 * 32])
    b2I = L.din("a3_b2c", [128, 8 * 8])
    sel8I = L.din("c_sel8", [8, 8 * 128], BF16)

    o_xnT_tiles = L.dout("xn3T_tiles", [NTILE, 128, D])
    o_qT = L.dout("q3T", [D, NTOK], BF16)
    o_kT = L.dout("k3T", [D, NTOK], BF16)
    o_vT = L.dout("v3T", [D, NTOK], BF16)
    o_g3T = L.dout("g3T", [D, NTOK])

    eyef = L.sb("eyef", [128, 128], F32)
    fcol = L.sb("fcol", [128, len(FC_VALS)], F32)
    o1f = L.sb("o1f", [1, 128], F32)
    sel8 = L.sb("sel8", [8, 8 * 128], BF16)
    cosb = L.sb("cosb", [128, NTOK], BF16)
    sinb = L.sb("sinb", [128, NTOK], BF16)
    xnT_b = L.sb("xnT_b", [128, NCH * NTOK], BF16)
    wres = L.sb("wres", [128, NCH * 1024], BF16)
    wrf = L.sb("wrf", [128, NCH * 8], F32)
    brr = L.sb("brr", [1, 8], F32)
    rbuf = L.sb("rbuf", [128, D], F32)
    b1c = L.sb("b1c", [128, 8 * 32], F32)
    b2c = L.sb("b2c", [128, 8 * 8], F32)
    b2s = L.sb("b2s", [128, 8], F32)
    mk8 = L.sb("mk8", [128, 8], F32)
    mstat = L.sb("mstat", [128, 1], F32)
    mT8 = L.sb("mT8", [8, NTOK], BF16)
    mbce = L.sb("mbce", [128, NTT * 512], BF16)
    rotA = L.sb("rotA", [128, 512], BF16)
    rotB = L.sb("rotB", [128, 512], BF16)
    stg_b = [L.sb(f"stg_b{i}", [128, 512], BF16) for i in range(2)]
    stg_f = [L.sb(f"stg_f{i}", [128, 512], F32) for i in range(2)]
    y_acc = L.sb("y_acc", [128, NTT * NCH * 512], F32)
    hb = L.sb("hb", [128, 4 * 512], BF16)
    tmpf = L.sb("tmpf", [128, 512], F32)
    w1b = [L.sb(f"w1b{i}", [128, NCH * 512], BF16) for i in range(2)]
    w2b = [L.sb(f"w2b{i}", [128, 4 * 1024], BF16) for i in range(2)]

    pb = [L.ps(f"pb{i}", [128, 512], F32) for i in range(6)]
    pu = L.ps("pu", [128, 512], F32)

    L.dma(L.y, "LD", eyef[:], C["eyef"])
    L.dma(L.y, "LD", fcol[:], C["fcol"])
    L.dma(L.y, "LD", o1f[:], C["ones1f"])
    L.dma(L.y, "LD", cosb[:], l3cos)
    L.dma(L.y, "LD", sinb[:], l3sin)
    L.dma(L.y, "LD", b1c[:], b1I)
    L.dma(L.y, "LD", b2c[:], b2I)
    L.dma(L.y, "LD", brr[:], brI)
    ld_c = L.cnt["LD"]
    L.dma(L.g, "GD", sel8[:], sel8I)
    L.wait(L.v, "LD", ld_c)
    L.wait(L.s, "LD", ld_c)
    b2v = b2c[:].rearrange("p (o e) -> p o e", o=8)
    L.v.tensor_reduce(b2s[:], b2v, axis=AX.X, op=ALU.add)
    L.inc(L.v, "VE")

    ve_ln, ln_st = emit_ln_fm(L, x3, o_xnT_tiles, xnT_b, eyef, pu[:, 0:128], fcol)

    bank_free = {}

    def proj(WT, kind, dst):
        L.wait(L.g, "PE")
        gd = L.dma(L.g, "GD", slab_dst(wres[:], NCH), WT.rearrange("(c p) f -> p c f", p=128))
        L.wait(L.t, "GD", gd)
        L.wait(L.t, "VE", ve_ln)
        for fc in range(NCH):
            for tt in range(NTT):
                bi = (fc * NTT + tt) % 2
                sem, cv = bank_free.get(bi, (None, 0))
                if sem is not None:
                    L.wait(L.t, sem, cv)
                for k in range(NCH):
                    L.t.matmul(pb[bi][:],
                               wres[:, k * 1024 + fc * 128: k * 1024 + (fc + 1) * 128],
                               xnT_b[:, k * NTOK + tt * TT: k * NTOK + (tt + 1) * TT],
                               start=(k == 0), stop=(k == NCH - 1))
                pe_d = L.inc(L.t, "PE")
                stg = stg_b[fc % 2]
                sd = bank_free.get(("st", fc % 2))
                L.wait(L.s, "PE", pe_d)
                L.wait(L.s, "VE")
                if sd is not None:
                    L.wait(L.s, "ST", sd[1])
                L.s.copy(stg[:], pb[bi][:])
                se_c = L.inc(L.s, "SE")
                bank_free[bi] = ("SE", se_c)
                if kind == "v":
                    L.wait(L.y, "SE", se_c)
                    bank_free[("st", fc % 2)] = ("ST", L.dma(
                        L.y, "ST", dst[fc * 128:(fc + 1) * 128, tt * TT:(tt + 1) * TT], stg[:]))
                else:
                    L.wait(L.v, "SE", se_c)
                    L.v.tensor_mul(rotA[:], stg[:], cosb[:, tt * TT:(tt + 1) * TT])
                    for r in range(4):
                        ro, so = r * 32, (r ^ 1) * 32
                        L.v.tensor_mul(rotB[so:so + 32, :], stg[ro:ro + 32, :],
                                       sinb[ro:ro + 32, tt * TT:(tt + 1) * TT])
                    L.v.tensor_add(stg[:], rotA[:], rotB[:])
                    ve_r = L.inc(L.v, "VE")
                    L.wait(L.y, "VE", ve_r)
                    bank_free[("st", fc % 2)] = ("ST", L.dma(
                        L.y, "ST", dst[fc * 128:(fc + 1) * 128, tt * TT:(tt + 1) * TT], stg[:]))

    proj(wqT, "q", o_qT)
    proj(wkT, "k", o_kT)
    proj(wvT, "v", o_vT)

    ld_wr = L.dma(L.y, "LD", slab_dst(wrf[:], NCH), WrI.rearrange("(c p) e -> p c e", p=128))
    rb_free = 0
    pu_free_t = 0
    for i in range(NTILE):
        L.wait(L.y, "ST", ln_st[i])
        if rb_free:
            L.wait(L.y, "PE", rb_free)
        ld_rb = L.dma(L.y, "LD", rbuf[:], o_xnT_tiles[i])
        L.wait(L.t, "LD", max(ld_wr, ld_rb))
        L.wait(L.t, "VE", pu_free_t)
        for c in range(NCH):
            L.t.matmul(pu[:, 0:8], rbuf[:, c * 128:(c + 1) * 128],
                       wrf[:, c * 8:(c + 1) * 8], start=(c == 0), stop=False)
        L.t.matmul(pu[:, 0:8], o1f[:], brr[:], start=False, stop=True)
        pe_r = L.inc(L.t, "PE")
        rb_free = pe_r
        L.wait(L.v, "PE", pe_r)
        L.v.tensor_copy(mk8[:], pu[:, 0:8])
        L.v.reduce_max(mstat[:], mk8[:], axis=AX.X)
        L.selfsync(L.v, "VE")
        L.v.tensor_scalar(mk8[:], mk8[:], mstat[:], None, op0=ALU.is_equal)
        ve_m = L.inc(L.v, "VE")
        L.wait(L.t, "VE", ve_m)
        L.t.transpose(pu[0:8, 128:256], mk8[:], eyef[:])
        pe_t = L.inc(L.t, "PE")
        L.wait(L.v, "PE", pe_t)
        L.v.tensor_copy(mT8[:, i * 128:(i + 1) * 128], pu[0:8, 128:256])
        pu_free_t = L.inc(L.v, "VE")

    w1_guard = {0: 0, 1: 0}
    w2_guard = {0: 0, 1: 0}
    g3_done = {}
    L.wait(L.v, "PE")
    L.wait(L.v, "SE")
    L.v.memset(y_acc[:], 0.0)
    ve_y0 = L.inc(L.v, "VE")
    for e in range(8):
        for tt in range(NTT):
            L.wait(L.t, "VE")
            L.t.matmul(pu[:], sel8[:, e * 128:(e + 1) * 128],
                       mT8[:, tt * TT:(tt + 1) * TT], start=True, stop=True)
            pe_mb = L.inc(L.t, "PE")
            L.wait(L.v, "PE", pe_mb)
            L.v.tensor_copy(mbce[:, tt * TT:(tt + 1) * TT], pu[:])
            L.inc(L.v, "VE")
        ve_mb = L.cnt["VE"]
        for blk in range(8):
            it = e * 8 + blk
            wb = w1b[it % 2]
            L.wait(L.g, "PE", w1_guard[it % 2])
            gd1 = L.dma(L.g, "GD", slab_dst(wb[:], NCH),
                        W1I[e][:, blk * 512:(blk + 1) * 512].rearrange("(c p) f -> p c f", p=128))
            wb2 = w2b[it % 2]
            L.wait(L.g, "PE", w2_guard[it % 2])
            gd2 = L.dma(L.g, "GD", slab_dst(wb2[:], 4),
                        W2I[e][blk * 512:(blk + 1) * 512, :].rearrange("(c p) f -> p c f", p=128))
            L.wait(L.t, "GD", gd1)
            for tt in range(NTT):
                for oc in range(4):
                    bi = 4 + (oc % 2)
                    sem, cv = bank_free.get(bi, (None, 0))
                    if sem is not None:
                        L.wait(L.t, sem, cv)
                    for k in range(NCH):
                        L.t.matmul(pb[bi][:],
                                   wb[:, k * 512 + oc * 128: k * 512 + (oc + 1) * 128],
                                   xnT_b[:, k * NTOK + tt * TT: k * NTOK + (tt + 1) * TT],
                                   start=(k == 0), stop=(k == NCH - 1))
                    pe_h = L.inc(L.t, "PE")
                    L.wait(L.v, "PE", pe_h)
                    L.wait(L.v, "SE")
                    if tt == 0 and blk == 0 and oc == 0:
                        L.wait(L.v, "VE", ve_mb)
                    L.v.tensor_mul(tmpf[:], pb[bi][:], mbce[:, tt * TT:(tt + 1) * TT])
                    bank_free[bi] = ("VE", (ve_h := L.inc(L.v, "VE")))
                    L.wait(L.s, "VE", ve_h)
                    L.s.activation(hb[:, oc * 512:(oc + 1) * 512], tmpf[:], AF.Relu,
                                   bias=b1c[:, e * 32 + blk * 4 + oc: e * 32 + blk * 4 + oc + 1])
                    L.inc(L.s, "SE")
                se_h = L.cnt["SE"]
                if tt == 0:
                    L.wait(L.t, "GD", gd2)
                L.wait(L.t, "SE", se_h)
                for oc in range(NCH):
                    bi = oc % 2
                    sem, cv = bank_free.get(bi, (None, 0))
                    if sem is not None:
                        L.wait(L.t, sem, cv)
                    for k in range(4):
                        L.t.matmul(pb[bi][:],
                                   wb2[:, k * 1024 + oc * 128: k * 1024 + (oc + 1) * 128],
                                   hb[:, k * 512:(k + 1) * 512],
                                   start=(k == 0), stop=(k == 3))
                    pe_y = L.inc(L.t, "PE")
                    L.wait(L.v, "PE", pe_y)
                    if e == 0 and blk == 0 and tt == 0 and oc < 2:
                        L.wait(L.v, "VE", ve_y0)
                    yo = (tt * NCH + oc) * 512
                    L.v.tensor_add(y_acc[:, yo:yo + 512],
                                   y_acc[:, yo:yo + 512], pb[bi][:])
                    bank_free[bi] = ("VE", L.inc(L.v, "VE"))
            w1_guard[it % 2] = L.cnt["PE"]
            w2_guard[it % 2] = L.cnt["PE"]
    ve_yf = L.cnt["VE"]
    for tt in range(NTT):
        for oc in range(NCH):
            stg = stg_f[oc % 2]
            gp = g3_done.get(oc % 2)
            L.wait(L.s, "VE", ve_yf)
            if gp is not None:
                L.wait(L.s, "ST", gp)
            yo = (tt * NCH + oc) * 512
            L.s.activation(stg[:], y_acc[:, yo:yo + 512], AF.Silu,
                           bias=b2s[:, oc:oc + 1])
            se_g = L.inc(L.s, "SE")
            L.wait(L.y, "SE", se_g)
            g3_done[oc % 2] = L.dma(L.y, "ST",
                                    o_g3T[oc * 128:(oc + 1) * 128, tt * TT:(tt + 1) * TT], stg[:])
    L.wait(L.y, "ST")
    return L


def prep_A3_weights(inp):
    perm = l3_perm()
    wq = fmaj(inp["L3_attn_wq"])
    wk = fmaj(inp["L3_attn_wk"])
    wv = fmaj(inp["L3_attn_wv"])
    b1 = np.asarray(inp["L3_ffn_b1"], np.float32)
    b2 = np.asarray(inp["L3_ffn_b2"], np.float32)
    b1c = np.zeros((128, 8 * 32), np.float32)
    for e in range(8):
        b1c[:, e * 32:(e + 1) * 32] = b1[e].reshape(32, 128).T
    b2c = np.zeros((128, 8 * 8), np.float32)
    for oc in range(8):
        for e in range(8):
            b2c[:, oc * 8 + e] = b2[e, oc * 128:(oc + 1) * 128]
    return {
        "a3_wqT": np.ascontiguousarray(wq[:, perm]),
        "a3_wkT": np.ascontiguousarray(wk[:, perm]),
        "a3_wvT": np.ascontiguousarray(wv),
        "a3_Wr": np.ascontiguousarray(np.asarray(inp["L3_ffn_Wr"], np.float32)),
        "a3_br": np.asarray(inp["L3_ffn_br"], np.float32).reshape(1, 8),
        "a3_W1": np.ascontiguousarray(np.asarray(inp["L3_ffn_W1"], np.float32)),
        "a3_W2": np.ascontiguousarray(np.asarray(inp["L3_ffn_W2"], np.float32)),
        "a3_b1c": b1c,
        "a3_b2c": b2c,
    }


def b3_consts():
    c = np.arange(1280)[None, :]
    p = np.arange(128)[:, None]
    M = (c >= p + 384).astype(np.float32)
    return {"c_caus": M.astype(BF)}


def build_B3():
    L = Launch("B3")
    C = declare_consts(L)
    qTh = L.din("b3_q", [8, 64, 4096], BF16)
    kTh = L.din("b3_k", [8, 64, 4096], BF16)
    vh = L.din("b3_v", [8, 128, 32 * 64], BF16)
    causI = L.din("c_caus", [128, 1280], BF16)
    o_aoT = L.dout("ao3T", [8, 64, 4096], BF16)

    fcol = L.sb("fcol", [128, len(FC_VALS)], F32)
    o64 = L.sb("o64", [64, 1], BF16)
    o128 = L.sb("o128", [128, 1], BF16)
    o1f = L.sb("o1f", [1, 128], F32)
    caus = L.sb("caus", [128, 1280], BF16)
    qta = L.sb("qta", [128, 4096], BF16)
    kta = L.sb("kta", [128, 4096], BF16)
    vtok = L.sb("vtok", [128, 32 * 64], BF16)
    PT = L.sb("PT", [128, 32 * 512], BF16)
    sqx = L.sb("sqx", [64, 4096], BF16)
    nq = L.sb("nq", [1, 4096], F32)
    nk = L.sb("nk", [1, 4096], F32)
    mk1 = L.sb("mk1", [1, 1], F32)
    brow = L.sb("brow", [1, 4096], F32)
    rrow = L.sb("rrow", [1, 512], F32)
    ravb = L.sb("ravb", [64, 512], F32)
    onesb = L.sb("onesb", [128, 64], BF16)
    ao_st = [L.sb(f"ao_st{i}", [64, 512], BF16) for i in range(2)]

    pb = [L.ps(f"pb{i}", [128, 512], F32) for i in range(7)]
    pu = L.ps("pu", [128, 512], F32)

    L.dma(L.y, "LD", fcol[:], C["fcol"])
    L.dma(L.y, "LD", o64[:], C["ones64c"])
    L.dma(L.y, "LD", o128[:], C["ones128c"])
    L.dma(L.y, "LD", o1f[:], C["ones1f"])
    L.dma(L.y, "LD", caus[:], causI)
    ld_c = L.cnt["LD"]
    L.wait(L.t, "LD", ld_c)
    L.wait(L.v, "LD", ld_c)
    L.wait(L.s, "LD", ld_c)
    L.v.memset(onesb[:], 1.0)
    L.v.memset(qta[64:66, :], 0.0)
    L.v.memset(kta[64:66, :], 0.0)
    L.inc(L.v, "VE")

    F8 = FC[0.125]
    sc18 = fcol[:, F8:F8 + 1]
    bank_free = {}
    ao_free = {}
    pe_unit_done = 0
    for u in range(8):
        L.wait(L.g, "PE", pe_unit_done)
        L.dma(L.g, "GD", qta[0:64, :], qTh[u])
        L.dma(L.g, "GD", kta[0:64, :], kTh[u])
        gd_v = L.dma(L.g, "GD", vtok[:], vh[u])
        L.wait(L.t, "GD", gd_v)
        for (src, dstrow) in ((qta, nq), (kta, nk)):
            L.wait(L.s, "GD", gd_v)
            L.wait(L.s, "VE")
            L.s.activation(sqx[:], src[0:64, :], AF.Square)
            se_sq = L.inc(L.s, "SE")
            L.wait(L.t, "SE", se_sq)
            for t8 in range(8):
                sem, cv = bank_free.get(7, (None, 0))
                if sem is not None:
                    L.wait(L.t, sem, cv)
                L.t.matmul(pu[0:1, 0:512], o64[:], sqx[:, t8 * 512:(t8 + 1) * 512],
                           start=True, stop=True)
                pe_n = L.inc(L.t, "PE")
                L.wait(L.s, "PE", pe_n)
                L.s.activation(dstrow[0:1, t8 * 512:(t8 + 1) * 512], pu[0:1, 0:512], AF.Sqrt)
                bank_free[7] = ("SE", L.inc(L.s, "SE"))
        se_rows = L.cnt["SE"]
        L.wait(L.v, "SE", se_rows)
        L.v.reduce_max(mk1[:], nk[:], axis=AX.X)
        L.selfsync(L.v, "VE")
        L.v.tensor_scalar(brow[:], nq[:], mk1[:], -1.0, op0=ALU.mult, op1=ALU.mult)
        L.v.tensor_copy(qta[64:65, :], brow[:])
        L.v.memset(kta[64:65, :], 1.0)
        ve_aug = L.inc(L.v, "VE")
        L.wait(L.t, "VE", ve_aug)
        for st in range(8):
            q0 = st * 512
            npres = q0 // 128 + 4
            for kc in range(npres):
                bi = kc % 4
                sem, cv = bank_free.get(bi, (None, 0))
                if sem is not None:
                    L.wait(L.t, sem, cv)
                L.t.matmul(pb[bi][:], kta[0:66, kc * 128:(kc + 1) * 128],
                           qta[0:66, q0:q0 + 512], start=True, stop=True)
                pe_s = L.inc(L.t, "PE")
                L.wait(L.s, "PE", pe_s)
                L.s.activation(PT[:, kc * 512:(kc + 1) * 512], pb[bi][:], AF.Exp, scale=sc18)
                se_e = L.inc(L.s, "SE")
                bank_free[bi] = ("SE", se_e)
                o = kc * 128 - q0
                if o >= -384:
                    L.wait(L.v, "SE", se_e)
                    L.v.tensor_mul(PT[:, kc * 512:(kc + 1) * 512],
                                   PT[:, kc * 512:(kc + 1) * 512],
                                   caus[:, 384 - o: 896 - o])
                    L.inc(L.v, "VE")
            se_pt = L.cnt["SE"]
            ve_pt = L.cnt["VE"]
            sem, cv = bank_free.get(5, (None, 0))
            if sem is not None:
                L.wait(L.t, sem, cv)
            L.wait(L.t, "SE", se_pt)
            L.wait(L.t, "VE", ve_pt)
            for kc in range(npres):
                L.t.matmul(pb[5][0:64, :], onesb[:], PT[:, kc * 512:(kc + 1) * 512],
                           start=(kc == 0), stop=(kc == npres - 1))
            pe_sum = L.inc(L.t, "PE")
            L.wait(L.s, "PE", pe_sum)
            L.wait(L.s, "VE")
            L.s.activation(ravb[:], pb[5][0:64, :], AF.Ln)
            L.s.activation(ravb[:], ravb[:], AF.Exp, scale=-1.0)
            se_rav = L.inc(L.s, "SE")
            bank_free[5] = ("SE", se_rav)
            sem, cv = bank_free.get(4, (None, 0))
            if sem is not None:
                L.wait(L.t, sem, cv)
            for kc in range(npres):
                L.t.matmul(pb[4][0:64, :], vtok[:, kc * 64:(kc + 1) * 64],
                           PT[:, kc * 512:(kc + 1) * 512],
                           start=(kc == 0), stop=(kc == npres - 1))
            pe_av = L.inc(L.t, "PE")
            stg = ao_st[st % 2]
            af = ao_free.get(st % 2)
            L.wait(L.v, "PE", pe_av)
            L.wait(L.v, "SE", se_rav)
            if af is not None:
                L.wait(L.v, "ST", af)
            L.v.tensor_mul(stg[:], pb[4][0:64, :], ravb[:])
            bank_free[4] = ("VE", L.inc(L.v, "VE"))
            L.wait(L.y, "VE")
            ao_free[st % 2] = L.dma(L.y, "ST", o_aoT[u, :, q0:q0 + 512], stg[:])
        pe_unit_done = L.cnt["PE"]
    L.wait(L.y, "ST")
    return L


def prep_B3(q3T, k3T, v3T):
    ins = []
    for c in range(NCORES):
        b = c // 2
        qs, ks, vs = [], [], []
        for j in range(8):
            h = (c % 2) * 8 + j
            rows = slice(h * 64, (h + 1) * 64)
            q_u = np.concatenate([q3T[2 * b][rows], q3T[2 * b + 1][rows]], axis=1)
            k_u = np.concatenate([k3T[2 * b][rows], k3T[2 * b + 1][rows]], axis=1)
            v_u = np.concatenate([v3T[2 * b][rows], v3T[2 * b + 1][rows]], axis=1)
            vt = np.ascontiguousarray(v_u.T).reshape(32, 128, 64).transpose(1, 0, 2).reshape(128, 32 * 64)
            qs.append(q_u); ks.append(k_u); vs.append(np.ascontiguousarray(vt))
        ins.append({"b3_q": np.stack(qs), "b3_k": np.stack(ks), "b3_v": np.stack(vs)})
    return ins


def unpack_B3(resB):
    outs = []
    for c in range(NCORES):
        b, half = c // 2, c % 2
        aoT = np.empty((D, NTOK), dtype=resB[0]["ao3T"].dtype)
        for h in range(16):
            src = 2 * b + (h // 8)
            j = h % 8
            aoT[h * 64:(h + 1) * 64, :] = resB[src]["ao3T"][j][:, half * NTOK:(half + 1) * NTOK]
        outs.append(aoT)
    return outs



_BUILT = {}


def _get(name, fn):
    if name not in _BUILT:
        _BUILT[name] = fn()
    return _BUILT[name]


def _unpack_xnT_h(tiles, ntiles):
    a = tiles.reshape(ntiles, 128, NCH, 128)
    return np.ascontiguousarray(a.transpose(2, 1, 0, 3).reshape(D, ntiles * 128))


def kernel(**inputs):
    inp = {k: np.asarray(v) for k, v in inputs.items()}
    consts = dict(const_inputs())
    consts.update(l2_consts())
    consts.update(a3_consts())
    consts.update(b3_consts())

    x0 = np.ascontiguousarray(inp["L0_attn_representations"].reshape(16384, D).astype(np.float32))
    xs = shard_tokens(x0)

    LA0 = _get("A0", build_A0)
    wA0 = prep_A0_weights(inp)
    resA0 = LA0.run([dict(consts, x0=xs[c], **wA0) for c in range(NCORES)])
    LB0 = _get("B0", build_B0)
    resB0 = LB0.run([dict(consts, **m) for m in prep_B0(resA0)])
    aoTs = unpack_B0(resB0)
    LC0 = _get("C0", build_C0)
    wC0 = prep_C0(inp)
    resC0 = LC0.run([dict(consts, **wC0, c0_aoT=aoTs[c],
                          c0_xnT=host_unpack_xnT(resA0[c]["xnT_tiles"]),
                          c0_x0T=fmaj(xs[c]), c0_g0T=resA0[c]["g0T"])
                     for c in range(NCORES)])

    LA1 = _get("A1", build_A1)
    wA1 = prep_A1_weights(inp)
    resA1 = LA1.run([dict(consts, x1=np.ascontiguousarray(resC0[c]["x1T"].T), **wA1)
                     for c in range(NCORES)])
    LB1 = _get("B1", build_B1)
    resB1 = LB1.run([dict(consts, **m) for m in prep_B1(resA1)])
    oTs = unpack_B1(resB1)
    LC1 = _get("C1", build_C1)
    wC1 = prep_C1(inp)
    resC1 = LC1.run([dict(consts, **wC1, c1_oT=oTs[c], c1_x1T=resC0[c]["x1T"],
                          c1_g1T=resA1[c]["g1T"]) for c in range(NCORES)])

    x2_full = np.ascontiguousarray(
        np.concatenate([resC1[c]["x2T"].T for c in range(NCORES)], axis=0))
    LA2 = _get("A2", build_A2)
    wA2 = prep_A2_weights(inp)
    x2hs = prep_A2_x2h(x2_full)
    resA2 = LA2.run([dict(consts, **wA2, x2h=x2hs[c]) for c in range(NCORES)])
    xn2_full = np.ascontiguousarray(np.concatenate(
        [_unpack_xnT_h(resA2[c]["xn2T_tiles"], NTOKH // 128)[:, HB:HB + NTOK].T
         for c in range(NCORES)], axis=0))
    LTB = _get("TBL", build_TBL)
    resTB = LTB.run([dict(consts, **l2_tbl_inputs(c)) for c in range(NCORES)])
    mstd, m0v = l2_masks()
    LF = _get("L2F", build_L2F)
    y_full = xn2_full
    for li in range(6):
        wL = prep_L2F_weights(inp, li)
        ys = prep_L2F_y(y_full, li == 0)
        insF = []
        for c in range(NCORES):
            m0c = m0v if c % 2 == 0 else mstd
            insF.append(dict(consts, **wL, l2_yT=ys[c], l2_m0=m0c, l2_ms=mstd,
                             **l2_keep(c),
                             l2_cs=resTB[c]["cs"], l2_t2=resTB[c]["t2"],
                             l2_ci=resTB[c]["ci"], l2_t2i=resTB[c]["t2i"]))
        resF = LF.run(insF)
        y_full = np.ascontiguousarray(
            np.concatenate([resF[c]["oyT"].T for c in range(NCORES)], axis=0))
    LC2 = _get("C2", build_C2)
    resC2 = LC2.run([dict(consts, c2_y6T=fmaj(y_full[c * NTOK:(c + 1) * NTOK]),
                          c2_s2T=resA2[c]["s2T"], c2_x2T=resC1[c]["x2T"])
                     for c in range(NCORES)])

    LT3 = _get("TBL3", build_TBL3)
    resT3 = LT3.run([dict(consts, **tbl3_inputs(c)) for c in range(NCORES)])
    LA3 = _get("A3", build_A3)
    wA3 = prep_A3_weights(inp)
    resA3 = LA3.run([dict(consts, **wA3, x3=np.ascontiguousarray(resC2[c]["x3T"].T),
                          l3cos=resT3[c]["l3cos"], l3sin=resT3[c]["l3sin"])
                     for c in range(NCORES)])
    LB3 = _get("B3", build_B3)
    insB3 = prep_B3([resA3[c]["q3T"] for c in range(NCORES)],
                    [resA3[c]["k3T"] for c in range(NCORES)],
                    [resA3[c]["v3T"] for c in range(NCORES)])
    resB3 = LB3.run([dict(consts, **m) for m in insB3])
    ao3Ts = unpack_B3(resB3)
    LC3 = _get("C3", build_C3)
    resC3 = LC3.run([dict(consts, c3_aoT=ao3Ts[c], c3_woT=fmaj(inp["L3_attn_wo"]),
                          c3_x3T=resC2[c]["x3T"], c3_g3T=resA3[c]["g3T"])
                     for c in range(NCORES)])

    out = np.concatenate([resC3[c]["x4T"].T for c in range(NCORES)], axis=0)
    return np.ascontiguousarray(out.reshape(4, 4096, D).astype(np.float32))
